# version 07
# baseline (speedup 1.0000x reference)
import os
import numpy as np
import ml_dtypes
import concourse.bass as bass
import concourse.mybir as mybir
from concourse.bass_utils import run_bass_kernel_spmd

F32 = mybir.dt.float32
BF16 = mybir.dt.bfloat16
AF = mybir.ActivationFunctionType
ALU = mybir.AluOpType
AX = mybir.AxisListType

DBG = bool(int(os.environ.get("MK_DBG", "0")))

T = 2048
C = 2048
INTER = 8192
SUP = 1024
NSUP = T // SUP
NEG = -1.0e9
EPS = 1e-5
SAFE = 1.02

BF = ml_dtypes.bfloat16


def _hmat(n):
    h = np.array([[1.0]], dtype=np.float32)
    while h.shape[0] < n:
        h = np.block([[h, h], [h, -h]])
    return h.astype(np.float32)


def _consts():
    H128 = _hmat(128)
    H64 = _hmat(64)
    BD = np.kron(np.eye(2, dtype=np.float32), H64)
    I128 = np.eye(128, dtype=np.float32)
    onescol = np.ones((128, 1), np.float32)
    onesrow = np.ones((1, 128), np.float32)
    inv = 1.0 / (10000.0 ** (np.arange(0, 256, 2, dtype=np.float64) / 256.0))
    f = np.arange(T, dtype=np.float64)[:, None] * inv[None, :]
    cosL = np.cos(f).T.astype(np.float32)
    sinL = np.sin(f).T.astype(np.float32)
    masks = np.zeros((128, 4 * 512), np.float32)
    for r in range(4):
        sp = np.arange(128)[:, None] + r * 128
        tf = np.arange(512)[None, :]
        masks[:, r * 512:(r + 1) * 512] = np.where(sp <= tf, 0.0, NEG)
    dirs = np.array([[sx, sy, sz] for sx in (1.0, -1.0) for sy in (1.0, -1.0)
                     for sz in (1.0, -1.0)], dtype=np.float32) / np.sqrt(3.0)
    return dict(
        cHb=H128.astype(BF), cBD=BD.astype(BF), cIDf=I128, cIDb=I128.astype(BF),
        cOnesColB=onescol.astype(BF), cOnesRowB=onesrow.astype(BF),
        cOnesRowF=onesrow, cCos=cosL.astype(BF), cSin=sinL.astype(BF),
        cMask=masks.astype(BF), cDirs=dirs.T.copy().astype(BF),
        cMisc=np.array([[EPS, 0.0, 0.0, 0.0]], np.float32),
    )


_CONSTS = _consts()


class Sem:
    __slots__ = ("h", "n", "base")

    def __init__(self, h):
        self.h = h
        self.n = 0
        self.base = 0


class DSem:
    __slots__ = ("cells",)

    def __init__(self, cells):
        self.cells = cells


class Env:
    pass


def _dsems(env, spec):
    return {nm: DSem(env.dcells[s0:s0 + d]) for nm, (s0, d) in spec.items()}


def _snap_all(env):
    for s in env.sems.values():
        s.base = s.n
    for c in env.dcells:
        c.base = c.n


def _entry_waits(env, eng):
    for s in env.sems.values():
        if s.base > 0:
            eng.wait_ge(s.h, s.base)
    for c in env.dcells:
        if c.base > 0:
            eng.wait_ge(c.h, c.base)


def inc(s, ins, k=1):
    ins.then_inc(s.h, k)
    s.n += k
    return ins


def dinc(ds, slot, ins):
    c = ds.cells[slot]
    ins.then_inc(c.h, 16)
    c.n += 16
    return ins


def dwait(eng, ds, slot, rounds):
    c = ds.cells[slot]
    eng.wait_ge(c.h, c.base + 16 * rounds)


def dwait_first(eng, ds, k):
    d = len(ds.cells)
    for s in range(d):
        cnt = (k - s + d - 1) // d
        if cnt > 0:
            eng.wait_ge(ds.cells[s].h, ds.cells[s].base + 16 * cnt)


def dwait_all(eng, ds):
    for c in ds.cells:
        if c.n > 0:
            eng.wait_ge(c.h, c.n)


def block_B1(env):
    nc = env.nc
    S = dict(env.sems)
    S.update(_dsems(env, {"sX0": (0, 2), "sXTw": (2, 2)}))
    _snap_all(env)
    with (
        nc.sbuf_tensor("b1_xbuf", [128, 2 * 2048], F32) as xbuf,
        nc.sbuf_tensor("b1_trb", [128, 2 * 128], F32) as trb,
        nc.sbuf_tensor("b1_vecload", [64, 6 * 128], F32) as vl,
        nc.psum_tensor("b1_pt0", [128, 512], F32) as pt0,
        nc.psum_tensor("b1_pt1", [128, 512], F32) as pt1,
        nc.Block() as blk,
    ):
        pt = [pt0, pt1]
        io = env.io

        NV = {"su": 0, "wdp": 1, "sd": 2, "cu": 3, "cg": 4}

        @blk.gpsimd
        def _(g):
            _entry_waits(env, g)
            for name in ("cHb", "cBD", "cIDf", "cIDb", "cOnesColB", "cOnesRowB",
                         "cOnesRowF", "cMask", "cDirs", "cMisc"):
                inc(S["cst"], g.dma_start(out=env.ct[name][:], in_=io[name][:]), 16)
            inc(S["cst"], g.dma_start(out=env.ct["hs"][:], in_=io["head_scales"].rearrange("(o h) -> o h", o=1)), 16)
            inc(S["cst"], g.dma_start(out=vl[0:16, 0 * 128:1 * 128],
                                      in_=io["SU_ug"].rearrange("(c p) -> c p", p=128)), 16)
            inc(S["cst"], g.dma_start(out=vl[0:16, 1 * 128:2 * 128],
                                      in_=io["Wscale_dp"].rearrange("(c p) -> c p", p=128)), 16)
            inc(S["cst"], g.dma_start(out=vl[0:64, 2 * 128:3 * 128],
                                      in_=io["SU_dp"].rearrange("(c p) -> c p", p=128)), 16)
            inc(S["cst"], g.dma_start(out=vl[0:64, 3 * 128:4 * 128],
                                      in_=io["Wscale_ug"][0:8192].rearrange("(c p) -> c p", p=128)), 16)
            inc(S["cst"], g.dma_start(out=vl[0:64, 4 * 128:5 * 128],
                                      in_=io["Wscale_ug"][8192:16384].rearrange("(c p) -> c p", p=128)), 16)
            for tt in range(16):
                if tt >= 2:
                    g.wait_ge(S["cP1t"].h, S["cP1t"].base + 16 * (tt - 1))
                dinc(S["sX0"], tt % 2, g.dma_start(out=xbuf[:, (tt % 2) * 2048:(tt % 2 + 1) * 2048],
                                                   in_=io["x"][tt * 128:(tt + 1) * 128, :]))

        @blk.sync
        def _(s):
            _entry_waits(env, s)
            for i in range(256):
                tt, k = divmod(i, 16)
                s.wait_ge(S["cP1e"].h, S["cP1e"].base + i + 1)
                dinc(S["sXTw"], i % 2, s.dma_start(out=env.xT.ap()[k * 128:(k + 1) * 128, tt * 128:(tt + 1) * 128],
                                                   in_=trb[:, (i % 2) * 128:(i % 2 + 1) * 128]))

        @blk.tensor
        def _(t):
            _entry_waits(env, t)
            t.wait_ge(S["cst"].h, S["cst"].n)
            for idx, (name, P) in enumerate([("su", 16), ("wdp", 16), ("sd", 64),
                                             ("cu", 64), ("cg", 64)]):
                sl = NV[name]
                if idx >= 2:
                    t.wait_ge(S["cDe"].h, S["cDe"].base + (idx - 1))
                inc(S["cDt"], t.transpose(pt[idx % 2][:, 0:P],
                                          vl[0:P, sl * 128:sl * 128 + 128],
                                          env.ct["cIDf"][0:P, 0:P]))
            t.wait_ge(S["cDe"].h, S["cDe"].base + 5)
            for i in range(256):
                tt, k = divmod(i, 16)
                if k == 0:
                    dwait(t, S["sX0"], tt % 2, tt // 2 + 1)
                if i >= 2:
                    t.wait_ge(S["cP1e"].h, S["cP1e"].base + (i - 1))
                inc(S["cP1t"], t.transpose(pt[i % 2][:, 0:128],
                                           xbuf[:, (tt % 2) * 2048 + k * 128:(tt % 2) * 2048 + (k + 1) * 128],
                                           env.ct["cIDf"][:, :]))

        @blk.scalar
        def _(a):
            _entry_waits(env, a)
            for idx, (name, P) in enumerate([("su", 16), ("wdp", 16), ("sd", 64),
                                             ("cu", 64), ("cg", 64)]):
                a.wait_ge(S["cDt"].h, S["cDt"].base + idx + 1)
                scale = 1.0
                if name == "wdp":
                    scale = 2.0 ** -7.5
                elif name in ("cu", "cg"):
                    scale = 2.0 ** -5.5
                dst = {"su": env.ct["su_t"], "wdp": env.ct["co2_t"], "sd": env.ct["sd_t"],
                       "cu": env.ct["cu_t"], "cg": env.ct["cg_t"]}[name]
                inc(S["cDe"], a.mul(dst[:, 0:P], pt[idx % 2][:, 0:P], scale))
            for i in range(256):
                a.wait_ge(S["cP1t"].h, S["cP1t"].base + i + 1)
                if i >= 2:
                    dwait(a, S["sXTw"], i % 2, i // 2)
                inc(S["cP1e"], a.copy(trb[:, (i % 2) * 128:(i % 2 + 1) * 128],
                                      pt[i % 2][:, 0:128]))

        @blk.vector
        def _(v):
            _entry_waits(env, v)
            v.wait_ge(S["cDe"].h, S["cDe"].base + 4)
            inc(S["cDw"], v.tensor_mul(env.ct["w2_t"][:, 0:64], env.ct["cu_t"][:, 0:64],
                                       env.ct["sd_t"][:, 0:64]))
            v.wait_ge(S["cst"].h, S["cst"].n)
            inc(S["cDw"], v.tensor_scalar(env.ct["hs1"][0:1, 0:8], env.ct["hs"][0:1, 0:8],
                                          -1.0, 1.0, ALU.mult, ALU.add))


def block_B2a(env, xin, u, tag, wdc=False):
    nc = env.nc
    S = dict(env.sems)
    S.update(_dsems(env, {"sXA1": (0, 2), "sXA2": (2, 2), "sSTr": (4, 2),
                          "sSTw": (6, 2), "sPBr": (8, 3), "sDmp": (11, 3)}))
    _snap_all(env)
    from contextlib import ExitStack
    with ExitStack() as _es:
        _e = _es.enter_context
        xc = _e(nc.sbuf_tensor(f"a_xc_{tag}", [128, 2 * 512], F32))
        xb = _e(nc.sbuf_tensor(f"a_xb_{tag}", [128, 2 * 512], BF16))
        sq = _e(nc.sbuf_tensor(f"a_sq_{tag}", [128, 2 * 512], BF16))
        xc2 = _e(nc.sbuf_tensor(f"a_x2_{tag}", [128, 2 * 512], F32))
        rows = _e(nc.sbuf_tensor(f"a_rows_{tag}", [1, 8 * 512], F32))
        ping = _e(nc.sbuf_tensor(f"a_ping_{tag}", [128, 16 * 512], F32))
        pong = _e(nc.sbuf_tensor(f"a_pong_{tag}", [128, 16 * 512], F32))
        ybf = _e(nc.sbuf_tensor(f"a_ybf_{tag}", [128, 16 * 512], BF16))
        ps_s1 = _e(nc.psum_tensor(f"a_ps0_{tag}", [128, 512], F32))
        ps_s2 = _e(nc.psum_tensor(f"a_ps1_{tag}", [128, 512], F32))
        ps_m = _e(nc.psum_tensor(f"a_pm_{tag}", [128, 512], F32))
        ps_r = _e(nc.psum_tensor(f"a_pr_{tag}", [128, 512], F32))
        ph0 = _e(nc.psum_tensor(f"a_ph0_{tag}", [128, 512], F32))
        ph1 = _e(nc.psum_tensor(f"a_ph1_{tag}", [128, 512], F32))
        if wdc:
            stdb = _e(nc.sbuf_tensor(f"a_std_{tag}", [128, 2 * 2048], BF16))
            pbr = _e(nc.sbuf_tensor(f"a_pbr_{tag}", [128, 3 * 512], BF16))
            pbe = _e(nc.sbuf_tensor(f"a_pbe_{tag}", [128, 3 * 512], BF16))
            pb0 = _e(nc.psum_tensor(f"a_pb0_{tag}", [128, 512], F32))
            pb1 = _e(nc.psum_tensor(f"a_pb1_{tag}", [128, 512], F32))
            pbp = [pb0, pb1]
        blk = _e(nc.Block())
        ph = [ph0, ph1]
        col0 = u * SUP
        RM, RE, RM2, RV, RR = 0, 1, 2, 3, 4

        @blk.gpsimd
        def _(g):
            _entry_waits(env, g)
            cast_j = [0]

            def maybe_cast():
                j = cast_j[0]
                if wdc and j < 8:
                    inc(S["sCd"], g.dma_start(
                        out=env.SNdown.ap()[j * 256:(j + 1) * 256, :],
                        in_=env.io["W_down"][j * 256:(j + 1) * 256, :]), 16)
                    cast_j[0] += 1

            for tc in range(2):
                for c in range(16):
                    i = tc * 16 + c
                    if i % 8 == 0:
                        maybe_cast()
                    if i >= 2:
                        g.wait_ge(S["cXA1"].h, S["cXA1"].base + (i - 1))
                        g.wait_ge(S["cSq"].h, S["cSq"].base + (i - 1))
                    dinc(S["sXA1"], i % 2, g.dma_start(
                        out=xc[:, (i % 2) * 512:(i % 2 + 1) * 512],
                        in_=xin.ap()[c * 128:(c + 1) * 128, col0 + tc * 512:col0 + (tc + 1) * 512]))
                for c in range(16):
                    i = tc * 16 + c
                    if i % 8 == 0:
                        maybe_cast()
                    if i >= 2:
                        g.wait_ge(S["cXA2"].h, S["cXA2"].base + (i - 1))
                    dinc(S["sXA2"], i % 2, g.dma_start(
                        out=xc2[:, (i % 2) * 512:(i % 2 + 1) * 512],
                        in_=xin.ap()[c * 128:(c + 1) * 128, col0 + tc * 512:col0 + (tc + 1) * 512]))
            if wdc:
                while cast_j[0] < 8:
                    maybe_cast()
                dmp = env.PBdump.ap()
                for i in range(256):
                    og, p0 = divmod(i, 64)
                    g.wait_ge(S["cPBe"].h, S["cPBe"].base + i + 1)
                    dinc(S["sDmp"], i % 3, g.dma_start(
                        out=dmp[og, p0], in_=pbe[:, (i % 3) * 512:(i % 3 + 1) * 512]))

        if wdc:
            @blk.sync
            def _(s):
                _entry_waits(env, s)
                for c in range(64):
                    if c == 0:
                        s.wait_ge(S["sCd"].h, S["sCd"].base + 16 * 8)
                    if c >= 2:
                        dwait(s, S["sSTw"], c % 2, c // 2)
                    dinc(S["sSTr"], c % 2, s.dma_start(
                        out=stdb[:, (c % 2) * 2048:(c % 2 + 1) * 2048],
                        in_=env.SNdown.ap()[0:2048, c * 128:(c + 1) * 128],
                        transpose=True))
                    dwait(s, S["sSTr"], c % 2, c // 2 + 1)
                    dinc(S["sSTw"], c % 2, s.dma_start(
                        out=env.STdown.ap()[c * 128:(c + 1) * 128, :],
                        in_=stdb[:, (c % 2) * 2048:(c % 2 + 1) * 2048]))
                st4 = env.STdown.ap().rearrange("(c e q) o -> e c q o", c=64, e=2, q=64)
                for i in range(256):
                    og, p0 = divmod(i, 64)
                    if i == 0:
                        dwait_first(s, S["sSTw"], 64)
                    if i >= 3:
                        s.wait_ge(S["cPBm"].h, S["cPBm"].base + (i - 2))
                    dinc(S["sPBr"], i % 3, s.dma_start(
                        out=pbr[:, (i % 3) * 512:(i % 3 + 1) * 512],
                        in_=st4[:, :, p0, og * 512:(og + 1) * 512]))

        @blk.vector
        def _(v):
            _entry_waits(env, v)
            for tc in range(2):
                for c in range(16):
                    i = tc * 16 + c
                    dwait(v, S["sXA1"], i % 2, i // 2 + 1)
                    if i >= 2:
                        v.wait_ge(S["cTS"].h, S["cTS"].base + (i - 1))
                    inc(S["cXA1"], v.tensor_copy(xb[:, (i % 2) * 512:(i % 2 + 1) * 512],
                                                 xc[:, (i % 2) * 512:(i % 2 + 1) * 512]))
                v.wait_ge(S["cRowA"].h, S["cRowA"].base + tc * 4 + 3)
                inc(S["cRowV"], v.tensor_sub(rows[0:1, RV * 512:RV * 512 + 512],
                                             rows[0:1, RE * 512:RE * 512 + 512],
                                             rows[0:1, RM2 * 512:RM2 * 512 + 512]))
                for c in range(16):
                    i = tc * 16 + c
                    dwait(v, S["sXA2"], i % 2, i // 2 + 1)
                    v.wait_ge(S["cTB"].h, S["cTB"].base + 2 * (tc + 1))
                    inc(S["cXA2"], v.tensor_sub(pong[:, c * 512:(c + 1) * 512],
                                                xc2[:, (i % 2) * 512:(i % 2 + 1) * 512],
                                                ps_m[:, :]))
                    inc(S["cAp"], v.scalar_tensor_tensor(
                        ping[:, c * 512:(c + 1) * 512],
                        pong[:, c * 512:(c + 1) * 512],
                        env.ct["su_t"][:, c:c + 1],
                        ps_r[:, :], ALU.mult, ALU.mult))
                bufs = [ping, pong]
                for si, h in enumerate([1, 2, 4, 8]):
                    src = bufs[si % 2]
                    dst = bufs[(si + 1) % 2]
                    if si == 3 and tc == 1:
                        v.wait_ge(S["cT128"].h, S["cT128"].base + 16)
                    for cpair in range(8):
                        blkb = (cpair // h) * (2 * h)
                        ci = blkb + (cpair % h)
                        cj = ci + h
                        for c_out, op, aa, bb in ((ci, ALU.add, ci, cj), (cj, ALU.subtract, ci, cj)):
                            if si < 3:
                                inc(S["cH16"], v.tensor_tensor(
                                    dst[:, c_out * 512:(c_out + 1) * 512],
                                    src[:, aa * 512:(aa + 1) * 512],
                                    src[:, bb * 512:(bb + 1) * 512], op))
                            else:
                                inc(S["cYb"], v.tensor_tensor(
                                    ybf[:, c_out * 512:(c_out + 1) * 512],
                                    src[:, aa * 512:(aa + 1) * 512],
                                    src[:, bb * 512:(bb + 1) * 512], op))
            if wdc:
                for i in range(256):
                    v.wait_ge(S["cPBm"].h, S["cPBm"].base + i + 1)
                    if i >= 3:
                        dwait(v, S["sDmp"], i % 3, i // 3)
                    inc(S["cPBe"], v.tensor_copy(pbe[:, (i % 3) * 512:(i % 3 + 1) * 512],
                                                 pbp[i % 2][:, :]))

        @blk.scalar
        def _(a):
            _entry_waits(env, a)
            for tc in range(2):
                for c in range(16):
                    i = tc * 16 + c
                    dwait(a, S["sXA1"], i % 2, i // 2 + 1)
                    if i >= 2:
                        a.wait_ge(S["cTS"].h, S["cTS"].base + (i - 1))
                    inc(S["cSq"], a.square(sq[:, (i % 2) * 512:(i % 2 + 1) * 512],
                                           xc[:, (i % 2) * 512:(i % 2 + 1) * 512]))
                a.wait_ge(S["cTS"].h, S["cTS"].base + 16 * (tc + 1))
                inc(S["cRowA"], a.mul(rows[0:1, RM * 512:RM * 512 + 512], ps_s1[0:1, :], 1.0 / 2048.0))
                inc(S["cRowA"], a.mul(rows[0:1, RE * 512:RE * 512 + 512], ps_s2[0:1, :], 1.0 / 2048.0))
                inc(S["cRowA"], a.square(rows[0:1, RM2 * 512:RM2 * 512 + 512],
                                         rows[0:1, RM * 512:RM * 512 + 512]))
                a.wait_ge(S["cRowV"].h, S["cRowV"].base + tc + 1)
                inc(S["cRowA"], a.activation(rows[0:1, 5 * 512:5 * 512 + 512],
                                             rows[0:1, RV * 512:RV * 512 + 512],
                                             AF.Ln, bias=EPS))
                inc(S["cRowR"], a.activation(rows[0:1, RR * 512:RR * 512 + 512],
                                             rows[0:1, 5 * 512:5 * 512 + 512],
                                             AF.Exp, scale=-0.5))
                for c in range(16):
                    a.wait_ge(S["cT128"].h, S["cT128"].base + tc * 16 + c + 1)
                    inc(S["cAe"], a.copy(env.APP[:, c * 1024 + tc * 512: c * 1024 + (tc + 1) * 512],
                                         ph[c % 2][:, :]))

        @blk.tensor
        def _(t):
            _entry_waits(env, t)
            for tc in range(2):
                if tc == 1:
                    t.wait_ge(S["cRowA"].h, S["cRowA"].base + 2)
                for c in range(16):
                    i = tc * 16 + c
                    t.wait_ge(S["cXA1"].h, S["cXA1"].base + i + 1)
                    t.wait_ge(S["cSq"].h, S["cSq"].base + i + 1)
                    t.matmul(ps_s1[0:1, :], env.ct["cOnesColB"][:, :],
                             xb[:, (i % 2) * 512:(i % 2 + 1) * 512],
                             start=(c == 0), stop=(c == 15))
                    inc(S["cTS"], t.matmul(ps_s2[0:1, :], env.ct["cOnesColB"][:, :],
                                           sq[:, (i % 2) * 512:(i % 2 + 1) * 512],
                                           start=(c == 0), stop=(c == 15)))
                if tc == 1:
                    t.wait_ge(S["cAp"].h, S["cAp"].base + 16)
                t.wait_ge(S["cRowA"].h, S["cRowA"].base + tc * 4 + 1)
                inc(S["cTB"], t.matmul(ps_m[:, :], env.ct["cOnesRowF"][:, :],
                                       rows[0:1, RM * 512:RM * 512 + 512], start=True, stop=True))
                t.wait_ge(S["cRowR"].h, S["cRowR"].base + tc + 1)
                inc(S["cTB"], t.matmul(ps_r[:, :], env.ct["cOnesRowF"][:, :],
                                       rows[0:1, RR * 512:RR * 512 + 512], start=True, stop=True))
                for c in range(16):
                    need = 2 * c + 1 if c < 8 else 2 * (c - 8) + 2
                    t.wait_ge(S["cYb"].h, S["cYb"].base + tc * 16 + need)
                    if c >= 2:
                        t.wait_ge(S["cAe"].h, S["cAe"].base + tc * 16 + (c - 1))
                    inc(S["cT128"], t.matmul(ph[c % 2][:, :], env.ct["cHb"][:, :],
                                             ybf[:, c * 512:(c + 1) * 512],
                                             start=True, stop=True))
            if wdc:
                for i in range(256):
                    dwait(t, S["sPBr"], i % 3, i // 3 + 1)
                    if i >= 2:
                        t.wait_ge(S["cPBe"].h, S["cPBe"].base + (i - 1))
                    inc(S["cPBm"], t.matmul(pbp[i % 2][:, :], env.ct["cBD"][:, :],
                                            pbr[:, (i % 3) * 512:(i % 3 + 1) * 512],
                                            start=True, stop=True))


def block_B2b(env, xin, xout, u, tag, wdc=False):
    nc = env.nc
    S = dict(env.sems)
    S.update(_dsems(env, {"sWU": (0, 3), "sWG": (3, 3), "sSE": (6, 3),
                          "sXD": (9, 4), "sXW": (13, 4),
                          "sRbr": (17, 3), "sSef": (20, 3),
                          "sTU": (23, 3), "sTG": (26, 3)}))
    _snap_all(env)
    GS = [3] * 21 + [1]
    GBASE = [3 * i for i in range(22)]
    from contextlib import ExitStack
    with ExitStack() as _es:
        _e = _es.enter_context
        wub = _e(nc.sbuf_tensor(f"b_wu_{tag}", [128, 3 * 512], BF16))
        wgb = _e(nc.sbuf_tensor(f"b_wg_{tag}", [128, 3 * 512], BF16))
        seb = _e(nc.sbuf_tensor(f"b_se_{tag}", [128, 3 * 512], BF16))
        zpb = _e(nc.sbuf_tensor(f"b_zp_{tag}", [128, 2 * 512], BF16))
        silb = _e(nc.sbuf_tensor(f"b_sil_{tag}", [128, 2 * 512], F32))
        xcb = _e(nc.sbuf_tensor(f"b_xc_{tag}", [128, 4 * 512], F32))
        xob = _e(nc.sbuf_tensor(f"b_xo_{tag}", [128, 4 * 512], F32))
        P = [_e(nc.psum_tensor(f"b_p{j}_{tag}", [128, 512], F32)) for j in range(8)]
        if wdc:
            rbb = _e(nc.sbuf_tensor(f"b_rbb_{tag}", [128, 3 * 512], BF16))
        blk = _e(nc.Block())
        col0 = u * SUP

        @blk.sync
        def _(s):
            _entry_waits(env, s)
            dmp4 = env.PBdump.ap().rearrange("og p0 (e c) o -> og e p0 c o", e=2)

            def reblock_read(i):
                og, c = divmod(i, 64)
                if i >= 3:
                    dwait(s, S["sSef"], i % 3, i // 3)
                dinc(S["sRbr"], i % 3, s.dma_start(
                    out=rbb[:, (i % 3) * 512:(i % 3 + 1) * 512],
                    in_=dmp4[og, :, :, c, :]))

            for i in range(256):
                g_, k = divmod(i, 16)
                if wdc:
                    reblock_read(i)
                    if k == 0:
                        s.wait_ge(S["sCu"].h, S["sCu"].base + 16 * (g_ + 1))
                if i >= 3:
                    s.wait_ge(S["cWU"].h, S["cWU"].base + (i - 2))
                    if wdc:
                        dwait(s, S["sTU"], i % 3, i // 3)
                if wdc:
                    dinc(S["sWU"], i % 3, s.dma_start(out=wub[:, (i % 3) * 512:(i % 3 + 1) * 512],
                                                      in_=env.SNup.ap()[g_ * 512:(g_ + 1) * 512,
                                                                        k * 128:(k + 1) * 128],
                                                      transpose=True))
                else:
                    dinc(S["sWU"], i % 3, s.dma_start(out=wub[:, (i % 3) * 512:(i % 3 + 1) * 512],
                                                      in_=env.STup.ap()[k * 128:(k + 1) * 128,
                                                                        g_ * 512:(g_ + 1) * 512]))
            for g3 in range(22):
                for k in range(16):
                    i = g3 * 16 + k
                    if wdc and k == 0:
                        nchunk = -(-(GBASE[g3] + GS[g3]) * 128 // 512)
                        s.wait_ge(S["sCg"].h, S["sCg"].base + 16 * nchunk)
                    if i >= 3:
                        s.wait_ge(S["cWG"].h, S["cWG"].base + (i - 2))
                        if wdc:
                            dwait(s, S["sTG"], i % 3, i // 3)
                    a0 = GBASE[g3]
                    na = GS[g3]
                    if wdc:
                        dinc(S["sWG"], i % 3, s.dma_start(out=wgb[:, (i % 3) * 512:(i % 3) * 512 + na * 128],
                                                          in_=env.SNgate.ap()[a0 * 128:a0 * 128 + na * 128,
                                                                              k * 128:(k + 1) * 128],
                                                          transpose=True))
                    else:
                        dinc(S["sWG"], i % 3, s.dma_start(out=wgb[:, (i % 3) * 512:(i % 3) * 512 + na * 128],
                                                          in_=env.STgate.ap()[k * 128:(k + 1) * 128,
                                                                              a0 * 128:a0 * 128 + na * 128]))
            sef = env.SEff.ap()
            for i in range(256):
                og, a_ = divmod(i, 64)
                if wdc:
                    dwait_first(s, S["sSef"], i + 1)
                if i >= 3:
                    s.wait_ge(S["cSE"].h, S["cSE"].base + (i - 2))
                dinc(S["sSE"], i % 3, s.dma_start(out=seb[:, (i % 3) * 512:(i % 3 + 1) * 512],
                                                  in_=sef[og, a_]))

        @blk.gpsimd
        def _(g):
            _entry_waits(env, g)
            if wdc:
                for j in range(16):
                    inc(S["sCu"], g.dma_start(
                        out=env.SNup.ap()[j * 512:(j + 1) * 512, :],
                        in_=env.io["W_up"][j * 512:(j + 1) * 512, :]), 16)
                for j in range(16):
                    inc(S["sCg"], g.dma_start(
                        out=env.SNgate.ap()[j * 512:(j + 1) * 512, :],
                        in_=env.io["W_gate"][j * 512:(j + 1) * 512, :]), 16)
                sef = env.SEff.ap()
                for i in range(256):
                    og, c = divmod(i, 64)
                    dwait(g, S["sRbr"], i % 3, i // 3 + 1)
                    dinc(S["sSef"], i % 3, g.dma_start(
                        out=sef[og, c], in_=rbb[:, (i % 3) * 512:(i % 3 + 1) * 512]))

            def rd(i):
                og, r8 = divmod(i, 8)
                tc, oci = divmod(r8, 4)
                oc = og * 4 + oci
                if i >= 4:
                    g.wait_ge(S["cEvD"].h, S["cEvD"].base + (i - 3))
                dinc(S["sXD"], i % 4, g.dma_start(
                    out=xcb[:, (i % 4) * 512:(i % 4 + 1) * 512],
                    in_=xin.ap()[oc * 128:(oc + 1) * 128, col0 + tc * 512:col0 + (tc + 1) * 512]))

            def wr(i):
                og, r8 = divmod(i, 8)
                tc, oci = divmod(r8, 4)
                oc = og * 4 + oci
                g.wait_ge(S["cEvD"].h, S["cEvD"].base + i + 1)
                dinc(S["sXW"], i % 4, g.dma_start(
                    out=xout.ap()[oc * 128:(oc + 1) * 128, col0 + tc * 512:col0 + (tc + 1) * 512],
                    in_=xob[:, (i % 4) * 512:(i % 4 + 1) * 512]))

            for i in range(4):
                rd(i)
            for i in range(32):
                wr(i)
                if i + 4 < 32:
                    rd(i + 4)

        @blk.tensor
        def _(t):
            _entry_waits(env, t)
            for g_ in range(16):
                if g_ >= 1:
                    t.wait_ge(S["cEvU"].h, S["cEvU"].base + 8 * g_)
                for k in range(16):
                    i = g_ * 16 + k
                    dwait(t, S["sWU"], i % 3, i // 3 + 1)
                    for tc in range(2):
                        for ai in range(4):
                            ins = t.matmul(P[tc * 4 + ai][:, :],
                                           wub[:, (i % 3) * 512 + ai * 128:(i % 3) * 512 + (ai + 1) * 128],
                                           env.APP[:, k * 1024 + tc * 512:k * 1024 + (tc + 1) * 512],
                                           start=(k == 0), stop=(k == 15))
                    inc(S["cWU"], ins)
            t.wait_ge(S["cEvU"].h, S["cEvU"].base + 128)
            for g3 in range(22):
                na = GS[g3]
                a0 = GBASE[g3]
                if g3 >= 1:
                    t.wait_ge(S["cSil"].h, S["cSil"].base + 2 * sum(GS[:g3]))
                for k in range(16):
                    i = g3 * 16 + k
                    dwait(t, S["sWG"], i % 3, i // 3 + 1)
                    for tc in range(2):
                        for ai in range(na):
                            ins = t.matmul(P[tc * 3 + ai][:, :],
                                           wgb[:, (i % 3) * 512 + ai * 128:(i % 3) * 512 + (ai + 1) * 128],
                                           env.APP[:, k * 1024 + tc * 512:k * 1024 + (tc + 1) * 512],
                                           start=(k == 0), stop=(k == 15))
                    inc(S["cWG"], ins)
                for tc in range(2):
                    for ai in range(na):
                        j = 2 * sum(GS[:g3]) + tc * na + ai
                        t.wait_ge(S["cZp"].h, S["cZp"].base + j + 1)
                        if j >= 2:
                            t.wait_ge(S["cZh"].h, S["cZh"].base + (j - 1))
                        inc(S["cTH"], t.matmul(P[6 + j % 2][:, :], env.ct["cHb"][:, :],
                                               zpb[:, (j % 2) * 512:(j % 2 + 1) * 512],
                                               start=True, stop=True))
            t.wait_ge(S["cZh"].h, S["cZh"].base + 128)
            t.wait_ge(S["cSil"].h, S["cSil"].base + 128)
            for og in range(4):
                if og >= 1:
                    t.wait_ge(S["cEvD"].h, S["cEvD"].base + 8 * og)
                for a_ in range(64):
                    i = og * 64 + a_
                    dwait(t, S["sSE"], i % 3, i // 3 + 1)
                    for tc in range(2):
                        for oci in range(4):
                            ins = t.matmul(P[tc * 4 + oci][:, :],
                                           seb[:, (i % 3) * 512 + oci * 128:(i % 3) * 512 + (oci + 1) * 128],
                                           env.UZ[:, a_ * 1024 + tc * 512:a_ * 1024 + (tc + 1) * 512],
                                           start=(a_ == 0), stop=(a_ == 63))
                    inc(S["cSE"], ins)

        @blk.scalar
        def _(a):
            _entry_waits(env, a)
            for g_ in range(16):
                if wdc:
                    for k in range(16):
                        i = g_ * 16 + k
                        a.wait_ge(S["cWU"].h, S["cWU"].base + i + 1)
                        dinc(S["sTU"], i % 3, a.dma_start(
                            out=env.STup.ap()[k * 128:(k + 1) * 128, g_ * 512:(g_ + 1) * 512],
                            in_=wub[:, (i % 3) * 512:(i % 3 + 1) * 512]))
                a.wait_ge(S["cWU"].h, S["cWU"].base + 16 * (g_ + 1))
                for tc in range(2):
                    for ai in range(4):
                        au = 4 * g_ + ai
                        inc(S["cEvU"], a.copy(
                            env.UZ[:, au * 1024 + tc * 512:au * 1024 + (tc + 1) * 512],
                            P[tc * 4 + ai][:, :]))
            for g3 in range(22):
                na = GS[g3]
                a0 = GBASE[g3]
                if wdc:
                    for k in range(16):
                        i = g3 * 16 + k
                        a.wait_ge(S["cWG"].h, S["cWG"].base + i + 1)
                        dinc(S["sTG"], i % 3, a.dma_start(
                            out=env.STgate.ap()[k * 128:(k + 1) * 128, a0 * 128:a0 * 128 + na * 128],
                            in_=wgb[:, (i % 3) * 512:(i % 3) * 512 + na * 128]))
                a.wait_ge(S["cWG"].h, S["cWG"].base + 16 * (g3 + 1))
                for tc in range(2):
                    for ai in range(na):
                        j = 2 * sum(GS[:g3]) + tc * na + ai
                        a_ = a0 + ai
                        if j >= 2:
                            a.wait_ge(S["cZp"].h, S["cZp"].base + (j - 1))
                        inc(S["cSil"], a.activation(
                            silb[:, (j % 2) * 512:(j % 2 + 1) * 512],
                            P[tc * 3 + ai][:, :], AF.Silu,
                            scale=env.ct["cg_t"][:, a_:a_ + 1]))
                for tc in range(2):
                    for ai in range(na):
                        j = 2 * sum(GS[:g3]) + tc * na + ai
                        a_ = a0 + ai
                        a.wait_ge(S["cTH"].h, S["cTH"].base + j + 1)
                        inc(S["cZh"], a.copy(
                            env.UZ[:, a_ * 1024 + tc * 512:a_ * 1024 + (tc + 1) * 512],
                            P[6 + j % 2][:, :]))

        @blk.vector
        def _(v):
            _entry_waits(env, v)
            for g3 in range(22):
                na = GS[g3]
                a0 = GBASE[g3]
                for tc in range(2):
                    for ai in range(na):
                        j = 2 * sum(GS[:g3]) + tc * na + ai
                        a_ = a0 + ai
                        v.wait_ge(S["cSil"].h, S["cSil"].base + j + 1)
                        if j >= 2:
                            v.wait_ge(S["cTH"].h, S["cTH"].base + (j - 1))
                        inc(S["cZp"], v.scalar_tensor_tensor(
                            zpb[:, (j % 2) * 512:(j % 2 + 1) * 512],
                            env.UZ[:, a_ * 1024 + tc * 512:a_ * 1024 + (tc + 1) * 512],
                            env.ct["w2_t"][:, a_:a_ + 1],
                            silb[:, (j % 2) * 512:(j % 2 + 1) * 512],
                            ALU.mult, ALU.mult))
            for og in range(4):
                v.wait_ge(S["cSE"].h, S["cSE"].base + 64 * (og + 1))
                for tc in range(2):
                    for oci in range(4):
                        i = og * 8 + tc * 4 + oci
                        oc = og * 4 + oci
                        dwait(v, S["sXD"], i % 4, i // 4 + 1)
                        if i >= 4:
                            dwait(v, S["sXW"], i % 4, i // 4)
                        inc(S["cEvD"], v.scalar_tensor_tensor(
                            xob[:, (i % 4) * 512:(i % 4 + 1) * 512],
                            P[tc * 4 + oci][:, :],
                            env.ct["co2_t"][:, oc:oc + 1],
                            xcb[:, (i % 4) * 512:(i % 4 + 1) * 512],
                            ALU.mult, ALU.add))


def block_B3a1(env, xin):
    nc = env.nc
    S = dict(env.sems)
    S.update(_dsems(env, {"sXA1": (0, 2), "sXA2": (2, 2)}))
    _snap_all(env)
    with (
        nc.sbuf_tensor("c1_xc", [128, 2 * 512], F32) as xc,
        nc.sbuf_tensor("c1_xb", [128, 2 * 512], BF16) as xb,
        nc.sbuf_tensor("c1_sq", [128, 2 * 512], BF16) as sq,
        nc.sbuf_tensor("c1_x2", [128, 2 * 512], F32) as xc2,
        nc.sbuf_tensor("c1_ap", [128, 2 * 512], F32) as apb,
        nc.sbuf_tensor("c1_rows", [1, 8 * 512], F32) as rows,
        nc.psum_tensor("c1_ps0", [128, 512], F32) as ps_s1,
        nc.psum_tensor("c1_ps1", [128, 512], F32) as ps_s2,
        nc.psum_tensor("c1_pm", [128, 512], F32) as ps_m,
        nc.psum_tensor("c1_pr", [128, 512], F32) as ps_r,
        nc.Block() as blk,
    ):
        RM, RE, RM2, RV, RR, RL = 0, 1, 2, 3, 4, 5
        NT = 4

        @blk.gpsimd
        def _(g):
            _entry_waits(env, g)
            for tc in range(NT):
                for c in range(16):
                    i = tc * 16 + c
                    if i >= 2:
                        g.wait_ge(S["cXA1"].h, S["cXA1"].base + (i - 1))
                        g.wait_ge(S["cSq"].h, S["cSq"].base + (i - 1))
                    dinc(S["sXA1"], i % 2, g.dma_start(
                        out=xc[:, (i % 2) * 512:(i % 2 + 1) * 512],
                        in_=xin.ap()[c * 128:(c + 1) * 128, tc * 512:(tc + 1) * 512]))
                for c in range(16):
                    i = tc * 16 + c
                    if i >= 2:
                        g.wait_ge(S["cXA2"].h, S["cXA2"].base + (i - 1))
                    dinc(S["sXA2"], i % 2, g.dma_start(
                        out=xc2[:, (i % 2) * 512:(i % 2 + 1) * 512],
                        in_=xin.ap()[c * 128:(c + 1) * 128, tc * 512:(tc + 1) * 512]))

        @blk.vector
        def _(v):
            _entry_waits(env, v)
            for tc in range(NT):
                for c in range(16):
                    i = tc * 16 + c
                    dwait(v, S["sXA1"], i % 2, i // 2 + 1)
                    if i >= 2:
                        v.wait_ge(S["cTS"].h, S["cTS"].base + (i - 1))
                    inc(S["cXA1"], v.tensor_copy(xb[:, (i % 2) * 512:(i % 2 + 1) * 512],
                                                 xc[:, (i % 2) * 512:(i % 2 + 1) * 512]))
                v.wait_ge(S["cRowA"].h, S["cRowA"].base + tc * 4 + 3)
                inc(S["cRowV"], v.tensor_sub(rows[0:1, RV * 512:RV * 512 + 512],
                                             rows[0:1, RE * 512:RE * 512 + 512],
                                             rows[0:1, RM2 * 512:RM2 * 512 + 512]))
                for c in range(16):
                    i = tc * 16 + c
                    dwait(v, S["sXA2"], i % 2, i // 2 + 1)
                    v.wait_ge(S["cTB"].h, S["cTB"].base + 2 * (tc + 1))
                    inc(S["cXA2"], v.tensor_sub(apb[:, (i % 2) * 512:(i % 2 + 1) * 512],
                                                xc2[:, (i % 2) * 512:(i % 2 + 1) * 512],
                                                ps_m[:, :]))
                    inc(S["cAp"], v.tensor_mul(
                        env.Y2[:, c * 2048 + tc * 512:c * 2048 + (tc + 1) * 512],
                        apb[:, (i % 2) * 512:(i % 2 + 1) * 512], ps_r[:, :]))

        @blk.scalar
        def _(a):
            _entry_waits(env, a)
            for tc in range(NT):
                for c in range(16):
                    i = tc * 16 + c
                    dwait(a, S["sXA1"], i % 2, i // 2 + 1)
                    if i >= 2:
                        a.wait_ge(S["cTS"].h, S["cTS"].base + (i - 1))
                    inc(S["cSq"], a.square(sq[:, (i % 2) * 512:(i % 2 + 1) * 512],
                                           xc[:, (i % 2) * 512:(i % 2 + 1) * 512]))
                a.wait_ge(S["cTS"].h, S["cTS"].base + 16 * (tc + 1))
                inc(S["cRowA"], a.mul(rows[0:1, RM * 512:RM * 512 + 512], ps_s1[0:1, :], 1.0 / 2048.0))
                inc(S["cRowA"], a.mul(rows[0:1, RE * 512:RE * 512 + 512], ps_s2[0:1, :], 1.0 / 2048.0))
                inc(S["cRowA"], a.square(rows[0:1, RM2 * 512:RM2 * 512 + 512],
                                         rows[0:1, RM * 512:RM * 512 + 512]))
                a.wait_ge(S["cRowV"].h, S["cRowV"].base + tc + 1)
                inc(S["cRowA"], a.activation(rows[0:1, RL * 512:RL * 512 + 512],
                                             rows[0:1, RV * 512:RV * 512 + 512],
                                             AF.Ln, bias=EPS))
                inc(S["cRowR"], a.activation(rows[0:1, RR * 512:RR * 512 + 512],
                                             rows[0:1, RL * 512:RL * 512 + 512],
                                             AF.Exp, scale=-0.5))

        @blk.tensor
        def _(t):
            _entry_waits(env, t)
            for tc in range(NT):
                if tc >= 1:
                    t.wait_ge(S["cRowA"].h, S["cRowA"].base + (tc - 1) * 4 + 2)
                for c in range(16):
                    i = tc * 16 + c
                    t.wait_ge(S["cXA1"].h, S["cXA1"].base + i + 1)
                    t.wait_ge(S["cSq"].h, S["cSq"].base + i + 1)
                    t.matmul(ps_s1[0:1, :], env.ct["cOnesColB"][:, :],
                             xb[:, (i % 2) * 512:(i % 2 + 1) * 512],
                             start=(c == 0), stop=(c == 15))
                    inc(S["cTS"], t.matmul(ps_s2[0:1, :], env.ct["cOnesColB"][:, :],
                                           sq[:, (i % 2) * 512:(i % 2 + 1) * 512],
                                           start=(c == 0), stop=(c == 15)))
                if tc >= 1:
                    t.wait_ge(S["cAp"].h, S["cAp"].base + 16 * tc)
                t.wait_ge(S["cRowA"].h, S["cRowA"].base + tc * 4 + 1)
                inc(S["cTB"], t.matmul(ps_m[:, :], env.ct["cOnesRowF"][:, :],
                                       rows[0:1, RM * 512:RM * 512 + 512], start=True, stop=True))
                t.wait_ge(S["cRowR"].h, S["cRowR"].base + tc + 1)
                inc(S["cTB"], t.matmul(ps_r[:, :], env.ct["cOnesRowF"][:, :],
                                       rows[0:1, RR * 512:RR * 512 + 512], start=True, stop=True))


def block_B3a2(env):
    nc = env.nc
    S = dict(env.sems)
    S.update(_dsems(env, {"sWQ": (0, 2), "sWV": (2, 2)}))
    _snap_all(env)
    QU = [("q", h, u4) for h in range(8) for u4 in range(4)] + \
         [("k", kv, u4) for kv in range(2) for u4 in range(4)]
    LOADS = [("q", h) for h in range(8)] + [("k", kv) for kv in range(2)]

    def load_of_unit(ui):
        kind, hh, u4 = QU[ui]
        return hh if kind == "q" else 8 + hh

    with (
        nc.sbuf_tensor("c2_wq", [128, 2 * 4096], BF16) as wqb,
        nc.sbuf_tensor("c2_wv", [128, 2 * 512], BF16) as wvb,
        nc.sbuf_tensor("c2_cos", [128, 2048], BF16) as cosT,
        nc.sbuf_tensor("c2_sin", [128, 2048], BF16) as sinT,
        nc.sbuf_tensor("c2_ro", [128, 4 * 512], F32) as rob,
        nc.psum_tensor("c2_pa0", [128, 512], F32) as pa0,
        nc.psum_tensor("c2_pb0", [128, 512], F32) as pb0,
        nc.psum_tensor("c2_pa1", [128, 512], F32) as pa1,
        nc.psum_tensor("c2_pb1", [128, 512], F32) as pb1,
        nc.psum_tensor("c2_pv0", [128, 512], F32) as pv0,
        nc.psum_tensor("c2_pv1", [128, 512], F32) as pv1,
        nc.Block() as blk,
    ):
        PA = [pa0, pa1]
        PB = [pb0, pb1]
        PV = [pv0, pv1]

        @blk.gpsimd
        def _(g):
            _entry_waits(env, g)
            inc(S["cst"], g.dma_start(out=cosT[:, :], in_=env.io["cCos"][:, :]), 16)
            inc(S["cst"], g.dma_start(out=sinT[:, :], in_=env.io["cSin"][:, :]), 16)
            for lh, (kind, hh) in enumerate(LOADS):
                if lh >= 2:
                    g.wait_ge(S["cQK"].h, S["cQK"].base + 4 * (lh - 1))
                src = env.io["Wq"] if kind == "q" else env.io["Wk"]
                dinc(S["sWQ"], lh % 2, g.dma_start(
                    out=wqb[:, (lh % 2) * 4096:(lh % 2 + 1) * 4096].rearrange("p (k d) -> p k d", k=16),
                    in_=src[:, hh * 256:(hh + 1) * 256].rearrange("(k p) d -> p k d", p=128)))
            for tt in range(16):
                for k in range(16):
                    i = tt * 16 + k
                    if i >= 2:
                        g.wait_ge(S["cVmm"].h, S["cVmm"].base + (i - 1))
                    dinc(S["sWV"], i % 2, g.dma_start(
                        out=wvb[:, (i % 2) * 512:(i % 2 + 1) * 512],
                        in_=env.io["Wv"][k * 128:(k + 1) * 128, :]))

        @blk.tensor
        def _(t):
            _entry_waits(env, t)
            for ui in range(len(QU)):
                kind, hh, u4 = QU[ui]
                par = ui % 2
                lh = load_of_unit(ui)
                dwait(t, S["sWQ"], lh % 2, lh // 2 + 1)
                if ui >= 2:
                    t.wait_ge(S["cRope"].h, S["cRope"].base + 6 * (ui - 1))
                for k in range(16):
                    wcol = (lh % 2) * 4096 + k * 256
                    t.matmul(PA[par][:, :], wqb[:, wcol:wcol + 128],
                             env.Y2[:, k * 2048 + u4 * 512:k * 2048 + (u4 + 1) * 512],
                             start=(k == 0), stop=(k == 15))
                    insB = t.matmul(PB[par][:, :], wqb[:, wcol + 128:wcol + 256],
                                    env.Y2[:, k * 2048 + u4 * 512:k * 2048 + (u4 + 1) * 512],
                                    start=(k == 0), stop=(k == 15))
                inc(S["cQK"], insB)
            for tt in range(16):
                if tt >= 2:
                    t.wait_ge(S["cVe"].h, S["cVe"].base + (tt - 1))
                for k in range(16):
                    i = tt * 16 + k
                    dwait(t, S["sWV"], i % 2, i // 2 + 1)
                    insV = t.matmul(PV[tt % 2][:, :],
                                    env.Y2[:, k * 2048 + tt * 128:k * 2048 + (tt + 1) * 128],
                                    wvb[:, (i % 2) * 512:(i % 2 + 1) * 512],
                                    start=(k == 0), stop=(k == 15))
                    inc(S["cVmm"], insV)
                insV = None

        @blk.vector
        def _(v):
            _entry_waits(env, v)
            v.wait_ge(S["cst"].h, S["cst"].n)
            for ui in range(len(QU)):
                kind, hh, u4 = QU[ui]
                par = ui % 2
                v.wait_ge(S["cQK"].h, S["cQK"].base + ui + 1)
                dst = env.QT if kind == "q" else env.KT
                c0 = (2 * hh) * 2048 + u4 * 512
                c1 = (2 * hh + 1) * 2048 + u4 * 512
                coss = cosT[:, u4 * 512:(u4 + 1) * 512]
                sins = sinT[:, u4 * 512:(u4 + 1) * 512]
                v.tensor_mul(rob[:, 0:512], PA[par][:, :], coss)
                v.tensor_mul(rob[:, 512:1024], PB[par][:, :], sins)
                v.tensor_mul(rob[:, 1024:1536], PB[par][:, :], coss)
                v.tensor_mul(rob[:, 1536:2048], PA[par][:, :], sins)
                v.tensor_sub(dst[:, c0:c0 + 512], rob[:, 0:512], rob[:, 512:1024])
                inc(S["cRope"], v.tensor_add(dst[:, c1:c1 + 512],
                                             rob[:, 1024:1536], rob[:, 1536:2048]), 6)

        @blk.scalar
        def _(a):
            _entry_waits(env, a)
            for tt in range(16):
                a.wait_ge(S["cVmm"].h, S["cVmm"].base + 16 * (tt + 1))
                inc(S["cVe"], a.copy(env.V[:, tt * 512:(tt + 1) * 512], PV[tt % 2][:, :]))


def block_B3b(env, xin, xout):
    nc = env.nc
    S = dict(env.sems)
    S.update(_dsems(env, {"sWO": (0, 2), "sXD3": (2, 2), "sXW3": (4, 2)}))
    _snap_all(env)
    from contextlib import ExitStack
    with ExitStack() as _es:
        _e = _es.enter_context
        Eb = _e(nc.sbuf_tensor("c3_E", [128, 2 * 512], BF16))
        gEb = _e(nc.sbuf_tensor("c3_gE", [128, 2 * 512], BF16))
        albe = _e(nc.sbuf_tensor("c3_ab", [128, 2 * 512], F32))
        rows = _e(nc.sbuf_tensor("c3_rows", [1, 3 * 512], F32))
        ocb = _e(nc.sbuf_tensor("c3_oc", [128, 2 * 512], F32))
        wob = _e(nc.sbuf_tensor("c3_wo", [128, 2 * 1024], BF16))
        xcb = _e(nc.sbuf_tensor("c3_xc", [128, 2 * 512], F32))
        xob = _e(nc.sbuf_tensor("c3_xo", [128, 2 * 512], F32))
        sqq = _e(nc.sbuf_tensor("c3_sqq", [128, 2 * 512], BF16))
        qpb = _e(nc.sbuf_tensor("c3_qpb", [1, 2048], BF16))
        kpb = _e(nc.sbuf_tensor("c3_kpb", [1, 2048], BF16))
        scl = _e(nc.sbuf_tensor("c3_scl", [1, 64], F32))
        mcol = _e(nc.sbuf_tensor("c3_mcol", [128, 16], F32))
        oe0 = _e(nc.psum_tensor("c3_oe0", [128, 512], F32))
        oe1 = _e(nc.psum_tensor("c3_oe1", [128, 512], F32))
        og0 = _e(nc.psum_tensor("c3_og0", [128, 512], F32))
        og1 = _e(nc.psum_tensor("c3_og1", [128, 512], F32))
        prs = _e(nc.psum_tensor("c3_rs", [128, 512], F32))
        pgr = _e(nc.psum_tensor("c3_gr", [128, 512], F32))
        ps0 = _e(nc.psum_tensor("c3_s0", [128, 512], F32))
        ps1 = _e(nc.psum_tensor("c3_s1", [128, 512], F32))
        blk = _e(nc.Block())
        OE = [oe0, oe1]
        OG = [og0, og1]
        PS = [ps0, ps1]
        def prostats(h):
            items = []
            if h % 4 == 0:
                items += [("k2", u) for u in range(4)]
            items += [("kp", u) for u in range(4)]
            items += [("qp", u) for u in range(4)]
            items += [("q2", u) for u in range(4)]
            return items

        pro_base = [0] * 9
        for h in range(8):
            pro_base[h + 1] = pro_base[h] + len(prostats(h))
        sqcnt = []
        acc = 0
        for h in range(8):
            for (kindp, u) in prostats(h):
                if kindp in ("q2", "k2"):
                    acc += 1
                sqcnt.append(acc)
        km_after_kmax = {}
        km_after_qmax = {}
        km = 0
        for h in range(8):
            if h % 4 == 0:
                km += 1
                km_after_kmax[h] = km
            km += 1
            km_after_qmax[h] = km
            km += 2

        def nsc_of(hi):
            return sum(2 * (4 * (x % 4) + 4) for x in range(hi))

        HU = [(h, u) for h in range(8) for u in range(4)]
        NJ = {hi: 4 * HU[hi][1] + 4 for hi in range(32)}

        def av_base(hi):
            return sum(NJ[x] for x in range(hi))

        @blk.tensor
        def _(t):
            _entry_waits(env, t)
            for h in range(8):
                kv = h // 4
                if h >= 1:
                    t.wait_ge(S["cAbe"].h, S["cAbe"].base + 2 * (4 * h))
                for pi, (kindp, u) in enumerate(prostats(h)):
                    gi = pro_base[h] + pi
                    if gi >= 1:
                        t.wait_ge(S["cStE"].h, S["cStE"].base + gi)
                    if kindp == "kp":
                        ins = t.matmul(PS[0][0:1, :], env.ct["cDirs"][:, h:h + 1],
                                       env.KT[0:3, (2 * kv) * 2048 + u * 512:(2 * kv) * 2048 + (u + 1) * 512],
                                       start=True, stop=True)
                    elif kindp == "qp":
                        ins = t.matmul(PS[0][0:1, :], env.ct["cDirs"][:, h:h + 1],
                                       env.QT[0:3, (2 * h) * 2048 + u * 512:(2 * h) * 2048 + (u + 1) * 512],
                                       start=True, stop=True)
                    else:
                        t.wait_ge(S["cSqQ"].h, S["cSqQ"].base + 2 * sqcnt[gi])
                        for half in range(2):
                            ins = t.matmul(PS[0][32:33, :], env.ct["cOnesColB"][:, :],
                                           sqq[:, half * 512:(half + 1) * 512],
                                           start=(half == 0), stop=(half == 1))
                    inc(S["cStM"], ins)
                t.wait_ge(S["cBias"].h, S["cBias"].base + 2 * (h + 1))
                inc(S["cBcM"], t.matmul(PS[1][:, 0:1], env.ct["cOnesRowF"][:, :],
                                        scl[0:1, 10:11], start=True, stop=True))
                inc(S["cBcM"], t.matmul(PS[1][:, 1:2], env.ct["cOnesRowF"][:, :],
                                        scl[0:1, 11:12], start=True, stop=True))
                for u in range(4):
                    hi = h * 4 + u
                    nj = NJ[hi]
                    nsc = nsc_of(hi)
                    if hi >= 1:
                        t.wait_ge(S["cRowF"].h, S["cRowF"].base + 2 * hi)
                        t.wait_ge(S["cCmb"].h, S["cCmb"].base + 6 * hi)
                    t.wait_ge(S["cMce"].h, S["cMce"].base + 2 * (h + 1))
                    if u == 0:
                        t.wait_ge(S["cStE"].h, S["cStE"].base + pro_base[h + 1])
                    for j in range(nj):
                        last_main = (j < 4 * u)
                        if nsc >= 2:
                            t.wait_ge(S["cExpE"].h, S["cExpE"].base + nsc - 1)
                        t.matmul(PS[nsc % 2][:, :],
                                 env.KT[:, (2 * kv) * 2048 + j * 128:(2 * kv) * 2048 + (j + 1) * 128],
                                 env.QT[:, (2 * h) * 2048 + u * 512:(2 * h) * 2048 + (u + 1) * 512],
                                 start=True, stop=False)
                        ins = t.matmul(PS[nsc % 2][:, :],
                                       env.KT[:, (2 * kv + 1) * 2048 + j * 128:(2 * kv + 1) * 2048 + (j + 1) * 128],
                                       env.QT[:, (2 * h + 1) * 2048 + u * 512:(2 * h + 1) * 2048 + (u + 1) * 512],
                                       start=False, stop=last_main)
                        if not last_main:
                            ins = t.matmul(PS[nsc % 2][:, :], env.ct["cIDb"][:, :],
                                           env.ct["cMask"][:, (j - 4 * u) * 512:(j - 4 * u + 1) * 512],
                                           start=False, stop=True)
                        inc(S["cScM"], ins)
                        nsc += 1
                        if nsc >= 2:
                            t.wait_ge(S["cExpE"].h, S["cExpE"].base + nsc - 1)
                        ins = t.matmul(PS[nsc % 2][:, :],
                                       kpb[0:1, j * 128:(j + 1) * 128],
                                       qpb[0:1, u * 512:(u + 1) * 512],
                                       start=True, stop=last_main)
                        if not last_main:
                            ins = t.matmul(PS[nsc % 2][:, :], env.ct["cIDb"][:, :],
                                           env.ct["cMask"][:, (j - 4 * u) * 512:(j - 4 * u + 1) * 512],
                                           start=False, stop=True)
                        inc(S["cScM"], ins)
                        nsc += 1
                        t.wait_ge(S["cExpE"].h, S["cExpE"].base + nsc)
                        t.matmul(prs[0:1, :], env.ct["cOnesColB"][:, :],
                                 Eb[:, (j % 2) * 512:(j % 2 + 1) * 512],
                                 start=(j == 0), stop=(j == nj - 1))
                        t.matmul(pgr[0:1, :], env.ct["cOnesColB"][:, :],
                                 gEb[:, (j % 2) * 512:(j % 2 + 1) * 512],
                                 start=(j == 0), stop=(j == nj - 1))
                        for half in range(2):
                            vsl = j * 512 + kv * 256 + half * 128
                            t.matmul(OE[half][:, :], env.V[:, vsl:vsl + 128],
                                     Eb[:, (j % 2) * 512:(j % 2 + 1) * 512],
                                     start=(j == 0), stop=(j == nj - 1))
                            ins = t.matmul(OG[half][:, :], env.V[:, vsl:vsl + 128],
                                           gEb[:, (j % 2) * 512:(j % 2 + 1) * 512],
                                           start=(j == 0), stop=(j == nj - 1))
                        inc(S["cAV"], ins)
                    t.wait_ge(S["cRowF"].h, S["cRowF"].base + 2 * (hi + 1))
                    t.wait_ge(S["cExpE"].h, S["cExpE"].base + nsc)
                    inc(S["cBcA"], t.matmul(PS[0][:, :], env.ct["cOnesRowF"][:, :],
                                            rows[0:1, 0:512], start=True, stop=True))
                    inc(S["cBcA"], t.matmul(PS[1][:, :], env.ct["cOnesRowF"][:, :],
                                            rows[0:1, 512:1024], start=True, stop=True))
            t.wait_ge(S["cCmb"].h, S["cCmb"].base + 32 * 6)
            t.wait_ge(S["cAbe"].h, S["cAbe"].base + 64)
            for oc in range(16):
                for u in range(4):
                    i = oc * 4 + u
                    w = oc * 2 + u // 2
                    if i >= 2:
                        t.wait_ge(S["cOe"].h, S["cOe"].base + (i - 1))
                    for l in range(2):
                        LI = w * 2 + l
                        dwait(t, S["sWO"], LI % 2, LI // 2 + 1)
                        for k8 in range(8):
                            k = l * 8 + k8
                            insO = t.matmul(PS[i % 2][:, :],
                                            wob[:, (LI % 2) * 1024 + k8 * 128:(LI % 2) * 1024 + (k8 + 1) * 128],
                                            env.AO[:, k * 2048 + u * 512:k * 2048 + (u + 1) * 512],
                                            start=(k == 0), stop=(k == 15))
                    inc(S["cWoM"], insO)

        @blk.vector
        def _(v):
            _entry_waits(env, v)
            for h in range(8):
                kv = h // 4
                for pi, (kindp, u) in enumerate(prostats(h)):
                    gi = pro_base[h] + pi
                    v.wait_ge(S["cStM"].h, S["cStM"].base + gi + 1)
                    if kindp == "kp":
                        if pi == [p for p, (kk, _) in enumerate(prostats(h)) if kk == "kp"][0] and h >= 1:
                            v.wait_ge(S["cScM"].h, S["cScM"].base + nsc_of(4 * h))
                        inc(S["cStE"], v.tensor_copy(kpb[0:1, u * 512:(u + 1) * 512], PS[0][0:1, :]))
                    elif kindp == "qp":
                        inc(S["cStE"], v.tensor_copy(qpb[0:1, u * 512:(u + 1) * 512], PS[0][0:1, :]))
                    elif kindp == "q2":
                        inc(S["cStE"], v.reduce_max(scl[0:1, u:u + 1], PS[0][32:33, :], AX.X))
                    else:
                        inc(S["cStE"], v.reduce_max(scl[0:1, 16 + u:17 + u], PS[0][32:33, :], AX.X))
                if h % 4 == 0:
                    inc(S["cKm"], v.reduce_max(scl[0:1, 20:21], scl[0:1, 16:20], AX.X))
                inc(S["cKm"], v.reduce_max(scl[0:1, 4:5], scl[0:1, 0:4], AX.X))
                inc(S["cKm"], v.reduce_max(scl[0:1, 8:9], qpb[0:1, :], AX.X,
                                           apply_absolute_value=True))
                inc(S["cKm"], v.reduce_max(scl[0:1, 9:10], kpb[0:1, :], AX.X,
                                           apply_absolute_value=True))
                v.wait_ge(S["cSqt"].h, S["cSqt"].base + (h + 1) + (h // 4 + 1))
                inc(S["cBias"], v.tensor_scalar(scl[0:1, 10:11], scl[0:1, 5:6],
                                                scl[0:1, 6 + kv:7 + kv], -SAFE / 16.0,
                                                ALU.mult, ALU.mult))
                inc(S["cBias"], v.tensor_scalar(scl[0:1, 11:12], scl[0:1, 8:9],
                                                scl[0:1, 9:10], -SAFE,
                                                ALU.mult, ALU.mult))
                for u in range(4):
                    hi = h * 4 + u
                    nj = NJ[hi]
                    v.wait_ge(S["cAV"].h, S["cAV"].base + av_base(hi) + nj)
                    v.reciprocal(rows[0:1, 1024:1536], prs[0:1, :])
                    inc(S["cRowF"], v.tensor_single_scalar(rows[0:1, 0:512],
                                                           rows[0:1, 1024:1536],
                                                           env.ct["hs1"][0:1, h:h + 1], ALU.mult))
                    v.reciprocal(rows[0:1, 1024:1536], pgr[0:1, :])
                    inc(S["cRowF"], v.tensor_single_scalar(rows[0:1, 512:1024],
                                                           rows[0:1, 1024:1536],
                                                           env.ct["hs"][0:1, h:h + 1], ALU.mult))
                    v.wait_ge(S["cAbe"].h, S["cAbe"].base + 2 * (hi + 1))
                    for half in range(2):
                        v.tensor_mul(ocb[:, 0:512], OE[half][:, :], albe[:, 0:512])
                        v.tensor_mul(ocb[:, 512:1024], OG[half][:, :], albe[:, 512:1024])
                        inc(S["cCmb"], v.tensor_add(
                            env.AO[:, (2 * h + half) * 2048 + u * 512:(2 * h + half) * 2048 + (u + 1) * 512],
                            ocb[:, 0:512], ocb[:, 512:1024]), 3)
            for oc in range(16):
                for u in range(4):
                    i = oc * 4 + u
                    v.wait_ge(S["cWoM"].h, S["cWoM"].base + i + 1)
                    dwait(v, S["sXD3"], i % 2, i // 2 + 1)
                    if i >= 2:
                        dwait(v, S["sXW3"], i % 2, i // 2)
                    inc(S["cOe"], v.tensor_add(xob[:, (i % 2) * 512:(i % 2 + 1) * 512],
                                               PS[i % 2][:, :],
                                               xcb[:, (i % 2) * 512:(i % 2 + 1) * 512]))

        @blk.scalar
        def _(a):
            _entry_waits(env, a)
            for h in range(8):
                kv = h // 4
                for pi, (kindp, u) in enumerate(prostats(h)):
                    gi = pro_base[h] + pi
                    if kindp == "k2":
                        src_c = 2 * kv
                        srcT = env.KT
                    elif kindp == "q2":
                        src_c = 2 * h
                        srcT = env.QT
                    else:
                        continue
                    if gi >= 1:
                        a.wait_ge(S["cStE"].h, S["cStE"].base + gi)
                    for half in range(2):
                        inc(S["cSqQ"], a.square(
                            sqq[:, half * 512:(half + 1) * 512],
                            srcT[:, (src_c + half) * 2048 + u * 512:(src_c + half) * 2048 + (u + 1) * 512]))
                if h % 4 == 0:
                    a.wait_ge(S["cKm"].h, S["cKm"].base + km_after_kmax[h])
                    inc(S["cSqt"], a.activation(scl[0:1, 6 + kv:7 + kv], scl[0:1, 20:21], AF.Sqrt))
                a.wait_ge(S["cKm"].h, S["cKm"].base + km_after_qmax[h])
                inc(S["cSqt"], a.activation(scl[0:1, 5:6], scl[0:1, 4:5], AF.Sqrt))
                a.wait_ge(S["cBcM"].h, S["cBcM"].base + 2 * (h + 1))
                inc(S["cMce"], a.copy(mcol[:, 2 * h:2 * h + 1], PS[1][:, 0:1]))
                inc(S["cMce"], a.copy(mcol[:, 2 * h + 1:2 * h + 2], PS[1][:, 1:2]))
                for u in range(4):
                    hi = h * 4 + u
                    nj = NJ[hi]
                    nsc = nsc_of(hi)
                    for j in range(nj):
                        a.wait_ge(S["cScM"].h, S["cScM"].base + nsc + 1)
                        if j >= 2:
                            a.wait_ge(S["cAV"].h, S["cAV"].base + av_base(hi) + (j - 1))
                        inc(S["cExpE"], a.activation(Eb[:, (j % 2) * 512:(j % 2 + 1) * 512],
                                                     PS[nsc % 2][:, :], AF.Exp,
                                                     bias=mcol[:, 2 * h:2 * h + 1], scale=0.0625))
                        nsc += 1
                        a.wait_ge(S["cScM"].h, S["cScM"].base + nsc + 1)
                        inc(S["cExpE"], a.activation(gEb[:, (j % 2) * 512:(j % 2 + 1) * 512],
                                                     PS[nsc % 2][:, :], AF.Exp,
                                                     bias=mcol[:, 2 * h + 1:2 * h + 2]))
                        nsc += 1
                    a.wait_ge(S["cBcA"].h, S["cBcA"].base + 2 * (hi + 1))
                    if hi >= 1:
                        a.wait_ge(S["cCmb"].h, S["cCmb"].base + 6 * hi)
                    inc(S["cAbe"], a.copy(albe[:, 0:512], PS[0][:, :]))
                    inc(S["cAbe"], a.copy(albe[:, 512:1024], PS[1][:, :]))

        @blk.gpsimd
        def _(g):
            _entry_waits(env, g)

            def ld(LI):
                w, l = divmod(LI, 2)
                oc, up = divmod(w, 2)
                if LI >= 2:
                    g.wait_ge(S["cWoM"].h, S["cWoM"].base + 2 * ((LI - 2) // 2 + 1))
                dinc(S["sWO"], LI % 2, g.dma_start(
                    out=wob[:, (LI % 2) * 1024:(LI % 2 + 1) * 1024].rearrange("p (k o) -> p k o", k=8),
                    in_=env.io["Wo"][l * 1024:(l + 1) * 1024, oc * 128:(oc + 1) * 128].rearrange("(k p) o -> p k o", p=128)))

            def rd(i):
                oc, u = divmod(i, 4)
                if i >= 2:
                    g.wait_ge(S["cOe"].h, S["cOe"].base + (i - 1))
                dinc(S["sXD3"], i % 2, g.dma_start(
                    out=xcb[:, (i % 2) * 512:(i % 2 + 1) * 512],
                    in_=xin.ap()[oc * 128:(oc + 1) * 128, u * 512:(u + 1) * 512]))

            def wr(i):
                oc, u = divmod(i, 4)
                g.wait_ge(S["cOe"].h, S["cOe"].base + i + 1)
                dinc(S["sXW3"], i % 2, g.dma_start(
                    out=xout.ap()[oc * 128:(oc + 1) * 128, u * 512:(u + 1) * 512],
                    in_=xob[:, (i % 2) * 512:(i % 2 + 1) * 512]))

            ld(0)
            ld(1)
            rd(0)
            rd(1)
            for i in range(64):
                if i >= 2 and i % 2 == 0:
                    ld(i)
                    ld(i + 1)
                wr(i)
                if i + 2 < 64:
                    rd(i + 2)



def block_P5(env, src, dst, final):
    nc = env.nc
    S = dict(env.sems)
    S.update(_dsems(env, {"sX0": (0, 2), "sXTw": (2, 2)}))
    _snap_all(env)
    with (
        nc.sbuf_tensor("p5_xbuf", [128, 2 * 2048], F32) as xbuf,
        nc.sbuf_tensor("p5_trb", [128, 2 * 128], F32) as trb,
        nc.psum_tensor("p5_pt0", [128, 128], F32) as pt0,
        nc.psum_tensor("p5_pt1", [128, 128], F32) as pt1,
        nc.Block() as blk,
    ):
        pt = [pt0, pt1]

        @blk.gpsimd
        def _(g):
            _entry_waits(env, g)
            for tt in range(16):
                if tt >= 2:
                    g.wait_ge(S["cP1t"].h, S["cP1t"].base + 16 * (tt - 1))
                dinc(S["sX0"], tt % 2, g.dma_start(out=xbuf[:, (tt % 2) * 2048:(tt % 2 + 1) * 2048],
                                                   in_=src.ap()[tt * 128:(tt + 1) * 128, :]))

        @blk.tensor
        def _(t):
            _entry_waits(env, t)
            for i in range(256):
                tt, k = divmod(i, 16)
                if k == 0:
                    dwait(t, S["sX0"], tt % 2, tt // 2 + 1)
                if i >= 2:
                    t.wait_ge(S["cP1e"].h, S["cP1e"].base + (i - 1))
                inc(S["cP1t"], t.transpose(pt[i % 2][:, :],
                                           xbuf[:, (tt % 2) * 2048 + k * 128:(tt % 2) * 2048 + (k + 1) * 128],
                                           env.ct["cIDf"][:, :]))

        @blk.scalar
        def _(a):
            _entry_waits(env, a)
            for i in range(256):
                a.wait_ge(S["cP1t"].h, S["cP1t"].base + i + 1)
                if i >= 2:
                    dwait(a, S["sXTw"], i % 2, i // 2)
                inc(S["cP1e"], a.copy(trb[:, (i % 2) * 128:(i % 2 + 1) * 128],
                                      pt[i % 2][:, :]))

        @blk.sync
        def _(s):
            _entry_waits(env, s)
            for i in range(256):
                tt, k = divmod(i, 16)
                s.wait_ge(S["cP1e"].h, S["cP1e"].base + i + 1)
                dinc(S["sXTw"], i % 2, s.dma_start(out=dst.ap()[k * 128:(k + 1) * 128, tt * 128:(tt + 1) * 128],
                                                   in_=trb[:, (i % 2) * 128:(i % 2 + 1) * 128]))
            if final:
                dwait_all(s, S["sXTw"])


def build_nc():
    nc = bass.Bass("TRN2", target_bir_lowering=False)
    env = Env()
    env.nc = nc

    io = {}
    for name, arr in _CONSTS.items():
        io[name] = nc.dram_tensor(name, list(arr.shape),
                                  BF16 if arr.dtype == BF else F32,
                                  kind="ExternalInput").ap()
    for name, shape in [("x", [T, C]), ("Wq", [C, C]), ("Wk", [C, 512]),
                        ("Wv", [C, 512]), ("Wo", [C, C]), ("head_scales", [8]),
                        ("SU_ug", [C]), ("SU_dp", [INTER]), ("Wscale_ug", [2 * INTER]),
                        ("Wscale_dp", [C]), ("W_up", [INTER, C]), ("W_gate", [INTER, C]),
                        ("W_down", [C, INTER])]:
        io[name] = nc.dram_tensor(name, shape, F32, kind="ExternalInput").ap()
    env.io = io

    env.xT = nc.dram_tensor("xT", [C, T], F32, kind="Internal")
    env.x1T = nc.dram_tensor("x1T", [C, T], F32, kind="Internal")
    env.x2T = nc.dram_tensor("x2T", [C, T], F32, kind="Internal")
    env.x3T = nc.dram_tensor("x3T", [C, T], F32, kind="Internal")
    env.SNup = nc.dram_tensor("SNup", [INTER, C], BF16, kind="Internal")
    env.SNgate = nc.dram_tensor("SNgate", [INTER, C], BF16, kind="Internal")
    env.SNdown = nc.dram_tensor("SNdown", [C, INTER], BF16, kind="Internal")
    env.STdown = nc.dram_tensor("STdown", [INTER, C], BF16, kind="Internal")
    env.STup = nc.dram_tensor("STup", [C, INTER], BF16, kind="Internal")
    env.STgate = nc.dram_tensor("STgate", [C, INTER], BF16, kind="Internal")
    env.PBdump = nc.dram_tensor("PBdump", [4, 64, 128, 512], BF16, kind="Internal")
    env.SEff = nc.dram_tensor("SEff", [4, 64, 128, 512], BF16, kind="Internal")
    env.y = nc.dram_tensor("y", [T, C], F32, kind="ExternalOutput")
    if DBG:
        env.dbg = {k: nc.dram_tensor(k, [C, T], F32, kind="ExternalOutput")
                   for k in ("dbg_x1", "dbg_x2")}
        env.dbg["dbg_app"] = nc.dram_tensor("dbg_app", [128, 16 * 1024], BF16, kind="ExternalOutput")
        env.dbg["dbg_seff"] = nc.dram_tensor("dbg_seff", [4, 64, 128, 512], BF16, kind="ExternalOutput")
        env.dbg["dbg_y2"] = nc.dram_tensor("dbg_y2", [128, 16 * 2048], BF16, kind="ExternalOutput")
        env.dbg["dbg_qt"] = nc.dram_tensor("dbg_qt", [128, 16 * 2048], BF16, kind="ExternalOutput")
        env.dbg["dbg_kt"] = nc.dram_tensor("dbg_kt", [128, 4 * 2048], BF16, kind="ExternalOutput")
        env.dbg["dbg_v"] = nc.dram_tensor("dbg_v", [128, 16 * 512], BF16, kind="ExternalOutput")

    sem_names = [
        "cst", "cP1t", "cP1e", "sCu", "sCg", "sCd",
        "cPBm", "cPBe", "cDt", "cDe", "cDw",
        "cXA1", "cSq", "cXA2", "cTS", "cRowA", "cRowV", "cRowR",
        "cTB", "cAp", "cH16", "cYb", "cT128", "cAe",
        "cWU", "cWG", "cSE", "cEvU", "cSil",
        "cZp", "cTH", "cZh", "cEvD",
        "cQK", "cRope", "cVmm", "cVe", "cStM", "cStE", "cSqQ",
        "cScM", "cExpE", "cAV", "cRowF", "cBcA", "cAbe", "cCmb", "cWoM", "cOe",
        "cBias", "cBcM", "cMce", "cKm", "cSqt",
    ]
    N_DCELLS = 29

    from contextlib import ExitStack
    with ExitStack() as st:
        env.sems = {}
        for nm in sem_names:
            env.sems[nm] = Sem(st.enter_context(nc.semaphore(nm)))
        env.dcells = [Sem(st.enter_context(nc.semaphore(f"dma{k}")))
                      for k in range(N_DCELLS)]

        ct = {}
        for name, arr in _CONSTS.items():
            if name in ("cCos", "cSin"):
                continue
            dt = BF16 if arr.dtype == BF else F32
            ct[name] = st.enter_context(nc.sbuf_tensor("t_" + name, list(arr.shape), dt))
        ct["hs"] = st.enter_context(nc.sbuf_tensor("t_hs", [1, 8], F32))
        ct["hs1"] = st.enter_context(nc.sbuf_tensor("t_hs1", [1, 8], F32))
        ct["su_t"] = st.enter_context(nc.sbuf_tensor("t_su", [128, 16], F32))
        ct["co2_t"] = st.enter_context(nc.sbuf_tensor("t_co2", [128, 16], F32))
        ct["sd_t"] = st.enter_context(nc.sbuf_tensor("t_sd", [128, 64], F32))
        ct["cu_t"] = st.enter_context(nc.sbuf_tensor("t_cu", [128, 64], F32))
        ct["cg_t"] = st.enter_context(nc.sbuf_tensor("t_cg", [128, 64], F32))
        ct["w2_t"] = st.enter_context(nc.sbuf_tensor("t_w2", [128, 64], F32))
        env.ct = ct
        env.cst = None
        nc.const_aps.register(F32, EPS, ct["cMisc"][0:1, 0:1])

        block_B1(env)

        def ffn_call(xin, xout, tagp, wdc_call=False):
            g_app = nc.sbuf_tensor("t_app" + tagp, [128, 16 * SUP], BF16)
            env.APP = g_app.__enter__()
            for u in range(NSUP):
                wdc = wdc_call and u == 0
                block_B2a(env, xin, u, tagp + f"u{u}", wdc=wdc)
                g_uz = nc.sbuf_tensor(f"t_uz_{tagp}u{u}", [128, 64 * SUP], BF16)
                env.UZ = g_uz.__enter__()
                block_B2b(env, xin, xout, u, tagp + f"u{u}", wdc=wdc)
                g_uz.__exit__(None, None, None)
            g_app.__exit__(None, None, None)

        ffn_call(env.xT, env.x1T, "f1", wdc_call=True)

        g_qt = nc.sbuf_tensor("t_qt", [128, 16 * 2048], BF16)
        g_kt = nc.sbuf_tensor("t_kt", [128, 4 * 2048], BF16)
        g_v = nc.sbuf_tensor("t_v", [128, 16 * 512], BF16)
        env.QT = g_qt.__enter__()
        env.KT = g_kt.__enter__()
        env.V = g_v.__enter__()
        g_y2 = nc.sbuf_tensor("t_y2", [128, 16 * 2048], BF16)
        env.Y2 = g_y2.__enter__()
        block_B3a1(env, env.x1T)
        block_B3a2(env)
        g_y2.__exit__(None, None, None)
        g_ao = nc.sbuf_tensor("t_ao", [128, 16 * 2048], BF16)
        env.AO = g_ao.__enter__()
        block_B3b(env, env.x1T, env.x2T)
        g_ao.__exit__(None, None, None)
        g_v.__exit__(None, None, None)
        g_kt.__exit__(None, None, None)
        g_qt.__exit__(None, None, None)

        ffn_call(env.x2T, env.x3T, "f2")

        block_P5(env, env.x3T, env.y, final=True)

        if DBG:
            block_dbg(env)
    return nc


def block_dbg(env):
    nc = env.nc
    S = env.sems
    _snap_all(env)
    with nc.Block() as blk:
        @blk.gpsimd
        def _(g):
            _entry_waits(env, g)
            for i in range(4):
                inc(S["cst"], g.dma_start(out=env.dbg["dbg_x1"].ap()[i * 512:(i + 1) * 512, :],
                                          in_=env.x1T.ap()[i * 512:(i + 1) * 512, :]), 16)
                inc(S["cst"], g.dma_start(out=env.dbg["dbg_x2"].ap()[i * 512:(i + 1) * 512, :],
                                          in_=env.x2T.ap()[i * 512:(i + 1) * 512, :]), 16)
            for og in range(4):
                inc(S["cst"], g.dma_start(out=env.dbg["dbg_seff"].ap()[og],
                                          in_=env.SEff.ap()[og]), 16)
            g.wait_ge(S["cst"].h, S["cst"].n)


_NC_CACHE = None


def kernel(**inputs):
    global _NC_CACHE
    if _NC_CACHE is None:
        _NC_CACHE = build_nc()
    nc = _NC_CACHE
    x = np.ascontiguousarray(np.asarray(inputs["L0_attn_x"], dtype=np.float32))
    base = {}
    for name, arr in _CONSTS.items():
        base[name] = np.ascontiguousarray(arr)
    key_map = {
        "Wq": "L0_attn_Wq", "Wk": "L0_attn_Wk", "Wv": "L0_attn_Wv", "Wo": "L0_attn_Wo",
        "head_scales": "L0_attn_head_scales", "SU_ug": "L0_ffn_SU_ug",
        "SU_dp": "L0_ffn_SU_dp", "Wscale_ug": "L0_ffn_Wscale_ug",
        "Wscale_dp": "L0_ffn_Wscale_dp", "W_up": "L0_ffn_W_up",
        "W_gate": "L0_ffn_W_gate", "W_down": "L0_ffn_W_down",
    }
    for k, src in key_map.items():
        base[k] = np.ascontiguousarray(np.asarray(inputs[src], dtype=np.float32))
    in_maps = []
    for b in range(8):
        m = dict(base)
        m["x"] = x[b]
        in_maps.append(m)
    res = run_bass_kernel_spmd(nc, in_maps, core_ids=list(range(8)))
    out = np.stack([np.asarray(r["y"]) for r in res.results], axis=0)
    if DBG:
        kernel._dbg = [{k: np.asarray(v) for k, v in r.items()} for r in res.results]
    return out


# version 08
# speedup vs baseline: 1.0387x; 1.0387x over previous
import os
import numpy as np
import ml_dtypes
import concourse.bass as bass
import concourse.mybir as mybir
from concourse.bass_utils import run_bass_kernel_spmd

F32 = mybir.dt.float32
BF16 = mybir.dt.bfloat16
AF = mybir.ActivationFunctionType
ALU = mybir.AluOpType
AX = mybir.AxisListType

DBG = bool(int(os.environ.get("MK_DBG", "0")))

T = 2048
C = 2048
INTER = 8192
SUP = 1024
NSUP = T // SUP
NEG = -1.0e9
EPS = 1e-5
SAFE = 1.02

BF = ml_dtypes.bfloat16


def _hmat(n):
    h = np.array([[1.0]], dtype=np.float32)
    while h.shape[0] < n:
        h = np.block([[h, h], [h, -h]])
    return h.astype(np.float32)


def _consts():
    H128 = _hmat(128)
    H64 = _hmat(64)
    BD = np.kron(np.eye(2, dtype=np.float32), H64)
    I128 = np.eye(128, dtype=np.float32)
    onescol = np.ones((128, 1), np.float32)
    onesrow = np.ones((1, 128), np.float32)
    inv = 1.0 / (10000.0 ** (np.arange(0, 256, 2, dtype=np.float64) / 256.0))
    f = np.arange(T, dtype=np.float64)[:, None] * inv[None, :]
    cosL = np.cos(f).T.astype(np.float32)
    sinL = np.sin(f).T.astype(np.float32)
    masks = np.zeros((128, 4 * 512), np.float32)
    for r in range(4):
        sp = np.arange(128)[:, None] + r * 128
        tf = np.arange(512)[None, :]
        masks[:, r * 512:(r + 1) * 512] = np.where(sp <= tf, 0.0, NEG)
    dirs = np.array([[sx, sy, sz] for sx in (1.0, -1.0) for sy in (1.0, -1.0)
                     for sz in (1.0, -1.0)], dtype=np.float32) / np.sqrt(3.0)
    return dict(
        cHb=H128.astype(BF), cBD=BD.astype(BF), cIDf=I128, cIDb=I128.astype(BF),
        cOnesColB=onescol.astype(BF), cOnesRowB=onesrow.astype(BF),
        cOnesRowF=onesrow, cCos=cosL.astype(BF), cSin=sinL.astype(BF),
        cMask=masks.astype(BF), cDirs=dirs.T.copy().astype(BF),
        cMisc=np.array([[EPS, 0.0, 0.0, 0.0]], np.float32),
    )


_CONSTS = _consts()


class Sem:
    __slots__ = ("h", "n", "base")

    def __init__(self, h):
        self.h = h
        self.n = 0
        self.base = 0


class DSem:
    __slots__ = ("cells",)

    def __init__(self, cells):
        self.cells = cells


class Env:
    pass


def _dsems(env, spec):
    return {nm: DSem(env.dcells[s0:s0 + d]) for nm, (s0, d) in spec.items()}


def _snap_all(env):
    for s in env.sems.values():
        s.base = s.n
    for c in env.dcells:
        c.base = c.n


def _entry_waits(env, eng):
    for s in env.sems.values():
        if s.base > 0:
            eng.wait_ge(s.h, s.base)
    for c in env.dcells:
        if c.base > 0:
            eng.wait_ge(c.h, c.base)


def inc(s, ins, k=1):
    ins.then_inc(s.h, k)
    s.n += k
    return ins


def dinc(ds, slot, ins):
    c = ds.cells[slot]
    ins.then_inc(c.h, 16)
    c.n += 16
    return ins


def dwait(eng, ds, slot, rounds):
    c = ds.cells[slot]
    eng.wait_ge(c.h, c.base + 16 * rounds)


def dwait_first(eng, ds, k):
    d = len(ds.cells)
    for s in range(d):
        cnt = (k - s + d - 1) // d
        if cnt > 0:
            eng.wait_ge(ds.cells[s].h, ds.cells[s].base + 16 * cnt)


def dwait_all(eng, ds):
    for c in ds.cells:
        if c.n > 0:
            eng.wait_ge(c.h, c.n)


def block_B1(env):
    nc = env.nc
    S = dict(env.sems)
    S.update(_dsems(env, {"sX0": (0, 2), "sXTw": (2, 2)}))
    _snap_all(env)
    with (
        nc.sbuf_tensor("b1_xbuf", [128, 2 * 2048], F32) as xbuf,
        nc.sbuf_tensor("b1_trb", [128, 2 * 128], F32) as trb,
        nc.sbuf_tensor("b1_vecload", [64, 6 * 128], F32) as vl,
        nc.psum_tensor("b1_pt0", [128, 512], F32) as pt0,
        nc.psum_tensor("b1_pt1", [128, 512], F32) as pt1,
        nc.Block() as blk,
    ):
        pt = [pt0, pt1]
        io = env.io

        NV = {"su": 0, "wdp": 1, "sd": 2, "cu": 3, "cg": 4}

        @blk.gpsimd
        def _(g):
            _entry_waits(env, g)
            for name in ("cHb", "cBD", "cIDf", "cIDb", "cOnesColB", "cOnesRowB",
                         "cOnesRowF", "cMask", "cDirs", "cMisc"):
                inc(S["cst"], g.dma_start(out=env.ct[name][:], in_=io[name][:]), 16)
            inc(S["cst"], g.dma_start(out=env.ct["hs"][:], in_=io["head_scales"].rearrange("(o h) -> o h", o=1)), 16)
            inc(S["cst"], g.dma_start(out=vl[0:16, 0 * 128:1 * 128],
                                      in_=io["SU_ug"].rearrange("(c p) -> c p", p=128)), 16)
            inc(S["cst"], g.dma_start(out=vl[0:16, 1 * 128:2 * 128],
                                      in_=io["Wscale_dp"].rearrange("(c p) -> c p", p=128)), 16)
            inc(S["cst"], g.dma_start(out=vl[0:64, 2 * 128:3 * 128],
                                      in_=io["SU_dp"].rearrange("(c p) -> c p", p=128)), 16)
            inc(S["cst"], g.dma_start(out=vl[0:64, 3 * 128:4 * 128],
                                      in_=io["Wscale_ug"][0:8192].rearrange("(c p) -> c p", p=128)), 16)
            inc(S["cst"], g.dma_start(out=vl[0:64, 4 * 128:5 * 128],
                                      in_=io["Wscale_ug"][8192:16384].rearrange("(c p) -> c p", p=128)), 16)
            for tt in range(16):
                if tt >= 2:
                    g.wait_ge(S["cP1t"].h, S["cP1t"].base + 16 * (tt - 1))
                dinc(S["sX0"], tt % 2, g.dma_start(out=xbuf[:, (tt % 2) * 2048:(tt % 2 + 1) * 2048],
                                                   in_=io["x"][tt * 128:(tt + 1) * 128, :]))

        @blk.sync
        def _(s):
            _entry_waits(env, s)
            for i in range(256):
                tt, k = divmod(i, 16)
                s.wait_ge(S["cP1e"].h, S["cP1e"].base + i + 1)
                dinc(S["sXTw"], i % 2, s.dma_start(out=env.xT.ap()[k * 128:(k + 1) * 128, tt * 128:(tt + 1) * 128],
                                                   in_=trb[:, (i % 2) * 128:(i % 2 + 1) * 128]))

        @blk.tensor
        def _(t):
            _entry_waits(env, t)
            t.wait_ge(S["cst"].h, S["cst"].n)
            for idx, (name, P) in enumerate([("su", 16), ("wdp", 16), ("sd", 64),
                                             ("cu", 64), ("cg", 64)]):
                sl = NV[name]
                if idx >= 2:
                    t.wait_ge(S["cDe"].h, S["cDe"].base + (idx - 1))
                inc(S["cDt"], t.transpose(pt[idx % 2][:, 0:P],
                                          vl[0:P, sl * 128:sl * 128 + 128],
                                          env.ct["cIDf"][0:P, 0:P]))
            t.wait_ge(S["cDe"].h, S["cDe"].base + 5)
            for i in range(256):
                tt, k = divmod(i, 16)
                if k == 0:
                    dwait(t, S["sX0"], tt % 2, tt // 2 + 1)
                if i >= 2:
                    t.wait_ge(S["cP1e"].h, S["cP1e"].base + (i - 1))
                inc(S["cP1t"], t.transpose(pt[i % 2][:, 0:128],
                                           xbuf[:, (tt % 2) * 2048 + k * 128:(tt % 2) * 2048 + (k + 1) * 128],
                                           env.ct["cIDf"][:, :]))

        @blk.scalar
        def _(a):
            _entry_waits(env, a)
            for idx, (name, P) in enumerate([("su", 16), ("wdp", 16), ("sd", 64),
                                             ("cu", 64), ("cg", 64)]):
                a.wait_ge(S["cDt"].h, S["cDt"].base + idx + 1)
                scale = 1.0
                if name == "wdp":
                    scale = 2.0 ** -7.5
                elif name in ("cu", "cg"):
                    scale = 2.0 ** -5.5
                dst = {"su": env.ct["su_t"], "wdp": env.ct["co2_t"], "sd": env.ct["sd_t"],
                       "cu": env.ct["cu_t"], "cg": env.ct["cg_t"]}[name]
                inc(S["cDe"], a.mul(dst[:, 0:P], pt[idx % 2][:, 0:P], scale))
            for i in range(256):
                a.wait_ge(S["cP1t"].h, S["cP1t"].base + i + 1)
                if i >= 2:
                    dwait(a, S["sXTw"], i % 2, i // 2)
                inc(S["cP1e"], a.copy(trb[:, (i % 2) * 128:(i % 2 + 1) * 128],
                                      pt[i % 2][:, 0:128]))

        @blk.vector
        def _(v):
            _entry_waits(env, v)
            v.wait_ge(S["cDe"].h, S["cDe"].base + 4)
            inc(S["cDw"], v.tensor_mul(env.ct["w2_t"][:, 0:64], env.ct["cu_t"][:, 0:64],
                                       env.ct["sd_t"][:, 0:64]))
            v.wait_ge(S["cst"].h, S["cst"].n)
            inc(S["cDw"], v.tensor_scalar(env.ct["hs1"][0:1, 0:8], env.ct["hs"][0:1, 0:8],
                                          -1.0, 1.0, ALU.mult, ALU.add))


def block_B2a(env, xin, u, tag, wdc=False):
    nc = env.nc
    S = dict(env.sems)
    S.update(_dsems(env, {"sXA1": (0, 2), "sXA2": (2, 2), "sSTr": (4, 2),
                          "sSTw": (6, 2), "sPBr": (8, 3), "sDmp": (11, 3)}))
    _snap_all(env)
    from contextlib import ExitStack
    with ExitStack() as _es:
        _e = _es.enter_context
        xc = _e(nc.sbuf_tensor(f"a_xc_{tag}", [128, 2 * 512], F32))
        xb = _e(nc.sbuf_tensor(f"a_xb_{tag}", [128, 2 * 512], BF16))
        sq = _e(nc.sbuf_tensor(f"a_sq_{tag}", [128, 2 * 512], BF16))
        xc2 = _e(nc.sbuf_tensor(f"a_x2_{tag}", [128, 2 * 512], F32))
        rows = _e(nc.sbuf_tensor(f"a_rows_{tag}", [1, 8 * 512], F32))
        ping = _e(nc.sbuf_tensor(f"a_ping_{tag}", [128, 16 * 512], F32))
        pong = _e(nc.sbuf_tensor(f"a_pong_{tag}", [128, 16 * 512], F32))
        ybf = _e(nc.sbuf_tensor(f"a_ybf_{tag}", [128, 16 * 512], BF16))
        ps_s1 = _e(nc.psum_tensor(f"a_ps0_{tag}", [128, 512], F32))
        ps_s2 = _e(nc.psum_tensor(f"a_ps1_{tag}", [128, 512], F32))
        ps_m = _e(nc.psum_tensor(f"a_pm_{tag}", [128, 512], F32))
        ps_r = _e(nc.psum_tensor(f"a_pr_{tag}", [128, 512], F32))
        ph0 = _e(nc.psum_tensor(f"a_ph0_{tag}", [128, 512], F32))
        ph1 = _e(nc.psum_tensor(f"a_ph1_{tag}", [128, 512], F32))
        if wdc:
            stdb = _e(nc.sbuf_tensor(f"a_std_{tag}", [128, 2 * 2048], BF16))
            pbr = _e(nc.sbuf_tensor(f"a_pbr_{tag}", [128, 3 * 512], BF16))
            pbe = _e(nc.sbuf_tensor(f"a_pbe_{tag}", [128, 3 * 512], BF16))
            pb0 = _e(nc.psum_tensor(f"a_pb0_{tag}", [128, 512], F32))
            pb1 = _e(nc.psum_tensor(f"a_pb1_{tag}", [128, 512], F32))
            pbp = [pb0, pb1]
        blk = _e(nc.Block())
        ph = [ph0, ph1]
        col0 = u * SUP
        RM, RE, RM2, RV, RR = 0, 1, 2, 3, 4

        @blk.gpsimd
        def _(g):
            _entry_waits(env, g)
            cast_j = [0]

            def maybe_cast():
                j = cast_j[0]
                if wdc and j < 8:
                    inc(S["sCd"], g.dma_start(
                        out=env.SNdown.ap()[j * 256:(j + 1) * 256, :],
                        in_=env.io["W_down"][j * 256:(j + 1) * 256, :]), 16)
                    cast_j[0] += 1

            for tc in range(2):
                for c in range(16):
                    i = tc * 16 + c
                    if i % 8 == 0:
                        maybe_cast()
                    if i >= 2:
                        g.wait_ge(S["cXA1"].h, S["cXA1"].base + (i - 1))
                        g.wait_ge(S["cSq"].h, S["cSq"].base + (i - 1))
                    dinc(S["sXA1"], i % 2, g.dma_start(
                        out=xc[:, (i % 2) * 512:(i % 2 + 1) * 512],
                        in_=xin.ap()[c * 128:(c + 1) * 128, col0 + tc * 512:col0 + (tc + 1) * 512]))
                for c in range(16):
                    i = tc * 16 + c
                    if i % 8 == 0:
                        maybe_cast()
                    if i >= 2:
                        g.wait_ge(S["cXA2"].h, S["cXA2"].base + (i - 1))
                    dinc(S["sXA2"], i % 2, g.dma_start(
                        out=xc2[:, (i % 2) * 512:(i % 2 + 1) * 512],
                        in_=xin.ap()[c * 128:(c + 1) * 128, col0 + tc * 512:col0 + (tc + 1) * 512]))
            if wdc:
                while cast_j[0] < 8:
                    maybe_cast()
                dmp = env.PBdump.ap()
                for i in range(256):
                    og, p0 = divmod(i, 64)
                    g.wait_ge(S["cPBe"].h, S["cPBe"].base + i + 1)
                    dinc(S["sDmp"], i % 3, g.dma_start(
                        out=dmp[og, p0], in_=pbe[:, (i % 3) * 512:(i % 3 + 1) * 512]))

        if wdc:
            @blk.sync
            def _(s):
                _entry_waits(env, s)
                for c in range(64):
                    if c == 0:
                        s.wait_ge(S["sCd"].h, S["sCd"].base + 16 * 8)
                    if c >= 2:
                        dwait(s, S["sSTw"], c % 2, c // 2)
                    dinc(S["sSTr"], c % 2, s.dma_start(
                        out=stdb[:, (c % 2) * 2048:(c % 2 + 1) * 2048],
                        in_=env.SNdown.ap()[0:2048, c * 128:(c + 1) * 128],
                        transpose=True))
                    dwait(s, S["sSTr"], c % 2, c // 2 + 1)
                    dinc(S["sSTw"], c % 2, s.dma_start(
                        out=env.STdown.ap()[c * 128:(c + 1) * 128, :],
                        in_=stdb[:, (c % 2) * 2048:(c % 2 + 1) * 2048]))
                st4 = env.STdown.ap().rearrange("(c e q) o -> e c q o", c=64, e=2, q=64)
                for i in range(256):
                    og, p0 = divmod(i, 64)
                    if i == 0:
                        dwait_first(s, S["sSTw"], 64)
                    if i >= 3:
                        s.wait_ge(S["cPBm"].h, S["cPBm"].base + (i - 2))
                    dinc(S["sPBr"], i % 3, s.dma_start(
                        out=pbr[:, (i % 3) * 512:(i % 3 + 1) * 512],
                        in_=st4[:, :, p0, og * 512:(og + 1) * 512]))

        @blk.vector
        def _(v):
            _entry_waits(env, v)
            for tc in range(2):
                for c in range(16):
                    i = tc * 16 + c
                    dwait(v, S["sXA1"], i % 2, i // 2 + 1)
                    if i >= 2:
                        v.wait_ge(S["cTS"].h, S["cTS"].base + (i - 1))
                    inc(S["cXA1"], v.tensor_copy(xb[:, (i % 2) * 512:(i % 2 + 1) * 512],
                                                 xc[:, (i % 2) * 512:(i % 2 + 1) * 512]))
                v.wait_ge(S["cRowA"].h, S["cRowA"].base + tc * 4 + 3)
                inc(S["cRowV"], v.tensor_sub(rows[0:1, RV * 512:RV * 512 + 512],
                                             rows[0:1, RE * 512:RE * 512 + 512],
                                             rows[0:1, RM2 * 512:RM2 * 512 + 512]))
                for c in range(16):
                    i = tc * 16 + c
                    dwait(v, S["sXA2"], i % 2, i // 2 + 1)
                    v.wait_ge(S["cTB"].h, S["cTB"].base + 2 * (tc + 1))
                    inc(S["cXA2"], v.tensor_sub(pong[:, c * 512:(c + 1) * 512],
                                                xc2[:, (i % 2) * 512:(i % 2 + 1) * 512],
                                                ps_m[:, :]))
                    inc(S["cAp"], v.scalar_tensor_tensor(
                        ping[:, c * 512:(c + 1) * 512],
                        pong[:, c * 512:(c + 1) * 512],
                        env.ct["su_t"][:, c:c + 1],
                        ps_r[:, :], ALU.mult, ALU.mult))
                bufs = [ping, pong]
                for si, h in enumerate([1, 2, 4, 8]):
                    src = bufs[si % 2]
                    dst = bufs[(si + 1) % 2]
                    if si == 3 and tc == 1:
                        v.wait_ge(S["cT128"].h, S["cT128"].base + 16)
                    for cpair in range(8):
                        blkb = (cpair // h) * (2 * h)
                        ci = blkb + (cpair % h)
                        cj = ci + h
                        for c_out, op, aa, bb in ((ci, ALU.add, ci, cj), (cj, ALU.subtract, ci, cj)):
                            if si < 3:
                                inc(S["cH16"], v.tensor_tensor(
                                    dst[:, c_out * 512:(c_out + 1) * 512],
                                    src[:, aa * 512:(aa + 1) * 512],
                                    src[:, bb * 512:(bb + 1) * 512], op))
                            else:
                                inc(S["cYb"], v.tensor_tensor(
                                    ybf[:, c_out * 512:(c_out + 1) * 512],
                                    src[:, aa * 512:(aa + 1) * 512],
                                    src[:, bb * 512:(bb + 1) * 512], op))
            if wdc:
                for i in range(256):
                    v.wait_ge(S["cPBm"].h, S["cPBm"].base + i + 1)
                    if i >= 3:
                        dwait(v, S["sDmp"], i % 3, i // 3)
                    inc(S["cPBe"], v.tensor_copy(pbe[:, (i % 3) * 512:(i % 3 + 1) * 512],
                                                 pbp[i % 2][:, :]))

        @blk.scalar
        def _(a):
            _entry_waits(env, a)
            for tc in range(2):
                for c in range(16):
                    i = tc * 16 + c
                    dwait(a, S["sXA1"], i % 2, i // 2 + 1)
                    if i >= 2:
                        a.wait_ge(S["cTS"].h, S["cTS"].base + (i - 1))
                    inc(S["cSq"], a.square(sq[:, (i % 2) * 512:(i % 2 + 1) * 512],
                                           xc[:, (i % 2) * 512:(i % 2 + 1) * 512]))
                a.wait_ge(S["cTS"].h, S["cTS"].base + 16 * (tc + 1))
                inc(S["cRowA"], a.mul(rows[0:1, RM * 512:RM * 512 + 512], ps_s1[0:1, :], 1.0 / 2048.0))
                inc(S["cRowA"], a.mul(rows[0:1, RE * 512:RE * 512 + 512], ps_s2[0:1, :], 1.0 / 2048.0))
                inc(S["cRowA"], a.square(rows[0:1, RM2 * 512:RM2 * 512 + 512],
                                         rows[0:1, RM * 512:RM * 512 + 512]))
                a.wait_ge(S["cRowV"].h, S["cRowV"].base + tc + 1)
                inc(S["cRowA"], a.activation(rows[0:1, 5 * 512:5 * 512 + 512],
                                             rows[0:1, RV * 512:RV * 512 + 512],
                                             AF.Ln, bias=EPS))
                inc(S["cRowR"], a.activation(rows[0:1, RR * 512:RR * 512 + 512],
                                             rows[0:1, 5 * 512:5 * 512 + 512],
                                             AF.Exp, scale=-0.5))
                for c in range(16):
                    a.wait_ge(S["cT128"].h, S["cT128"].base + tc * 16 + c + 1)
                    inc(S["cAe"], a.copy(env.APP[:, c * 1024 + tc * 512: c * 1024 + (tc + 1) * 512],
                                         ph[c % 2][:, :]))

        @blk.tensor
        def _(t):
            _entry_waits(env, t)
            for tc in range(2):
                if tc == 1:
                    t.wait_ge(S["cRowA"].h, S["cRowA"].base + 2)
                for c in range(16):
                    i = tc * 16 + c
                    t.wait_ge(S["cXA1"].h, S["cXA1"].base + i + 1)
                    t.wait_ge(S["cSq"].h, S["cSq"].base + i + 1)
                    t.matmul(ps_s1[0:1, :], env.ct["cOnesColB"][:, :],
                             xb[:, (i % 2) * 512:(i % 2 + 1) * 512],
                             start=(c == 0), stop=(c == 15))
                    inc(S["cTS"], t.matmul(ps_s2[0:1, :], env.ct["cOnesColB"][:, :],
                                           sq[:, (i % 2) * 512:(i % 2 + 1) * 512],
                                           start=(c == 0), stop=(c == 15)))
                if tc == 1:
                    t.wait_ge(S["cAp"].h, S["cAp"].base + 16)
                t.wait_ge(S["cRowA"].h, S["cRowA"].base + tc * 4 + 1)
                inc(S["cTB"], t.matmul(ps_m[:, :], env.ct["cOnesRowF"][:, :],
                                       rows[0:1, RM * 512:RM * 512 + 512], start=True, stop=True))
                t.wait_ge(S["cRowR"].h, S["cRowR"].base + tc + 1)
                inc(S["cTB"], t.matmul(ps_r[:, :], env.ct["cOnesRowF"][:, :],
                                       rows[0:1, RR * 512:RR * 512 + 512], start=True, stop=True))
                for c in range(16):
                    need = 2 * c + 1 if c < 8 else 2 * (c - 8) + 2
                    t.wait_ge(S["cYb"].h, S["cYb"].base + tc * 16 + need)
                    if c >= 2:
                        t.wait_ge(S["cAe"].h, S["cAe"].base + tc * 16 + (c - 1))
                    inc(S["cT128"], t.matmul(ph[c % 2][:, :], env.ct["cHb"][:, :],
                                             ybf[:, c * 512:(c + 1) * 512],
                                             start=True, stop=True))
            if wdc:
                for i in range(256):
                    dwait(t, S["sPBr"], i % 3, i // 3 + 1)
                    if i >= 2:
                        t.wait_ge(S["cPBe"].h, S["cPBe"].base + (i - 1))
                    inc(S["cPBm"], t.matmul(pbp[i % 2][:, :], env.ct["cBD"][:, :],
                                            pbr[:, (i % 3) * 512:(i % 3 + 1) * 512],
                                            start=True, stop=True))


def block_B2b(env, xin, xout, u, tag, wdc=False):
    nc = env.nc
    S = dict(env.sems)
    S.update(_dsems(env, {"sWU": (0, 3), "sWG": (3, 3), "sSE": (6, 3),
                          "sXD": (9, 4), "sXW": (13, 4),
                          "sRbr": (17, 3), "sSef": (20, 3),
                          "sTU": (23, 3), "sTG": (26, 3)}))
    _snap_all(env)
    GS = [3] * 21 + [1]
    GBASE = [3 * i for i in range(22)]
    from contextlib import ExitStack
    with ExitStack() as _es:
        _e = _es.enter_context
        wub = _e(nc.sbuf_tensor(f"b_wu_{tag}", [128, 2 * 2048], BF16))
        wgb = _e(nc.sbuf_tensor(f"b_wg_{tag}", [128, 2 * 2048], BF16))
        seb = _e(nc.sbuf_tensor(f"b_se_{tag}", [128, 2 * 2048], BF16))
        zpb = _e(nc.sbuf_tensor(f"b_zp_{tag}", [128, 2 * 512], BF16))
        silb = _e(nc.sbuf_tensor(f"b_sil_{tag}", [128, 2 * 512], BF16))
        xcb = _e(nc.sbuf_tensor(f"b_xc_{tag}", [128, 2 * 512], F32))
        xob = _e(nc.sbuf_tensor(f"b_xo_{tag}", [128, 2 * 512], F32))
        P = [_e(nc.psum_tensor(f"b_p{j}_{tag}", [128, 512], F32)) for j in range(8)]
        if wdc:
            rbb = _e(nc.sbuf_tensor(f"b_rbb_{tag}", [128, 3 * 512], BF16))
        blk = _e(nc.Block())
        col0 = u * SUP

        @blk.sync
        def _(s):
            _entry_waits(env, s)
            dmp4 = env.PBdump.ap().rearrange("og p0 (e c) o -> og e p0 c o", e=2)

            def reblock_read(i):
                og, c = divmod(i, 64)
                if i >= 3:
                    dwait(s, S["sSef"], i % 3, i // 3)
                dinc(S["sRbr"], i % 3, s.dma_start(
                    out=rbb[:, (i % 3) * 512:(i % 3 + 1) * 512],
                    in_=dmp4[og, :, :, c, :]))

            if wdc:
                for i in range(256):
                    g_, k = divmod(i, 16)
                    reblock_read(i)
                    if k == 0:
                        s.wait_ge(S["sCu"].h, S["sCu"].base + 16 * (g_ + 1))
                    if i >= 3:
                        s.wait_ge(S["cWU"].h, S["cWU"].base + (i - 2))
                        dwait(s, S["sTU"], i % 3, i // 3)
                    dinc(S["sWU"], i % 3, s.dma_start(out=wub[:, (i % 3) * 512:(i % 3 + 1) * 512],
                                                      in_=env.SNup.ap()[g_ * 512:(g_ + 1) * 512,
                                                                        k * 128:(k + 1) * 128],
                                                      transpose=True))
            else:
                for b in range(64):
                    g_, kb = divmod(b, 4)
                    if b >= 2:
                        s.wait_ge(S["cWU"].h, S["cWU"].base + 4 * (b - 1))
                    dinc(S["sWU"], b % 2, s.dma_start(
                        out=wub[:, (b % 2) * 2048:(b % 2 + 1) * 2048].rearrange("p (kk f) -> p kk f", kk=4),
                        in_=env.STup.ap()[kb * 512:(kb + 1) * 512,
                                          g_ * 512:(g_ + 1) * 512].rearrange("(kk p) f -> p kk f", p=128)))
            if wdc:
                for g3 in range(22):
                    for k in range(16):
                        i = g3 * 16 + k
                        if k == 0:
                            nchunk = -(-(GBASE[g3] + GS[g3]) * 128 // 512)
                            s.wait_ge(S["sCg"].h, S["sCg"].base + 16 * nchunk)
                        if i >= 3:
                            s.wait_ge(S["cWG"].h, S["cWG"].base + (i - 2))
                            dwait(s, S["sTG"], i % 3, i // 3)
                        a0 = GBASE[g3]
                        na = GS[g3]
                        dinc(S["sWG"], i % 3, s.dma_start(out=wgb[:, (i % 3) * 512:(i % 3) * 512 + na * 128],
                                                          in_=env.SNgate.ap()[a0 * 128:a0 * 128 + na * 128,
                                                                              k * 128:(k + 1) * 128],
                                                          transpose=True))
            else:
                for b in range(88):
                    g3, kb = divmod(b, 4)
                    a0 = GBASE[g3]
                    na = GS[g3]
                    if b >= 2:
                        s.wait_ge(S["cWG"].h, S["cWG"].base + 4 * (b - 1))
                    dinc(S["sWG"], b % 2, s.dma_start(
                        out=wgb[:, (b % 2) * 2048:(b % 2) * 2048 + 4 * na * 128].rearrange(
                            "p (kk f) -> p kk f", kk=4),
                        in_=env.STgate.ap()[kb * 512:(kb + 1) * 512,
                                            a0 * 128:a0 * 128 + na * 128].rearrange("(kk p) f -> p kk f", p=128)))
            sef = env.SEff.ap()
            for b in range(64):
                og, ab = divmod(b, 16)
                if wdc:
                    dwait_first(s, S["sSef"], 4 * b + 4)
                if b >= 2:
                    s.wait_ge(S["cSE"].h, S["cSE"].base + 4 * (b - 1))
                dinc(S["sSE"], b % 2, s.dma_start(
                    out=seb[:, (b % 2) * 2048:(b % 2 + 1) * 2048].rearrange("p (a o) -> p a o", a=4),
                    in_=sef[og, 4 * ab:4 * ab + 4].rearrange("a p o -> p a o")))

        @blk.gpsimd
        def _(g):
            _entry_waits(env, g)
            if wdc:
                for j in range(16):
                    inc(S["sCu"], g.dma_start(
                        out=env.SNup.ap()[j * 512:(j + 1) * 512, :],
                        in_=env.io["W_up"][j * 512:(j + 1) * 512, :]), 16)
                for j in range(16):
                    inc(S["sCg"], g.dma_start(
                        out=env.SNgate.ap()[j * 512:(j + 1) * 512, :],
                        in_=env.io["W_gate"][j * 512:(j + 1) * 512, :]), 16)
                sef = env.SEff.ap()
                for i in range(256):
                    og, c = divmod(i, 64)
                    dwait(g, S["sRbr"], i % 3, i // 3 + 1)
                    dinc(S["sSef"], i % 3, g.dma_start(
                        out=sef[og, c], in_=rbb[:, (i % 3) * 512:(i % 3 + 1) * 512]))

            def rd(i):
                og, r8 = divmod(i, 8)
                tc, oci = divmod(r8, 4)
                oc = og * 4 + oci
                if i >= 2:
                    g.wait_ge(S["cEvD"].h, S["cEvD"].base + (i - 1))
                dinc(S["sXD"], i % 2, g.dma_start(
                    out=xcb[:, (i % 2) * 512:(i % 2 + 1) * 512],
                    in_=xin.ap()[oc * 128:(oc + 1) * 128, col0 + tc * 512:col0 + (tc + 1) * 512]))

            def wr(i):
                og, r8 = divmod(i, 8)
                tc, oci = divmod(r8, 4)
                oc = og * 4 + oci
                g.wait_ge(S["cEvD"].h, S["cEvD"].base + i + 1)
                dinc(S["sXW"], i % 2, g.dma_start(
                    out=xout.ap()[oc * 128:(oc + 1) * 128, col0 + tc * 512:col0 + (tc + 1) * 512],
                    in_=xob[:, (i % 2) * 512:(i % 2 + 1) * 512]))

            for i in range(2):
                rd(i)
            for i in range(32):
                wr(i)
                if i + 2 < 32:
                    rd(i + 2)

        @blk.tensor
        def _(t):
            _entry_waits(env, t)
            for g_ in range(16):
                if g_ >= 1:
                    t.wait_ge(S["cEvU"].h, S["cEvU"].base + 8 * g_)
                for k in range(16):
                    i = g_ * 16 + k
                    if wdc:
                        dwait(t, S["sWU"], i % 3, i // 3 + 1)
                        wcol = (i % 3) * 512
                    else:
                        if k % 4 == 0:
                            dwait(t, S["sWU"], (i // 4) % 2, i // 8 + 1)
                        wcol = ((i // 4) % 2) * 2048 + (k % 4) * 512
                    for tc in range(2):
                        for ai in range(4):
                            ins = t.matmul(P[tc * 4 + ai][:, :],
                                           wub[:, wcol + ai * 128:wcol + (ai + 1) * 128],
                                           env.APP[:, k * 1024 + tc * 512:k * 1024 + (tc + 1) * 512],
                                           start=(k == 0), stop=(k == 15))
                    inc(S["cWU"], ins)
            t.wait_ge(S["cEvU"].h, S["cEvU"].base + 128)
            for g3 in range(22):
                na = GS[g3]
                a0 = GBASE[g3]
                if g3 >= 1:
                    t.wait_ge(S["cSil"].h, S["cSil"].base + 2 * sum(GS[:g3]))
                for k in range(16):
                    i = g3 * 16 + k
                    if wdc:
                        dwait(t, S["sWG"], i % 3, i // 3 + 1)
                        wcol = (i % 3) * 512
                    else:
                        if k % 4 == 0:
                            dwait(t, S["sWG"], (i // 4) % 2, i // 8 + 1)
                        wcol = ((i // 4) % 2) * 2048 + (k % 4) * na * 128
                    for tc in range(2):
                        for ai in range(na):
                            ins = t.matmul(P[tc * 3 + ai][:, :],
                                           wgb[:, wcol + ai * 128:wcol + (ai + 1) * 128],
                                           env.APP[:, k * 1024 + tc * 512:k * 1024 + (tc + 1) * 512],
                                           start=(k == 0), stop=(k == 15))
                    inc(S["cWG"], ins)
                for tc in range(2):
                    for ai in range(na):
                        j = 2 * sum(GS[:g3]) + tc * na + ai
                        t.wait_ge(S["cZp"].h, S["cZp"].base + j + 1)
                        if j >= 2:
                            t.wait_ge(S["cZh"].h, S["cZh"].base + (j - 1))
                        inc(S["cTH"], t.matmul(P[6 + j % 2][:, :], env.ct["cHb"][:, :],
                                               zpb[:, (j % 2) * 512:(j % 2 + 1) * 512],
                                               start=True, stop=True))
            t.wait_ge(S["cZh"].h, S["cZh"].base + 128)
            t.wait_ge(S["cSil"].h, S["cSil"].base + 128)
            for og in range(4):
                if og >= 1:
                    t.wait_ge(S["cEvD"].h, S["cEvD"].base + 8 * og)
                for a_ in range(64):
                    i = og * 64 + a_
                    if a_ % 4 == 0:
                        dwait(t, S["sSE"], (i // 4) % 2, i // 8 + 1)
                    scol = ((i // 4) % 2) * 2048 + (a_ % 4) * 512
                    for tc in range(2):
                        for oci in range(4):
                            ins = t.matmul(P[tc * 4 + oci][:, :],
                                           seb[:, scol + oci * 128:scol + (oci + 1) * 128],
                                           env.UZ[:, a_ * 1024 + tc * 512:a_ * 1024 + (tc + 1) * 512],
                                           start=(a_ == 0), stop=(a_ == 63))
                    inc(S["cSE"], ins)

        @blk.scalar
        def _(a):
            _entry_waits(env, a)
            for g_ in range(16):
                if wdc:
                    for k in range(16):
                        i = g_ * 16 + k
                        a.wait_ge(S["cWU"].h, S["cWU"].base + i + 1)
                        dinc(S["sTU"], i % 3, a.dma_start(
                            out=env.STup.ap()[k * 128:(k + 1) * 128, g_ * 512:(g_ + 1) * 512],
                            in_=wub[:, (i % 3) * 512:(i % 3 + 1) * 512]))
                a.wait_ge(S["cWU"].h, S["cWU"].base + 16 * (g_ + 1))
                for tc in range(2):
                    for ai in range(4):
                        au = 4 * g_ + ai
                        inc(S["cEvU"], a.copy(
                            env.UZ[:, au * 1024 + tc * 512:au * 1024 + (tc + 1) * 512],
                            P[tc * 4 + ai][:, :]))
            for g3 in range(22):
                na = GS[g3]
                a0 = GBASE[g3]
                if wdc:
                    for k in range(16):
                        i = g3 * 16 + k
                        a.wait_ge(S["cWG"].h, S["cWG"].base + i + 1)
                        dinc(S["sTG"], i % 3, a.dma_start(
                            out=env.STgate.ap()[k * 128:(k + 1) * 128, a0 * 128:a0 * 128 + na * 128],
                            in_=wgb[:, (i % 3) * 512:(i % 3) * 512 + na * 128]))
                a.wait_ge(S["cWG"].h, S["cWG"].base + 16 * (g3 + 1))
                for tc in range(2):
                    for ai in range(na):
                        j = 2 * sum(GS[:g3]) + tc * na + ai
                        a_ = a0 + ai
                        if j >= 2:
                            a.wait_ge(S["cZp"].h, S["cZp"].base + (j - 1))
                        inc(S["cSil"], a.activation(
                            silb[:, (j % 2) * 512:(j % 2 + 1) * 512],
                            P[tc * 3 + ai][:, :], AF.Silu,
                            scale=env.ct["cg_t"][:, a_:a_ + 1]))
                for tc in range(2):
                    for ai in range(na):
                        j = 2 * sum(GS[:g3]) + tc * na + ai
                        a_ = a0 + ai
                        a.wait_ge(S["cTH"].h, S["cTH"].base + j + 1)
                        inc(S["cZh"], a.copy(
                            env.UZ[:, a_ * 1024 + tc * 512:a_ * 1024 + (tc + 1) * 512],
                            P[6 + j % 2][:, :]))

        @blk.vector
        def _(v):
            _entry_waits(env, v)
            for g3 in range(22):
                na = GS[g3]
                a0 = GBASE[g3]
                for tc in range(2):
                    for ai in range(na):
                        j = 2 * sum(GS[:g3]) + tc * na + ai
                        a_ = a0 + ai
                        v.wait_ge(S["cSil"].h, S["cSil"].base + j + 1)
                        if j >= 2:
                            v.wait_ge(S["cTH"].h, S["cTH"].base + (j - 1))
                        inc(S["cZp"], v.scalar_tensor_tensor(
                            zpb[:, (j % 2) * 512:(j % 2 + 1) * 512],
                            env.UZ[:, a_ * 1024 + tc * 512:a_ * 1024 + (tc + 1) * 512],
                            env.ct["w2_t"][:, a_:a_ + 1],
                            silb[:, (j % 2) * 512:(j % 2 + 1) * 512],
                            ALU.mult, ALU.mult))
            for og in range(4):
                v.wait_ge(S["cSE"].h, S["cSE"].base + 64 * (og + 1))
                for tc in range(2):
                    for oci in range(4):
                        i = og * 8 + tc * 4 + oci
                        oc = og * 4 + oci
                        dwait(v, S["sXD"], i % 2, i // 2 + 1)
                        if i >= 2:
                            dwait(v, S["sXW"], i % 2, i // 2)
                        inc(S["cEvD"], v.scalar_tensor_tensor(
                            xob[:, (i % 2) * 512:(i % 2 + 1) * 512],
                            P[tc * 4 + oci][:, :],
                            env.ct["co2_t"][:, oc:oc + 1],
                            xcb[:, (i % 2) * 512:(i % 2 + 1) * 512],
                            ALU.mult, ALU.add))


def block_B3a1(env, xin):
    nc = env.nc
    S = dict(env.sems)
    S.update(_dsems(env, {"sXA1": (0, 2), "sXA2": (2, 2)}))
    _snap_all(env)
    with (
        nc.sbuf_tensor("c1_xc", [128, 2 * 512], F32) as xc,
        nc.sbuf_tensor("c1_xb", [128, 2 * 512], BF16) as xb,
        nc.sbuf_tensor("c1_sq", [128, 2 * 512], BF16) as sq,
        nc.sbuf_tensor("c1_x2", [128, 2 * 512], F32) as xc2,
        nc.sbuf_tensor("c1_ap", [128, 2 * 512], F32) as apb,
        nc.sbuf_tensor("c1_rows", [1, 8 * 512], F32) as rows,
        nc.psum_tensor("c1_ps0", [128, 512], F32) as ps_s1,
        nc.psum_tensor("c1_ps1", [128, 512], F32) as ps_s2,
        nc.psum_tensor("c1_pm", [128, 512], F32) as ps_m,
        nc.psum_tensor("c1_pr", [128, 512], F32) as ps_r,
        nc.Block() as blk,
    ):
        RM, RE, RM2, RV, RR, RL = 0, 1, 2, 3, 4, 5
        NT = 4

        @blk.gpsimd
        def _(g):
            _entry_waits(env, g)
            for tc in range(NT):
                for c in range(16):
                    i = tc * 16 + c
                    if i >= 2:
                        g.wait_ge(S["cXA1"].h, S["cXA1"].base + (i - 1))
                        g.wait_ge(S["cSq"].h, S["cSq"].base + (i - 1))
                    dinc(S["sXA1"], i % 2, g.dma_start(
                        out=xc[:, (i % 2) * 512:(i % 2 + 1) * 512],
                        in_=xin.ap()[c * 128:(c + 1) * 128, tc * 512:(tc + 1) * 512]))
                for c in range(16):
                    i = tc * 16 + c
                    if i >= 2:
                        g.wait_ge(S["cXA2"].h, S["cXA2"].base + (i - 1))
                    dinc(S["sXA2"], i % 2, g.dma_start(
                        out=xc2[:, (i % 2) * 512:(i % 2 + 1) * 512],
                        in_=xin.ap()[c * 128:(c + 1) * 128, tc * 512:(tc + 1) * 512]))

        @blk.vector
        def _(v):
            _entry_waits(env, v)
            for tc in range(NT):
                for c in range(16):
                    i = tc * 16 + c
                    dwait(v, S["sXA1"], i % 2, i // 2 + 1)
                    if i >= 2:
                        v.wait_ge(S["cTS"].h, S["cTS"].base + (i - 1))
                    inc(S["cXA1"], v.tensor_copy(xb[:, (i % 2) * 512:(i % 2 + 1) * 512],
                                                 xc[:, (i % 2) * 512:(i % 2 + 1) * 512]))
                v.wait_ge(S["cRowA"].h, S["cRowA"].base + tc * 4 + 3)
                inc(S["cRowV"], v.tensor_sub(rows[0:1, RV * 512:RV * 512 + 512],
                                             rows[0:1, RE * 512:RE * 512 + 512],
                                             rows[0:1, RM2 * 512:RM2 * 512 + 512]))
                for c in range(16):
                    i = tc * 16 + c
                    dwait(v, S["sXA2"], i % 2, i // 2 + 1)
                    v.wait_ge(S["cTB"].h, S["cTB"].base + 2 * (tc + 1))
                    inc(S["cXA2"], v.tensor_sub(apb[:, (i % 2) * 512:(i % 2 + 1) * 512],
                                                xc2[:, (i % 2) * 512:(i % 2 + 1) * 512],
                                                ps_m[:, :]))
                    inc(S["cAp"], v.tensor_mul(
                        env.Y2[:, c * 2048 + tc * 512:c * 2048 + (tc + 1) * 512],
                        apb[:, (i % 2) * 512:(i % 2 + 1) * 512], ps_r[:, :]))

        @blk.scalar
        def _(a):
            _entry_waits(env, a)
            for tc in range(NT):
                for c in range(16):
                    i = tc * 16 + c
                    dwait(a, S["sXA1"], i % 2, i // 2 + 1)
                    if i >= 2:
                        a.wait_ge(S["cTS"].h, S["cTS"].base + (i - 1))
                    inc(S["cSq"], a.square(sq[:, (i % 2) * 512:(i % 2 + 1) * 512],
                                           xc[:, (i % 2) * 512:(i % 2 + 1) * 512]))
                a.wait_ge(S["cTS"].h, S["cTS"].base + 16 * (tc + 1))
                inc(S["cRowA"], a.mul(rows[0:1, RM * 512:RM * 512 + 512], ps_s1[0:1, :], 1.0 / 2048.0))
                inc(S["cRowA"], a.mul(rows[0:1, RE * 512:RE * 512 + 512], ps_s2[0:1, :], 1.0 / 2048.0))
                inc(S["cRowA"], a.square(rows[0:1, RM2 * 512:RM2 * 512 + 512],
                                         rows[0:1, RM * 512:RM * 512 + 512]))
                a.wait_ge(S["cRowV"].h, S["cRowV"].base + tc + 1)
                inc(S["cRowA"], a.activation(rows[0:1, RL * 512:RL * 512 + 512],
                                             rows[0:1, RV * 512:RV * 512 + 512],
                                             AF.Ln, bias=EPS))
                inc(S["cRowR"], a.activation(rows[0:1, RR * 512:RR * 512 + 512],
                                             rows[0:1, RL * 512:RL * 512 + 512],
                                             AF.Exp, scale=-0.5))

        @blk.tensor
        def _(t):
            _entry_waits(env, t)
            for tc in range(NT):
                if tc >= 1:
                    t.wait_ge(S["cRowA"].h, S["cRowA"].base + (tc - 1) * 4 + 2)
                for c in range(16):
                    i = tc * 16 + c
                    t.wait_ge(S["cXA1"].h, S["cXA1"].base + i + 1)
                    t.wait_ge(S["cSq"].h, S["cSq"].base + i + 1)
                    t.matmul(ps_s1[0:1, :], env.ct["cOnesColB"][:, :],
                             xb[:, (i % 2) * 512:(i % 2 + 1) * 512],
                             start=(c == 0), stop=(c == 15))
                    inc(S["cTS"], t.matmul(ps_s2[0:1, :], env.ct["cOnesColB"][:, :],
                                           sq[:, (i % 2) * 512:(i % 2 + 1) * 512],
                                           start=(c == 0), stop=(c == 15)))
                if tc >= 1:
                    t.wait_ge(S["cAp"].h, S["cAp"].base + 16 * tc)
                t.wait_ge(S["cRowA"].h, S["cRowA"].base + tc * 4 + 1)
                inc(S["cTB"], t.matmul(ps_m[:, :], env.ct["cOnesRowF"][:, :],
                                       rows[0:1, RM * 512:RM * 512 + 512], start=True, stop=True))
                t.wait_ge(S["cRowR"].h, S["cRowR"].base + tc + 1)
                inc(S["cTB"], t.matmul(ps_r[:, :], env.ct["cOnesRowF"][:, :],
                                       rows[0:1, RR * 512:RR * 512 + 512], start=True, stop=True))


def block_B3a2(env):
    nc = env.nc
    S = dict(env.sems)
    S.update(_dsems(env, {"sWQ": (0, 2), "sWV": (2, 2)}))
    _snap_all(env)
    QU = [("q", h, u4) for h in range(8) for u4 in range(4)] + \
         [("k", kv, u4) for kv in range(2) for u4 in range(4)]
    LOADS = [("q", h) for h in range(8)] + [("k", kv) for kv in range(2)]

    def load_of_unit(ui):
        kind, hh, u4 = QU[ui]
        return hh if kind == "q" else 8 + hh

    with (
        nc.sbuf_tensor("c2_wq", [128, 2 * 4096], BF16) as wqb,
        nc.sbuf_tensor("c2_wv", [128, 2 * 2048], BF16) as wvb,
        nc.sbuf_tensor("c2_cos", [128, 2048], BF16) as cosT,
        nc.sbuf_tensor("c2_sin", [128, 2048], BF16) as sinT,
        nc.sbuf_tensor("c2_ro", [128, 4 * 512], F32) as rob,
        nc.psum_tensor("c2_pa0", [128, 512], F32) as pa0,
        nc.psum_tensor("c2_pb0", [128, 512], F32) as pb0,
        nc.psum_tensor("c2_pa1", [128, 512], F32) as pa1,
        nc.psum_tensor("c2_pb1", [128, 512], F32) as pb1,
        nc.psum_tensor("c2_pv0", [128, 512], F32) as pv0,
        nc.psum_tensor("c2_pv1", [128, 512], F32) as pv1,
        nc.Block() as blk,
    ):
        PA = [pa0, pa1]
        PB = [pb0, pb1]
        PV = [pv0, pv1]

        @blk.gpsimd
        def _(g):
            _entry_waits(env, g)
            inc(S["cst"], g.dma_start(out=cosT[:, :], in_=env.io["cCos"][:, :]), 16)
            inc(S["cst"], g.dma_start(out=sinT[:, :], in_=env.io["cSin"][:, :]), 16)
            for lh, (kind, hh) in enumerate(LOADS):
                if lh >= 2:
                    g.wait_ge(S["cQK"].h, S["cQK"].base + 4 * (lh - 1))
                src = env.io["Wq"] if kind == "q" else env.io["Wk"]
                dinc(S["sWQ"], lh % 2, g.dma_start(
                    out=wqb[:, (lh % 2) * 4096:(lh % 2 + 1) * 4096].rearrange("p (k d) -> p k d", k=16),
                    in_=src[:, hh * 256:(hh + 1) * 256].rearrange("(k p) d -> p k d", p=128)))
            for b in range(64):
                tt, kb = divmod(b, 4)
                if b >= 2:
                    g.wait_ge(S["cVmm"].h, S["cVmm"].base + 4 * b - 4)
                dinc(S["sWV"], b % 2, g.dma_start(
                    out=wvb[:, (b % 2) * 2048:(b % 2 + 1) * 2048].rearrange("p (kk d) -> p kk d", kk=4),
                    in_=env.io["Wv"][kb * 512:(kb + 1) * 512, :].rearrange("(kk p) d -> p kk d", p=128)))

        @blk.tensor
        def _(t):
            _entry_waits(env, t)
            for ui in range(len(QU)):
                kind, hh, u4 = QU[ui]
                par = ui % 2
                lh = load_of_unit(ui)
                dwait(t, S["sWQ"], lh % 2, lh // 2 + 1)
                if ui >= 2:
                    t.wait_ge(S["cRope"].h, S["cRope"].base + 6 * (ui - 1))
                for k in range(16):
                    wcol = (lh % 2) * 4096 + k * 256
                    t.matmul(PA[par][:, :], wqb[:, wcol:wcol + 128],
                             env.Y2[:, k * 2048 + u4 * 512:k * 2048 + (u4 + 1) * 512],
                             start=(k == 0), stop=(k == 15))
                    insB = t.matmul(PB[par][:, :], wqb[:, wcol + 128:wcol + 256],
                                    env.Y2[:, k * 2048 + u4 * 512:k * 2048 + (u4 + 1) * 512],
                                    start=(k == 0), stop=(k == 15))
                inc(S["cQK"], insB)
            for tt in range(16):
                if tt >= 2:
                    t.wait_ge(S["cVe"].h, S["cVe"].base + (tt - 1))
                for k in range(16):
                    i = tt * 16 + k
                    if k % 4 == 0:
                        dwait(t, S["sWV"], (i // 4) % 2, i // 8 + 1)
                    vcol = ((i // 4) % 2) * 2048 + (k % 4) * 512
                    insV = t.matmul(PV[tt % 2][:, :],
                                    env.Y2[:, k * 2048 + tt * 128:k * 2048 + (tt + 1) * 128],
                                    wvb[:, vcol:vcol + 512],
                                    start=(k == 0), stop=(k == 15))
                    inc(S["cVmm"], insV)
                insV = None

        @blk.vector
        def _(v):
            _entry_waits(env, v)
            v.wait_ge(S["cst"].h, S["cst"].n)
            for ui in range(len(QU)):
                kind, hh, u4 = QU[ui]
                par = ui % 2
                v.wait_ge(S["cQK"].h, S["cQK"].base + ui + 1)
                dst = env.QT if kind == "q" else env.KT
                c0 = (2 * hh) * 2048 + u4 * 512
                c1 = (2 * hh + 1) * 2048 + u4 * 512
                coss = cosT[:, u4 * 512:(u4 + 1) * 512]
                sins = sinT[:, u4 * 512:(u4 + 1) * 512]
                v.tensor_mul(rob[:, 0:512], PA[par][:, :], coss)
                v.tensor_mul(rob[:, 512:1024], PB[par][:, :], sins)
                v.tensor_mul(rob[:, 1024:1536], PB[par][:, :], coss)
                v.tensor_mul(rob[:, 1536:2048], PA[par][:, :], sins)
                v.tensor_sub(dst[:, c0:c0 + 512], rob[:, 0:512], rob[:, 512:1024])
                inc(S["cRope"], v.tensor_add(dst[:, c1:c1 + 512],
                                             rob[:, 1024:1536], rob[:, 1536:2048]), 6)

        @blk.scalar
        def _(a):
            _entry_waits(env, a)
            for tt in range(16):
                a.wait_ge(S["cVmm"].h, S["cVmm"].base + 16 * (tt + 1))
                inc(S["cVe"], a.copy(env.V[:, tt * 512:(tt + 1) * 512], PV[tt % 2][:, :]))


def block_B3b(env, xin, xout):
    nc = env.nc
    S = dict(env.sems)
    S.update(_dsems(env, {"sWO": (0, 2), "sXD3": (2, 2), "sXW3": (4, 2)}))
    _snap_all(env)
    from contextlib import ExitStack
    with ExitStack() as _es:
        _e = _es.enter_context
        Eb = _e(nc.sbuf_tensor("c3_E", [128, 2 * 512], BF16))
        gEb = _e(nc.sbuf_tensor("c3_gE", [128, 2 * 512], BF16))
        albe = _e(nc.sbuf_tensor("c3_ab", [128, 2 * 512], F32))
        rows = _e(nc.sbuf_tensor("c3_rows", [1, 3 * 512], F32))
        ocb = _e(nc.sbuf_tensor("c3_oc", [128, 2 * 512], F32))
        wob = _e(nc.sbuf_tensor("c3_wo", [128, 2 * 1024], BF16))
        xcb = _e(nc.sbuf_tensor("c3_xc", [128, 2 * 512], F32))
        xob = _e(nc.sbuf_tensor("c3_xo", [128, 2 * 512], F32))
        sqq = _e(nc.sbuf_tensor("c3_sqq", [128, 2 * 512], BF16))
        qpb = _e(nc.sbuf_tensor("c3_qpb", [1, 2048], BF16))
        kpb = _e(nc.sbuf_tensor("c3_kpb", [1, 2048], BF16))
        scl = _e(nc.sbuf_tensor("c3_scl", [1, 64], F32))
        mcol = _e(nc.sbuf_tensor("c3_mcol", [128, 16], F32))
        oe0 = _e(nc.psum_tensor("c3_oe0", [128, 512], F32))
        oe1 = _e(nc.psum_tensor("c3_oe1", [128, 512], F32))
        og0 = _e(nc.psum_tensor("c3_og0", [128, 512], F32))
        og1 = _e(nc.psum_tensor("c3_og1", [128, 512], F32))
        prs = _e(nc.psum_tensor("c3_rs", [128, 512], F32))
        pgr = _e(nc.psum_tensor("c3_gr", [128, 512], F32))
        ps0 = _e(nc.psum_tensor("c3_s0", [128, 512], F32))
        ps1 = _e(nc.psum_tensor("c3_s1", [128, 512], F32))
        blk = _e(nc.Block())
        OE = [oe0, oe1]
        OG = [og0, og1]
        PS = [ps0, ps1]
        def prostats(h):
            items = []
            if h % 4 == 0:
                items += [("k2", u) for u in range(4)]
            items += [("kp", u) for u in range(4)]
            items += [("qp", u) for u in range(4)]
            items += [("q2", u) for u in range(4)]
            return items

        pro_base = [0] * 9
        for h in range(8):
            pro_base[h + 1] = pro_base[h] + len(prostats(h))
        sqcnt = []
        acc = 0
        for h in range(8):
            for (kindp, u) in prostats(h):
                if kindp in ("q2", "k2"):
                    acc += 1
                sqcnt.append(acc)
        km_after_kmax = {}
        km_after_qmax = {}
        km = 0
        for h in range(8):
            if h % 4 == 0:
                km += 1
                km_after_kmax[h] = km
            km += 1
            km_after_qmax[h] = km
            km += 2

        def nsc_of(hi):
            return sum(2 * (4 * (x % 4) + 4) for x in range(hi))

        HU = [(h, u) for h in range(8) for u in range(4)]
        NJ = {hi: 4 * HU[hi][1] + 4 for hi in range(32)}

        def av_base(hi):
            return sum(NJ[x] for x in range(hi))

        @blk.tensor
        def _(t):
            _entry_waits(env, t)
            for h in range(8):
                kv = h // 4
                if h >= 1:
                    t.wait_ge(S["cAbe"].h, S["cAbe"].base + 2 * (4 * h))
                for pi, (kindp, u) in enumerate(prostats(h)):
                    gi = pro_base[h] + pi
                    if gi >= 1:
                        t.wait_ge(S["cStE"].h, S["cStE"].base + gi)
                    if kindp == "kp":
                        ins = t.matmul(PS[0][0:1, :], env.ct["cDirs"][:, h:h + 1],
                                       env.KT[0:3, (2 * kv) * 2048 + u * 512:(2 * kv) * 2048 + (u + 1) * 512],
                                       start=True, stop=True)
                    elif kindp == "qp":
                        ins = t.matmul(PS[0][0:1, :], env.ct["cDirs"][:, h:h + 1],
                                       env.QT[0:3, (2 * h) * 2048 + u * 512:(2 * h) * 2048 + (u + 1) * 512],
                                       start=True, stop=True)
                    else:
                        t.wait_ge(S["cSqQ"].h, S["cSqQ"].base + 2 * sqcnt[gi])
                        for half in range(2):
                            ins = t.matmul(PS[0][32:33, :], env.ct["cOnesColB"][:, :],
                                           sqq[:, half * 512:(half + 1) * 512],
                                           start=(half == 0), stop=(half == 1))
                    inc(S["cStM"], ins)
                t.wait_ge(S["cBias"].h, S["cBias"].base + 2 * (h + 1))
                inc(S["cBcM"], t.matmul(PS[1][:, 0:1], env.ct["cOnesRowF"][:, :],
                                        scl[0:1, 10:11], start=True, stop=True))
                inc(S["cBcM"], t.matmul(PS[1][:, 1:2], env.ct["cOnesRowF"][:, :],
                                        scl[0:1, 11:12], start=True, stop=True))
                for u in range(4):
                    hi = h * 4 + u
                    nj = NJ[hi]
                    nsc = nsc_of(hi)
                    if hi >= 1:
                        t.wait_ge(S["cRowF"].h, S["cRowF"].base + 2 * hi)
                        t.wait_ge(S["cCmb"].h, S["cCmb"].base + 6 * hi)
                    t.wait_ge(S["cMce"].h, S["cMce"].base + 2 * (h + 1))
                    if u == 0:
                        t.wait_ge(S["cStE"].h, S["cStE"].base + pro_base[h + 1])
                    for j in range(nj):
                        last_main = (j < 4 * u)
                        if nsc >= 2:
                            t.wait_ge(S["cExpE"].h, S["cExpE"].base + nsc - 1)
                        t.matmul(PS[nsc % 2][:, :],
                                 env.KT[:, (2 * kv) * 2048 + j * 128:(2 * kv) * 2048 + (j + 1) * 128],
                                 env.QT[:, (2 * h) * 2048 + u * 512:(2 * h) * 2048 + (u + 1) * 512],
                                 start=True, stop=False)
                        ins = t.matmul(PS[nsc % 2][:, :],
                                       env.KT[:, (2 * kv + 1) * 2048 + j * 128:(2 * kv + 1) * 2048 + (j + 1) * 128],
                                       env.QT[:, (2 * h + 1) * 2048 + u * 512:(2 * h + 1) * 2048 + (u + 1) * 512],
                                       start=False, stop=last_main)
                        if not last_main:
                            ins = t.matmul(PS[nsc % 2][:, :], env.ct["cIDb"][:, :],
                                           env.ct["cMask"][:, (j - 4 * u) * 512:(j - 4 * u + 1) * 512],
                                           start=False, stop=True)
                        inc(S["cScM"], ins)
                        nsc += 1
                        if nsc >= 2:
                            t.wait_ge(S["cExpE"].h, S["cExpE"].base + nsc - 1)
                        ins = t.matmul(PS[nsc % 2][:, :],
                                       kpb[0:1, j * 128:(j + 1) * 128],
                                       qpb[0:1, u * 512:(u + 1) * 512],
                                       start=True, stop=last_main)
                        if not last_main:
                            ins = t.matmul(PS[nsc % 2][:, :], env.ct["cIDb"][:, :],
                                           env.ct["cMask"][:, (j - 4 * u) * 512:(j - 4 * u + 1) * 512],
                                           start=False, stop=True)
                        inc(S["cScM"], ins)
                        nsc += 1
                        t.wait_ge(S["cExpE"].h, S["cExpE"].base + nsc)
                        t.matmul(prs[0:1, :], env.ct["cOnesColB"][:, :],
                                 Eb[:, (j % 2) * 512:(j % 2 + 1) * 512],
                                 start=(j == 0), stop=(j == nj - 1))
                        t.matmul(pgr[0:1, :], env.ct["cOnesColB"][:, :],
                                 gEb[:, (j % 2) * 512:(j % 2 + 1) * 512],
                                 start=(j == 0), stop=(j == nj - 1))
                        for half in range(2):
                            vsl = j * 512 + kv * 256 + half * 128
                            t.matmul(OE[half][:, :], env.V[:, vsl:vsl + 128],
                                     Eb[:, (j % 2) * 512:(j % 2 + 1) * 512],
                                     start=(j == 0), stop=(j == nj - 1))
                            ins = t.matmul(OG[half][:, :], env.V[:, vsl:vsl + 128],
                                           gEb[:, (j % 2) * 512:(j % 2 + 1) * 512],
                                           start=(j == 0), stop=(j == nj - 1))
                        inc(S["cAV"], ins)
                    t.wait_ge(S["cRowF"].h, S["cRowF"].base + 2 * (hi + 1))
                    t.wait_ge(S["cExpE"].h, S["cExpE"].base + nsc)
                    inc(S["cBcA"], t.matmul(PS[0][:, :], env.ct["cOnesRowF"][:, :],
                                            rows[0:1, 0:512], start=True, stop=True))
                    inc(S["cBcA"], t.matmul(PS[1][:, :], env.ct["cOnesRowF"][:, :],
                                            rows[0:1, 512:1024], start=True, stop=True))
            t.wait_ge(S["cCmb"].h, S["cCmb"].base + 32 * 6)
            t.wait_ge(S["cAbe"].h, S["cAbe"].base + 64)
            for oc in range(16):
                for u in range(4):
                    i = oc * 4 + u
                    w = oc * 2 + u // 2
                    if i >= 2:
                        t.wait_ge(S["cOe"].h, S["cOe"].base + (i - 1))
                    for l in range(2):
                        LI = w * 2 + l
                        dwait(t, S["sWO"], LI % 2, LI // 2 + 1)
                        for k8 in range(8):
                            k = l * 8 + k8
                            insO = t.matmul(PS[i % 2][:, :],
                                            wob[:, (LI % 2) * 1024 + k8 * 128:(LI % 2) * 1024 + (k8 + 1) * 128],
                                            env.AO[:, k * 2048 + u * 512:k * 2048 + (u + 1) * 512],
                                            start=(k == 0), stop=(k == 15))
                    inc(S["cWoM"], insO)

        @blk.vector
        def _(v):
            _entry_waits(env, v)
            for h in range(8):
                kv = h // 4
                for pi, (kindp, u) in enumerate(prostats(h)):
                    gi = pro_base[h] + pi
                    v.wait_ge(S["cStM"].h, S["cStM"].base + gi + 1)
                    if kindp == "kp":
                        if pi == [p for p, (kk, _) in enumerate(prostats(h)) if kk == "kp"][0] and h >= 1:
                            v.wait_ge(S["cScM"].h, S["cScM"].base + nsc_of(4 * h))
                        inc(S["cStE"], v.tensor_copy(kpb[0:1, u * 512:(u + 1) * 512], PS[0][0:1, :]))
                    elif kindp == "qp":
                        inc(S["cStE"], v.tensor_copy(qpb[0:1, u * 512:(u + 1) * 512], PS[0][0:1, :]))
                    elif kindp == "q2":
                        inc(S["cStE"], v.reduce_max(scl[0:1, u:u + 1], PS[0][32:33, :], AX.X))
                    else:
                        inc(S["cStE"], v.reduce_max(scl[0:1, 16 + u:17 + u], PS[0][32:33, :], AX.X))
                if h % 4 == 0:
                    inc(S["cKm"], v.reduce_max(scl[0:1, 20:21], scl[0:1, 16:20], AX.X))
                inc(S["cKm"], v.reduce_max(scl[0:1, 4:5], scl[0:1, 0:4], AX.X))
                inc(S["cKm"], v.reduce_max(scl[0:1, 8:9], qpb[0:1, :], AX.X,
                                           apply_absolute_value=True))
                inc(S["cKm"], v.reduce_max(scl[0:1, 9:10], kpb[0:1, :], AX.X,
                                           apply_absolute_value=True))
                v.wait_ge(S["cSqt"].h, S["cSqt"].base + (h + 1) + (h // 4 + 1))
                inc(S["cBias"], v.tensor_scalar(scl[0:1, 10:11], scl[0:1, 5:6],
                                                scl[0:1, 6 + kv:7 + kv], -SAFE / 16.0,
                                                ALU.mult, ALU.mult))
                inc(S["cBias"], v.tensor_scalar(scl[0:1, 11:12], scl[0:1, 8:9],
                                                scl[0:1, 9:10], -SAFE,
                                                ALU.mult, ALU.mult))
                for u in range(4):
                    hi = h * 4 + u
                    nj = NJ[hi]
                    v.wait_ge(S["cAV"].h, S["cAV"].base + av_base(hi) + nj)
                    v.reciprocal(rows[0:1, 1024:1536], prs[0:1, :])
                    inc(S["cRowF"], v.tensor_single_scalar(rows[0:1, 0:512],
                                                           rows[0:1, 1024:1536],
                                                           env.ct["hs1"][0:1, h:h + 1], ALU.mult))
                    v.reciprocal(rows[0:1, 1024:1536], pgr[0:1, :])
                    inc(S["cRowF"], v.tensor_single_scalar(rows[0:1, 512:1024],
                                                           rows[0:1, 1024:1536],
                                                           env.ct["hs"][0:1, h:h + 1], ALU.mult))
                    v.wait_ge(S["cAbe"].h, S["cAbe"].base + 2 * (hi + 1))
                    for half in range(2):
                        v.tensor_mul(ocb[:, 0:512], OE[half][:, :], albe[:, 0:512])
                        v.tensor_mul(ocb[:, 512:1024], OG[half][:, :], albe[:, 512:1024])
                        inc(S["cCmb"], v.tensor_add(
                            env.AO[:, (2 * h + half) * 2048 + u * 512:(2 * h + half) * 2048 + (u + 1) * 512],
                            ocb[:, 0:512], ocb[:, 512:1024]), 3)
            for oc in range(16):
                for u in range(4):
                    i = oc * 4 + u
                    v.wait_ge(S["cWoM"].h, S["cWoM"].base + i + 1)
                    dwait(v, S["sXD3"], i % 2, i // 2 + 1)
                    if i >= 2:
                        dwait(v, S["sXW3"], i % 2, i // 2)
                    inc(S["cOe"], v.tensor_add(xob[:, (i % 2) * 512:(i % 2 + 1) * 512],
                                               PS[i % 2][:, :],
                                               xcb[:, (i % 2) * 512:(i % 2 + 1) * 512]))

        @blk.scalar
        def _(a):
            _entry_waits(env, a)
            for h in range(8):
                kv = h // 4
                for pi, (kindp, u) in enumerate(prostats(h)):
                    gi = pro_base[h] + pi
                    if kindp == "k2":
                        src_c = 2 * kv
                        srcT = env.KT
                    elif kindp == "q2":
                        src_c = 2 * h
                        srcT = env.QT
                    else:
                        continue
                    if gi >= 1:
                        a.wait_ge(S["cStE"].h, S["cStE"].base + gi)
                    for half in range(2):
                        inc(S["cSqQ"], a.square(
                            sqq[:, half * 512:(half + 1) * 512],
                            srcT[:, (src_c + half) * 2048 + u * 512:(src_c + half) * 2048 + (u + 1) * 512]))
                if h % 4 == 0:
                    a.wait_ge(S["cKm"].h, S["cKm"].base + km_after_kmax[h])
                    inc(S["cSqt"], a.activation(scl[0:1, 6 + kv:7 + kv], scl[0:1, 20:21], AF.Sqrt))
                a.wait_ge(S["cKm"].h, S["cKm"].base + km_after_qmax[h])
                inc(S["cSqt"], a.activation(scl[0:1, 5:6], scl[0:1, 4:5], AF.Sqrt))
                a.wait_ge(S["cBcM"].h, S["cBcM"].base + 2 * (h + 1))
                inc(S["cMce"], a.copy(mcol[:, 2 * h:2 * h + 1], PS[1][:, 0:1]))
                inc(S["cMce"], a.copy(mcol[:, 2 * h + 1:2 * h + 2], PS[1][:, 1:2]))
                for u in range(4):
                    hi = h * 4 + u
                    nj = NJ[hi]
                    nsc = nsc_of(hi)
                    for j in range(nj):
                        a.wait_ge(S["cScM"].h, S["cScM"].base + nsc + 1)
                        if j >= 2:
                            a.wait_ge(S["cAV"].h, S["cAV"].base + av_base(hi) + (j - 1))
                        inc(S["cExpE"], a.activation(Eb[:, (j % 2) * 512:(j % 2 + 1) * 512],
                                                     PS[nsc % 2][:, :], AF.Exp,
                                                     bias=mcol[:, 2 * h:2 * h + 1], scale=0.0625))
                        nsc += 1
                        a.wait_ge(S["cScM"].h, S["cScM"].base + nsc + 1)
                        inc(S["cExpE"], a.activation(gEb[:, (j % 2) * 512:(j % 2 + 1) * 512],
                                                     PS[nsc % 2][:, :], AF.Exp,
                                                     bias=mcol[:, 2 * h + 1:2 * h + 2]))
                        nsc += 1
                    a.wait_ge(S["cBcA"].h, S["cBcA"].base + 2 * (hi + 1))
                    if hi >= 1:
                        a.wait_ge(S["cCmb"].h, S["cCmb"].base + 6 * hi)
                    inc(S["cAbe"], a.copy(albe[:, 0:512], PS[0][:, :]))
                    inc(S["cAbe"], a.copy(albe[:, 512:1024], PS[1][:, :]))

        @blk.gpsimd
        def _(g):
            _entry_waits(env, g)

            def ld(LI):
                w, l = divmod(LI, 2)
                oc, up = divmod(w, 2)
                if LI >= 2:
                    g.wait_ge(S["cWoM"].h, S["cWoM"].base + 2 * ((LI - 2) // 2 + 1))
                dinc(S["sWO"], LI % 2, g.dma_start(
                    out=wob[:, (LI % 2) * 1024:(LI % 2 + 1) * 1024].rearrange("p (k o) -> p k o", k=8),
                    in_=env.io["Wo"][l * 1024:(l + 1) * 1024, oc * 128:(oc + 1) * 128].rearrange("(k p) o -> p k o", p=128)))

            def rd(i):
                oc, u = divmod(i, 4)
                if i >= 2:
                    g.wait_ge(S["cOe"].h, S["cOe"].base + (i - 1))
                dinc(S["sXD3"], i % 2, g.dma_start(
                    out=xcb[:, (i % 2) * 512:(i % 2 + 1) * 512],
                    in_=xin.ap()[oc * 128:(oc + 1) * 128, u * 512:(u + 1) * 512]))

            def wr(i):
                oc, u = divmod(i, 4)
                g.wait_ge(S["cOe"].h, S["cOe"].base + i + 1)
                dinc(S["sXW3"], i % 2, g.dma_start(
                    out=xout.ap()[oc * 128:(oc + 1) * 128, u * 512:(u + 1) * 512],
                    in_=xob[:, (i % 2) * 512:(i % 2 + 1) * 512]))

            ld(0)
            ld(1)
            rd(0)
            rd(1)
            for i in range(64):
                if i >= 2 and i % 2 == 0:
                    ld(i)
                    ld(i + 1)
                wr(i)
                if i + 2 < 64:
                    rd(i + 2)



def block_P5(env, src, dst, final):
    nc = env.nc
    S = dict(env.sems)
    S.update(_dsems(env, {"sX0": (0, 2), "sXTw": (2, 2)}))
    _snap_all(env)
    with (
        nc.sbuf_tensor("p5_xbuf", [128, 2 * 2048], F32) as xbuf,
        nc.sbuf_tensor("p5_trb", [128, 2 * 128], F32) as trb,
        nc.psum_tensor("p5_pt0", [128, 128], F32) as pt0,
        nc.psum_tensor("p5_pt1", [128, 128], F32) as pt1,
        nc.Block() as blk,
    ):
        pt = [pt0, pt1]

        @blk.gpsimd
        def _(g):
            _entry_waits(env, g)
            for tt in range(16):
                if tt >= 2:
                    g.wait_ge(S["cP1t"].h, S["cP1t"].base + 16 * (tt - 1))
                dinc(S["sX0"], tt % 2, g.dma_start(out=xbuf[:, (tt % 2) * 2048:(tt % 2 + 1) * 2048],
                                                   in_=src.ap()[tt * 128:(tt + 1) * 128, :]))

        @blk.tensor
        def _(t):
            _entry_waits(env, t)
            for i in range(256):
                tt, k = divmod(i, 16)
                if k == 0:
                    dwait(t, S["sX0"], tt % 2, tt // 2 + 1)
                if i >= 2:
                    t.wait_ge(S["cP1e"].h, S["cP1e"].base + (i - 1))
                inc(S["cP1t"], t.transpose(pt[i % 2][:, :],
                                           xbuf[:, (tt % 2) * 2048 + k * 128:(tt % 2) * 2048 + (k + 1) * 128],
                                           env.ct["cIDf"][:, :]))

        @blk.scalar
        def _(a):
            _entry_waits(env, a)
            for i in range(256):
                a.wait_ge(S["cP1t"].h, S["cP1t"].base + i + 1)
                if i >= 2:
                    dwait(a, S["sXTw"], i % 2, i // 2)
                inc(S["cP1e"], a.copy(trb[:, (i % 2) * 128:(i % 2 + 1) * 128],
                                      pt[i % 2][:, :]))

        @blk.sync
        def _(s):
            _entry_waits(env, s)
            for i in range(256):
                tt, k = divmod(i, 16)
                s.wait_ge(S["cP1e"].h, S["cP1e"].base + i + 1)
                dinc(S["sXTw"], i % 2, s.dma_start(out=dst.ap()[k * 128:(k + 1) * 128, tt * 128:(tt + 1) * 128],
                                                   in_=trb[:, (i % 2) * 128:(i % 2 + 1) * 128]))
            if final:
                dwait_all(s, S["sXTw"])


def build_nc():
    nc = bass.Bass("TRN2", target_bir_lowering=False)
    env = Env()
    env.nc = nc

    io = {}
    for name, arr in _CONSTS.items():
        io[name] = nc.dram_tensor(name, list(arr.shape),
                                  BF16 if arr.dtype == BF else F32,
                                  kind="ExternalInput").ap()
    for name, shape in [("x", [T, C]), ("Wq", [C, C]), ("Wk", [C, 512]),
                        ("Wv", [C, 512]), ("Wo", [C, C]), ("head_scales", [8]),
                        ("SU_ug", [C]), ("SU_dp", [INTER]), ("Wscale_ug", [2 * INTER]),
                        ("Wscale_dp", [C]), ("W_up", [INTER, C]), ("W_gate", [INTER, C]),
                        ("W_down", [C, INTER])]:
        io[name] = nc.dram_tensor(name, shape, F32, kind="ExternalInput").ap()
    env.io = io

    env.xT = nc.dram_tensor("xT", [C, T], F32, kind="Internal")
    env.x1T = nc.dram_tensor("x1T", [C, T], F32, kind="Internal")
    env.x2T = nc.dram_tensor("x2T", [C, T], F32, kind="Internal")
    env.x3T = nc.dram_tensor("x3T", [C, T], F32, kind="Internal")
    env.SNup = nc.dram_tensor("SNup", [INTER, C], BF16, kind="Internal")
    env.SNgate = nc.dram_tensor("SNgate", [INTER, C], BF16, kind="Internal")
    env.SNdown = nc.dram_tensor("SNdown", [C, INTER], BF16, kind="Internal")
    env.STdown = nc.dram_tensor("STdown", [INTER, C], BF16, kind="Internal")
    env.STup = nc.dram_tensor("STup", [C, INTER], BF16, kind="Internal")
    env.STgate = nc.dram_tensor("STgate", [C, INTER], BF16, kind="Internal")
    env.PBdump = nc.dram_tensor("PBdump", [4, 64, 128, 512], BF16, kind="Internal")
    env.SEff = nc.dram_tensor("SEff", [4, 64, 128, 512], BF16, kind="Internal")
    env.y = nc.dram_tensor("y", [T, C], F32, kind="ExternalOutput")
    if DBG:
        env.dbg = {k: nc.dram_tensor(k, [C, T], F32, kind="ExternalOutput")
                   for k in ("dbg_x1", "dbg_x2")}
        env.dbg["dbg_app"] = nc.dram_tensor("dbg_app", [128, 16 * 1024], BF16, kind="ExternalOutput")
        env.dbg["dbg_seff"] = nc.dram_tensor("dbg_seff", [4, 64, 128, 512], BF16, kind="ExternalOutput")
        env.dbg["dbg_y2"] = nc.dram_tensor("dbg_y2", [128, 16 * 2048], BF16, kind="ExternalOutput")
        env.dbg["dbg_qt"] = nc.dram_tensor("dbg_qt", [128, 16 * 2048], BF16, kind="ExternalOutput")
        env.dbg["dbg_kt"] = nc.dram_tensor("dbg_kt", [128, 4 * 2048], BF16, kind="ExternalOutput")
        env.dbg["dbg_v"] = nc.dram_tensor("dbg_v", [128, 16 * 512], BF16, kind="ExternalOutput")

    sem_names = [
        "cst", "cP1t", "cP1e", "sCu", "sCg", "sCd",
        "cPBm", "cPBe", "cDt", "cDe", "cDw",
        "cXA1", "cSq", "cXA2", "cTS", "cRowA", "cRowV", "cRowR",
        "cTB", "cAp", "cH16", "cYb", "cT128", "cAe",
        "cWU", "cWG", "cSE", "cEvU", "cSil",
        "cZp", "cTH", "cZh", "cEvD",
        "cQK", "cRope", "cVmm", "cVe", "cStM", "cStE", "cSqQ",
        "cScM", "cExpE", "cAV", "cRowF", "cBcA", "cAbe", "cCmb", "cWoM", "cOe",
        "cBias", "cBcM", "cMce", "cKm", "cSqt",
    ]
    N_DCELLS = 29

    from contextlib import ExitStack
    with ExitStack() as st:
        env.sems = {}
        for nm in sem_names:
            env.sems[nm] = Sem(st.enter_context(nc.semaphore(nm)))
        env.dcells = [Sem(st.enter_context(nc.semaphore(f"dma{k}")))
                      for k in range(N_DCELLS)]

        ct = {}
        for name, arr in _CONSTS.items():
            if name in ("cCos", "cSin"):
                continue
            dt = BF16 if arr.dtype == BF else F32
            ct[name] = st.enter_context(nc.sbuf_tensor("t_" + name, list(arr.shape), dt))
        ct["hs"] = st.enter_context(nc.sbuf_tensor("t_hs", [1, 8], F32))
        ct["hs1"] = st.enter_context(nc.sbuf_tensor("t_hs1", [1, 8], F32))
        ct["su_t"] = st.enter_context(nc.sbuf_tensor("t_su", [128, 16], F32))
        ct["co2_t"] = st.enter_context(nc.sbuf_tensor("t_co2", [128, 16], F32))
        ct["sd_t"] = st.enter_context(nc.sbuf_tensor("t_sd", [128, 64], F32))
        ct["cu_t"] = st.enter_context(nc.sbuf_tensor("t_cu", [128, 64], F32))
        ct["cg_t"] = st.enter_context(nc.sbuf_tensor("t_cg", [128, 64], F32))
        ct["w2_t"] = st.enter_context(nc.sbuf_tensor("t_w2", [128, 64], F32))
        env.ct = ct
        env.cst = None
        nc.const_aps.register(F32, EPS, ct["cMisc"][0:1, 0:1])

        block_B1(env)

        def ffn_call(xin, xout, tagp, wdc_call=False):
            g_app = nc.sbuf_tensor("t_app" + tagp, [128, 16 * SUP], BF16)
            env.APP = g_app.__enter__()
            for u in range(NSUP):
                wdc = wdc_call and u == 0
                block_B2a(env, xin, u, tagp + f"u{u}", wdc=wdc)
                g_uz = nc.sbuf_tensor(f"t_uz_{tagp}u{u}", [128, 64 * SUP], BF16)
                env.UZ = g_uz.__enter__()
                block_B2b(env, xin, xout, u, tagp + f"u{u}", wdc=wdc)
                g_uz.__exit__(None, None, None)
            g_app.__exit__(None, None, None)

        ffn_call(env.xT, env.x1T, "f1", wdc_call=True)

        g_qt = nc.sbuf_tensor("t_qt", [128, 16 * 2048], BF16)
        g_kt = nc.sbuf_tensor("t_kt", [128, 4 * 2048], BF16)
        g_v = nc.sbuf_tensor("t_v", [128, 16 * 512], BF16)
        env.QT = g_qt.__enter__()
        env.KT = g_kt.__enter__()
        env.V = g_v.__enter__()
        g_y2 = nc.sbuf_tensor("t_y2", [128, 16 * 2048], BF16)
        env.Y2 = g_y2.__enter__()
        block_B3a1(env, env.x1T)
        block_B3a2(env)
        g_y2.__exit__(None, None, None)
        g_ao = nc.sbuf_tensor("t_ao", [128, 16 * 2048], BF16)
        env.AO = g_ao.__enter__()
        block_B3b(env, env.x1T, env.x2T)
        g_ao.__exit__(None, None, None)
        g_v.__exit__(None, None, None)
        g_kt.__exit__(None, None, None)
        g_qt.__exit__(None, None, None)

        ffn_call(env.x2T, env.x3T, "f2")

        block_P5(env, env.x3T, env.y, final=True)

        if DBG:
            block_dbg(env)
    return nc


def block_dbg(env):
    nc = env.nc
    S = env.sems
    _snap_all(env)
    with nc.Block() as blk:
        @blk.gpsimd
        def _(g):
            _entry_waits(env, g)
            for i in range(4):
                inc(S["cst"], g.dma_start(out=env.dbg["dbg_x1"].ap()[i * 512:(i + 1) * 512, :],
                                          in_=env.x1T.ap()[i * 512:(i + 1) * 512, :]), 16)
                inc(S["cst"], g.dma_start(out=env.dbg["dbg_x2"].ap()[i * 512:(i + 1) * 512, :],
                                          in_=env.x2T.ap()[i * 512:(i + 1) * 512, :]), 16)
            for og in range(4):
                inc(S["cst"], g.dma_start(out=env.dbg["dbg_seff"].ap()[og],
                                          in_=env.SEff.ap()[og]), 16)
            g.wait_ge(S["cst"].h, S["cst"].n)


_NC_CACHE = None


def kernel(**inputs):
    global _NC_CACHE
    if _NC_CACHE is None:
        _NC_CACHE = build_nc()
    nc = _NC_CACHE
    x = np.ascontiguousarray(np.asarray(inputs["L0_attn_x"], dtype=np.float32))
    base = {}
    for name, arr in _CONSTS.items():
        base[name] = np.ascontiguousarray(arr)
    key_map = {
        "Wq": "L0_attn_Wq", "Wk": "L0_attn_Wk", "Wv": "L0_attn_Wv", "Wo": "L0_attn_Wo",
        "head_scales": "L0_attn_head_scales", "SU_ug": "L0_ffn_SU_ug",
        "SU_dp": "L0_ffn_SU_dp", "Wscale_ug": "L0_ffn_Wscale_ug",
        "Wscale_dp": "L0_ffn_Wscale_dp", "W_up": "L0_ffn_W_up",
        "W_gate": "L0_ffn_W_gate", "W_down": "L0_ffn_W_down",
    }
    for k, src in key_map.items():
        base[k] = np.ascontiguousarray(np.asarray(inputs[src], dtype=np.float32))
    in_maps = []
    for b in range(8):
        m = dict(base)
        m["x"] = x[b]
        in_maps.append(m)
    res = run_bass_kernel_spmd(nc, in_maps, core_ids=list(range(8)))
    out = np.stack([np.asarray(r["y"]) for r in res.results], axis=0)
    if DBG:
        kernel._dbg = [{k: np.asarray(v) for k, v in r.items()} for r in res.results]
    return out


# version 09
# speedup vs baseline: 1.0758x; 1.0357x over previous
import os
import numpy as np
import ml_dtypes
import concourse.bass as bass
import concourse.mybir as mybir
from concourse.bass_utils import run_bass_kernel_spmd

F32 = mybir.dt.float32
BF16 = mybir.dt.bfloat16
AF = mybir.ActivationFunctionType
ALU = mybir.AluOpType
AX = mybir.AxisListType

DBG = bool(int(os.environ.get("MK_DBG", "0")))

T = 2048
C = 2048
INTER = 8192
SUP = 1024
NSUP = T // SUP
NEG = -1.0e9
EPS = 1e-5
SAFE = 1.02

BF = ml_dtypes.bfloat16


def _hmat(n):
    h = np.array([[1.0]], dtype=np.float32)
    while h.shape[0] < n:
        h = np.block([[h, h], [h, -h]])
    return h.astype(np.float32)


def _consts():
    H128 = _hmat(128)
    H64 = _hmat(64)
    BD = np.kron(np.eye(2, dtype=np.float32), H64)
    I128 = np.eye(128, dtype=np.float32)
    onescol = np.ones((128, 1), np.float32)
    onesrow = np.ones((1, 128), np.float32)
    inv = 1.0 / (10000.0 ** (np.arange(0, 256, 2, dtype=np.float64) / 256.0))
    f = np.arange(T, dtype=np.float64)[:, None] * inv[None, :]
    cosL = np.cos(f).T.astype(np.float32)
    sinL = np.sin(f).T.astype(np.float32)
    masks = np.zeros((128, 4 * 512), np.float32)
    for r in range(4):
        sp = np.arange(128)[:, None] + r * 128
        tf = np.arange(512)[None, :]
        masks[:, r * 512:(r + 1) * 512] = np.where(sp <= tf, 0.0, NEG)
    dirs = np.array([[sx, sy, sz] for sx in (1.0, -1.0) for sy in (1.0, -1.0)
                     for sz in (1.0, -1.0)], dtype=np.float32) / np.sqrt(3.0)
    return dict(
        cHb=H128.astype(BF), cBD=BD.astype(BF), cIDf=I128, cIDb=I128.astype(BF),
        cOnesColB=onescol.astype(BF), cOnesRowB=onesrow.astype(BF),
        cOnesRowF=onesrow, cCos=cosL.astype(BF), cSin=sinL.astype(BF),
        cMask=masks.astype(BF), cDirs=dirs.T.copy().astype(BF),
        cMisc=np.array([[EPS, 0.0, 0.0, 0.0]], np.float32),
    )


_CONSTS = _consts()


class Sem:
    __slots__ = ("h", "n", "base")

    def __init__(self, h):
        self.h = h
        self.n = 0
        self.base = 0


class DSem:
    __slots__ = ("cells",)

    def __init__(self, cells):
        self.cells = cells


class Env:
    pass


def _dsems(env, spec):
    return {nm: DSem(env.dcells[s0:s0 + d]) for nm, (s0, d) in spec.items()}


def _snap_all(env):
    for s in env.sems.values():
        s.base = s.n
    for c in env.dcells:
        c.base = c.n


def _entry_waits(env, eng):
    for s in env.sems.values():
        if s.base > 0:
            eng.wait_ge(s.h, s.base)
    for c in env.dcells:
        if c.base > 0:
            eng.wait_ge(c.h, c.base)


def inc(s, ins, k=1):
    ins.then_inc(s.h, k)
    s.n += k
    return ins


def dinc(ds, slot, ins):
    c = ds.cells[slot]
    ins.then_inc(c.h, 16)
    c.n += 16
    return ins


def dwait(eng, ds, slot, rounds):
    c = ds.cells[slot]
    eng.wait_ge(c.h, c.base + 16 * rounds)


def dwait_first(eng, ds, k):
    d = len(ds.cells)
    for s in range(d):
        cnt = (k - s + d - 1) // d
        if cnt > 0:
            eng.wait_ge(ds.cells[s].h, ds.cells[s].base + 16 * cnt)


def dwait_all(eng, ds):
    for c in ds.cells:
        if c.n > 0:
            eng.wait_ge(c.h, c.n)


def block_B1(env):
    nc = env.nc
    S = dict(env.sems)
    S.update(_dsems(env, {"sX0": (0, 2), "sXTw": (2, 2)}))
    _snap_all(env)
    with (
        nc.sbuf_tensor("b1_xbuf", [128, 2 * 2048], F32) as xbuf,
        nc.sbuf_tensor("b1_trb", [128, 2 * 128], F32) as trb,
        nc.sbuf_tensor("b1_vecload", [64, 6 * 128], F32) as vl,
        nc.psum_tensor("b1_pt0", [128, 512], F32) as pt0,
        nc.psum_tensor("b1_pt1", [128, 512], F32) as pt1,
        nc.psum_tensor("b1_pt2", [128, 512], F32) as pt2,
        nc.psum_tensor("b1_pt3", [128, 512], F32) as pt3,
        nc.Block() as blk,
    ):
        pt = [pt0, pt1, pt2, pt3]
        io = env.io

        NV = {"su": 0, "wdp": 1, "sd": 2, "cu": 3, "cg": 4}

        @blk.gpsimd
        def _(g):
            _entry_waits(env, g)
            for name in ("cHb", "cBD", "cIDf", "cIDb", "cOnesColB", "cOnesRowB",
                         "cOnesRowF", "cMask", "cDirs", "cMisc"):
                inc(S["cst"], g.dma_start(out=env.ct[name][:], in_=io[name][:]), 16)
            inc(S["cst"], g.dma_start(out=env.ct["hs"][:], in_=io["head_scales"].rearrange("(o h) -> o h", o=1)), 16)
            inc(S["cst"], g.dma_start(out=vl[0:16, 0 * 128:1 * 128],
                                      in_=io["SU_ug"].rearrange("(c p) -> c p", p=128)), 16)
            inc(S["cst"], g.dma_start(out=vl[0:16, 1 * 128:2 * 128],
                                      in_=io["Wscale_dp"].rearrange("(c p) -> c p", p=128)), 16)
            inc(S["cst"], g.dma_start(out=vl[0:64, 2 * 128:3 * 128],
                                      in_=io["SU_dp"].rearrange("(c p) -> c p", p=128)), 16)
            inc(S["cst"], g.dma_start(out=vl[0:64, 3 * 128:4 * 128],
                                      in_=io["Wscale_ug"][0:8192].rearrange("(c p) -> c p", p=128)), 16)
            inc(S["cst"], g.dma_start(out=vl[0:64, 4 * 128:5 * 128],
                                      in_=io["Wscale_ug"][8192:16384].rearrange("(c p) -> c p", p=128)), 16)
            for tt in range(16):
                if tt >= 2:
                    g.wait_ge(S["cP1t"].h, S["cP1t"].base + 16 * (tt - 1))
                dinc(S["sX0"], tt % 2, g.dma_start(out=xbuf[:, (tt % 2) * 2048:(tt % 2 + 1) * 2048],
                                                   in_=io["x"][tt * 128:(tt + 1) * 128, :]))

        @blk.sync
        def _(s):
            _entry_waits(env, s)
            for i in range(256):
                tt, k = divmod(i, 16)
                s.wait_ge(S["cP1e"].h, S["cP1e"].base + i + 1)
                dinc(S["sXTw"], i % 2, s.dma_start(out=env.xT.ap()[k * 128:(k + 1) * 128, tt * 128:(tt + 1) * 128],
                                                   in_=trb[:, (i % 2) * 128:(i % 2 + 1) * 128]))

        @blk.tensor
        def _(t):
            _entry_waits(env, t)
            t.wait_ge(S["cst"].h, S["cst"].n)
            for idx, (name, P) in enumerate([("su", 16), ("wdp", 16), ("sd", 64),
                                             ("cu", 64), ("cg", 64)]):
                sl = NV[name]
                if idx >= 2:
                    t.wait_ge(S["cDe"].h, S["cDe"].base + (idx - 1))
                inc(S["cDt"], t.transpose(pt[idx % 2][:, 0:P],
                                          vl[0:P, sl * 128:sl * 128 + 128],
                                          env.ct["cIDf"][0:P, 0:P]))
            t.wait_ge(S["cDe"].h, S["cDe"].base + 5)
            for i in range(256):
                tt, k = divmod(i, 16)
                if k == 0:
                    dwait(t, S["sX0"], tt % 2, tt // 2 + 1)
                if i >= 4:
                    t.wait_ge(S["cP1e"].h, S["cP1e"].base + (i - 3))
                inc(S["cP1t"], t.transpose(pt[i % 4][:, 0:128],
                                           xbuf[:, (tt % 2) * 2048 + k * 128:(tt % 2) * 2048 + (k + 1) * 128],
                                           env.ct["cIDf"][:, :]))

        @blk.scalar
        def _(a):
            _entry_waits(env, a)
            for idx, (name, P) in enumerate([("su", 16), ("wdp", 16), ("sd", 64),
                                             ("cu", 64), ("cg", 64)]):
                a.wait_ge(S["cDt"].h, S["cDt"].base + idx + 1)
                scale = 1.0
                if name == "wdp":
                    scale = 2.0 ** -7.5
                elif name in ("cu", "cg"):
                    scale = 2.0 ** -5.5
                dst = {"su": env.ct["su_t"], "wdp": env.ct["co2_t"], "sd": env.ct["sd_t"],
                       "cu": env.ct["cu_t"], "cg": env.ct["cg_t"]}[name]
                inc(S["cDe"], a.mul(dst[:, 0:P], pt[idx % 2][:, 0:P], scale))
            for i in range(256):
                a.wait_ge(S["cP1t"].h, S["cP1t"].base + i + 1)
                if i >= 2:
                    dwait(a, S["sXTw"], i % 2, i // 2)
                inc(S["cP1e"], a.copy(trb[:, (i % 2) * 128:(i % 2 + 1) * 128],
                                      pt[i % 4][:, 0:128]))

        @blk.vector
        def _(v):
            _entry_waits(env, v)
            v.wait_ge(S["cDe"].h, S["cDe"].base + 4)
            inc(S["cDw"], v.tensor_mul(env.ct["w2_t"][:, 0:64], env.ct["cu_t"][:, 0:64],
                                       env.ct["sd_t"][:, 0:64]))
            v.wait_ge(S["cst"].h, S["cst"].n)
            inc(S["cDw"], v.tensor_scalar(env.ct["hs1"][0:1, 0:8], env.ct["hs"][0:1, 0:8],
                                          -1.0, 1.0, ALU.mult, ALU.add))


def block_B2a(env, xin, u, tag, wdc=False):
    nc = env.nc
    S = dict(env.sems)
    S.update(_dsems(env, {"sXA1": (0, 2), "sXA2": (2, 2), "sSTr": (4, 2),
                          "sSTw": (6, 2), "sPBr": (8, 3), "sDmp": (11, 3)}))
    _snap_all(env)
    from contextlib import ExitStack
    with ExitStack() as _es:
        _e = _es.enter_context
        xc = _e(nc.sbuf_tensor(f"a_xc_{tag}", [128, 2 * 512], F32))
        xb = _e(nc.sbuf_tensor(f"a_xb_{tag}", [128, 2 * 512], BF16))
        sq = _e(nc.sbuf_tensor(f"a_sq_{tag}", [128, 2 * 512], BF16))
        xc2 = _e(nc.sbuf_tensor(f"a_x2_{tag}", [128, 2 * 512], F32))
        rows = _e(nc.sbuf_tensor(f"a_rows_{tag}", [1, 8 * 512], F32))
        ping = _e(nc.sbuf_tensor(f"a_ping_{tag}", [128, 16 * 512], BF16))
        pong = _e(nc.sbuf_tensor(f"a_pong_{tag}", [128, 16 * 512], BF16))
        ybf = _e(nc.sbuf_tensor(f"a_ybf_{tag}", [128, 16 * 512], BF16))
        ps_s1 = _e(nc.psum_tensor(f"a_ps0_{tag}", [128, 512], F32))
        ps_s2 = _e(nc.psum_tensor(f"a_ps1_{tag}", [128, 512], F32))
        ps_m = _e(nc.psum_tensor(f"a_pm_{tag}", [128, 512], F32))
        ps_r = _e(nc.psum_tensor(f"a_pr_{tag}", [128, 512], F32))
        nph = 2 if wdc else 4
        ph = [_e(nc.psum_tensor(f"a_ph{j}_{tag}", [128, 512], F32))
              for j in range(nph)]
        if wdc:
            stdb = _e(nc.sbuf_tensor(f"a_std_{tag}", [128, 2 * 2048], BF16))
            pbr = _e(nc.sbuf_tensor(f"a_pbr_{tag}", [128, 2 * 2048], BF16))
            pbe = _e(nc.sbuf_tensor(f"a_pbe_{tag}", [128, 2 * 2048], BF16))
            pb0 = _e(nc.psum_tensor(f"a_pb0_{tag}", [128, 512], F32))
            pb1 = _e(nc.psum_tensor(f"a_pb1_{tag}", [128, 512], F32))
            pbp = [pb0, pb1]
        blk = _e(nc.Block())
        col0 = u * SUP
        RM, RE, RM2, RV, RR = 0, 1, 2, 3, 4

        @blk.gpsimd
        def _(g):
            _entry_waits(env, g)
            cast_j = [0]

            def maybe_cast():
                j = cast_j[0]
                if wdc and j < 8:
                    inc(S["sCd"], g.dma_start(
                        out=env.SNdown.ap()[j * 256:(j + 1) * 256, :],
                        in_=env.io["W_down"][j * 256:(j + 1) * 256, :]), 16)
                    cast_j[0] += 1

            for tc in range(2):
                for c in range(16):
                    i = tc * 16 + c
                    if i % 8 == 0:
                        maybe_cast()
                    if i >= 2:
                        g.wait_ge(S["cXA1"].h, S["cXA1"].base + (i - 1))
                        g.wait_ge(S["cSq"].h, S["cSq"].base + (i - 1))
                    dinc(S["sXA1"], i % 2, g.dma_start(
                        out=xc[:, (i % 2) * 512:(i % 2 + 1) * 512],
                        in_=xin.ap()[c * 128:(c + 1) * 128, col0 + tc * 512:col0 + (tc + 1) * 512]))
                for c in range(16):
                    i = tc * 16 + c
                    if i % 8 == 0:
                        maybe_cast()
                    if i >= 2:
                        g.wait_ge(S["cXA2"].h, S["cXA2"].base + (i - 1))
                    dinc(S["sXA2"], i % 2, g.dma_start(
                        out=xc2[:, (i % 2) * 512:(i % 2 + 1) * 512],
                        in_=xin.ap()[c * 128:(c + 1) * 128, col0 + tc * 512:col0 + (tc + 1) * 512]))
            if wdc:
                while cast_j[0] < 8:
                    maybe_cast()
                dmp = env.PBdump.ap()
                for pb in range(64):
                    g.wait_ge(S["cPBe"].h, S["cPBe"].base + 4 * pb + 4)
                    dinc(S["sDmp"], pb % 2, g.dma_start(
                        out=dmp[:, pb].rearrange("g p o -> p g o"),
                        in_=pbe[:, (pb % 2) * 2048:(pb % 2 + 1) * 2048].rearrange("p (g o) -> p g o", g=4)))

        if wdc:
            @blk.sync
            def _(s):
                _entry_waits(env, s)
                for c in range(64):
                    if c == 0:
                        s.wait_ge(S["sCd"].h, S["sCd"].base + 16 * 8)
                    if c >= 2:
                        dwait(s, S["sSTw"], c % 2, c // 2)
                    dinc(S["sSTr"], c % 2, s.dma_start(
                        out=stdb[:, (c % 2) * 2048:(c % 2 + 1) * 2048],
                        in_=env.SNdown.ap()[0:2048, c * 128:(c + 1) * 128],
                        transpose=True))
                    dwait(s, S["sSTr"], c % 2, c // 2 + 1)
                    dinc(S["sSTw"], c % 2, s.dma_start(
                        out=env.STdown.ap()[c * 128:(c + 1) * 128, :],
                        in_=stdb[:, (c % 2) * 2048:(c % 2 + 1) * 2048]))
                st4 = env.STdown.ap().rearrange("(c e q) o -> e c q o", c=64, e=2, q=64)
                for pb in range(64):
                    if pb == 0:
                        dwait_first(s, S["sSTw"], 64)
                    if pb >= 2:
                        s.wait_ge(S["cPBm"].h, S["cPBm"].base + 4 * (pb - 1))
                    dinc(S["sPBr"], pb % 2, s.dma_start(
                        out=pbr[:, (pb % 2) * 2048:(pb % 2 + 1) * 2048],
                        in_=st4[:, :, pb, :]))

        @blk.vector
        def _(v):
            _entry_waits(env, v)
            for tc in range(2):
                for c in range(16):
                    i = tc * 16 + c
                    dwait(v, S["sXA1"], i % 2, i // 2 + 1)
                    if i >= 2:
                        v.wait_ge(S["cTS"].h, S["cTS"].base + (i - 1))
                    inc(S["cXA1"], v.tensor_copy(xb[:, (i % 2) * 512:(i % 2 + 1) * 512],
                                                 xc[:, (i % 2) * 512:(i % 2 + 1) * 512]))
                v.wait_ge(S["cRowA"].h, S["cRowA"].base + tc * 4 + 3)
                inc(S["cRowV"], v.tensor_sub(rows[0:1, RV * 512:RV * 512 + 512],
                                             rows[0:1, RE * 512:RE * 512 + 512],
                                             rows[0:1, RM2 * 512:RM2 * 512 + 512]))
                for c in range(16):
                    i = tc * 16 + c
                    dwait(v, S["sXA2"], i % 2, i // 2 + 1)
                    v.wait_ge(S["cTB"].h, S["cTB"].base + 2 * (tc + 1))
                    inc(S["cXA2"], v.tensor_sub(pong[:, c * 512:(c + 1) * 512],
                                                xc2[:, (i % 2) * 512:(i % 2 + 1) * 512],
                                                ps_m[:, :]))
                    inc(S["cAp"], v.scalar_tensor_tensor(
                        ping[:, c * 512:(c + 1) * 512],
                        pong[:, c * 512:(c + 1) * 512],
                        env.ct["su_t"][:, c:c + 1],
                        ps_r[:, :], ALU.mult, ALU.mult))
                bufs = [ping, pong]
                for si, h in enumerate([1, 2, 4, 8]):
                    src = bufs[si % 2]
                    dst = bufs[(si + 1) % 2]
                    if si == 3 and tc == 1:
                        v.wait_ge(S["cT128"].h, S["cT128"].base + 16)
                    for cpair in range(8):
                        blkb = (cpair // h) * (2 * h)
                        ci = blkb + (cpair % h)
                        cj = ci + h
                        for c_out, op, aa, bb in ((ci, ALU.add, ci, cj), (cj, ALU.subtract, ci, cj)):
                            if si < 3:
                                inc(S["cH16"], v.tensor_tensor(
                                    dst[:, c_out * 512:(c_out + 1) * 512],
                                    src[:, aa * 512:(aa + 1) * 512],
                                    src[:, bb * 512:(bb + 1) * 512], op))
                            else:
                                inc(S["cYb"], v.tensor_tensor(
                                    ybf[:, c_out * 512:(c_out + 1) * 512],
                                    src[:, aa * 512:(aa + 1) * 512],
                                    src[:, bb * 512:(bb + 1) * 512], op))
            if wdc:
                for i in range(256):
                    v.wait_ge(S["cPBm"].h, S["cPBm"].base + i + 1)
                    if i >= 8:
                        dwait(v, S["sDmp"], (i // 4) % 2, i // 8)
                    ecol = ((i // 4) % 2) * 2048 + (i % 4) * 512
                    inc(S["cPBe"], v.tensor_copy(pbe[:, ecol:ecol + 512],
                                                 pbp[i % 2][:, :]))

        @blk.scalar
        def _(a):
            _entry_waits(env, a)
            for tc in range(2):
                for c in range(16):
                    i = tc * 16 + c
                    dwait(a, S["sXA1"], i % 2, i // 2 + 1)
                    if i >= 2:
                        a.wait_ge(S["cTS"].h, S["cTS"].base + (i - 1))
                    inc(S["cSq"], a.square(sq[:, (i % 2) * 512:(i % 2 + 1) * 512],
                                           xc[:, (i % 2) * 512:(i % 2 + 1) * 512]))
                a.wait_ge(S["cTS"].h, S["cTS"].base + 16 * (tc + 1))
                inc(S["cRowA"], a.mul(rows[0:1, RM * 512:RM * 512 + 512], ps_s1[0:1, :], 1.0 / 2048.0))
                inc(S["cRowA"], a.mul(rows[0:1, RE * 512:RE * 512 + 512], ps_s2[0:1, :], 1.0 / 2048.0))
                inc(S["cRowA"], a.square(rows[0:1, RM2 * 512:RM2 * 512 + 512],
                                         rows[0:1, RM * 512:RM * 512 + 512]))
                a.wait_ge(S["cRowV"].h, S["cRowV"].base + tc + 1)
                inc(S["cRowA"], a.activation(rows[0:1, 5 * 512:5 * 512 + 512],
                                             rows[0:1, RV * 512:RV * 512 + 512],
                                             AF.Ln, bias=EPS))
                inc(S["cRowR"], a.activation(rows[0:1, RR * 512:RR * 512 + 512],
                                             rows[0:1, 5 * 512:5 * 512 + 512],
                                             AF.Exp, scale=-0.5))
                for c in range(16):
                    a.wait_ge(S["cT128"].h, S["cT128"].base + tc * 16 + c + 1)
                    inc(S["cAe"], a.copy(env.APP[:, c * 1024 + tc * 512: c * 1024 + (tc + 1) * 512],
                                         ph[c % nph][:, :]))

        @blk.tensor
        def _(t):
            _entry_waits(env, t)
            for tc in range(2):
                if tc == 1:
                    t.wait_ge(S["cRowA"].h, S["cRowA"].base + 2)
                for c in range(16):
                    i = tc * 16 + c
                    t.wait_ge(S["cXA1"].h, S["cXA1"].base + i + 1)
                    t.wait_ge(S["cSq"].h, S["cSq"].base + i + 1)
                    t.matmul(ps_s1[0:1, :], env.ct["cOnesColB"][:, :],
                             xb[:, (i % 2) * 512:(i % 2 + 1) * 512],
                             start=(c == 0), stop=(c == 15))
                    inc(S["cTS"], t.matmul(ps_s2[0:1, :], env.ct["cOnesColB"][:, :],
                                           sq[:, (i % 2) * 512:(i % 2 + 1) * 512],
                                           start=(c == 0), stop=(c == 15)))
                if tc == 1:
                    t.wait_ge(S["cAp"].h, S["cAp"].base + 16)
                t.wait_ge(S["cRowA"].h, S["cRowA"].base + tc * 4 + 1)
                inc(S["cTB"], t.matmul(ps_m[:, :], env.ct["cOnesRowF"][:, :],
                                       rows[0:1, RM * 512:RM * 512 + 512], start=True, stop=True))
                t.wait_ge(S["cRowR"].h, S["cRowR"].base + tc + 1)
                inc(S["cTB"], t.matmul(ps_r[:, :], env.ct["cOnesRowF"][:, :],
                                       rows[0:1, RR * 512:RR * 512 + 512], start=True, stop=True))
                for c in range(16):
                    need = 2 * c + 1 if c < 8 else 2 * (c - 8) + 2
                    t.wait_ge(S["cYb"].h, S["cYb"].base + tc * 16 + need)
                    if c >= nph:
                        t.wait_ge(S["cAe"].h, S["cAe"].base + tc * 16 + (c - nph + 1))
                    inc(S["cT128"], t.matmul(ph[c % nph][:, :], env.ct["cHb"][:, :],
                                             ybf[:, c * 512:(c + 1) * 512],
                                             start=True, stop=True))
            if wdc:
                for i in range(256):
                    if i % 4 == 0:
                        dwait(t, S["sPBr"], (i // 4) % 2, i // 8 + 1)
                    if i >= 2:
                        t.wait_ge(S["cPBe"].h, S["cPBe"].base + (i - 1))
                    pcol = ((i // 4) % 2) * 2048 + (i % 4) * 512
                    inc(S["cPBm"], t.matmul(pbp[i % 2][:, :], env.ct["cBD"][:, :],
                                            pbr[:, pcol:pcol + 512],
                                            start=True, stop=True))


def block_B2b(env, xin, xout, u, tag, wdc=False):
    nc = env.nc
    S = dict(env.sems)
    S.update(_dsems(env, {"sWU": (0, 3), "sWG": (3, 3), "sSE": (6, 3),
                          "sXD": (9, 4), "sXW": (13, 4),
                          "sRbr": (17, 3), "sSef": (20, 3),
                          "sTU": (23, 3), "sTG": (26, 3)}))
    _snap_all(env)
    GS = [3] * 21 + [1]
    GBASE = [3 * i for i in range(22)]
    from contextlib import ExitStack
    with ExitStack() as _es:
        _e = _es.enter_context
        wub = _e(nc.sbuf_tensor(f"b_wu_{tag}", [128, 2 * 2048], BF16))
        wgb = _e(nc.sbuf_tensor(f"b_wg_{tag}", [128, 2 * 2048], BF16))
        seb = _e(nc.sbuf_tensor(f"b_se_{tag}", [128, 2 * 2048], BF16))
        zpb = _e(nc.sbuf_tensor(f"b_zp_{tag}", [128, 2 * 512], BF16))
        silb = _e(nc.sbuf_tensor(f"b_sil_{tag}", [128, 2 * 512], BF16))
        xcb = _e(nc.sbuf_tensor(f"b_xc_{tag}", [128, 2 * 512], F32))
        xob = _e(nc.sbuf_tensor(f"b_xo_{tag}", [128, 2 * 512], F32))
        P = [_e(nc.psum_tensor(f"b_p{j}_{tag}", [128, 512], F32)) for j in range(8)]
        if wdc:
            rbb = _e(nc.sbuf_tensor(f"b_rbb_{tag}", [128, 3 * 512], BF16))
        blk = _e(nc.Block())
        col0 = u * SUP

        @blk.sync
        def _(s):
            _entry_waits(env, s)
            dmp4 = env.PBdump.ap().rearrange("og p0 (e c) o -> og e p0 c o", e=2)

            def reblock_read(i):
                og, c = divmod(i, 64)
                if i >= 3:
                    dwait(s, S["sSef"], i % 3, i // 3)
                dinc(S["sRbr"], i % 3, s.dma_start(
                    out=rbb[:, (i % 3) * 512:(i % 3 + 1) * 512],
                    in_=dmp4[og, :, :, c, :]))

            if wdc:
                for i in range(256):
                    g_, k = divmod(i, 16)
                    reblock_read(i)
                    if k == 0:
                        s.wait_ge(S["sCu"].h, S["sCu"].base + 16 * (g_ + 1))
                    if i >= 3:
                        s.wait_ge(S["cWU"].h, S["cWU"].base + (i - 2))
                        dwait(s, S["sTU"], i % 3, i // 3)
                    dinc(S["sWU"], i % 3, s.dma_start(out=wub[:, (i % 3) * 512:(i % 3 + 1) * 512],
                                                      in_=env.SNup.ap()[g_ * 512:(g_ + 1) * 512,
                                                                        k * 128:(k + 1) * 128],
                                                      transpose=True))
            else:
                for b in range(64):
                    g_, kb = divmod(b, 4)
                    if b >= 2:
                        s.wait_ge(S["cWU"].h, S["cWU"].base + 4 * (b - 1))
                    dinc(S["sWU"], b % 2, s.dma_start(
                        out=wub[:, (b % 2) * 2048:(b % 2 + 1) * 2048].rearrange("p (kk f) -> p kk f", kk=4),
                        in_=env.STup.ap()[kb * 512:(kb + 1) * 512,
                                          g_ * 512:(g_ + 1) * 512].rearrange("(kk p) f -> p kk f", p=128)))
            if wdc:
                for g3 in range(22):
                    for k in range(16):
                        i = g3 * 16 + k
                        if k == 0:
                            nchunk = -(-(GBASE[g3] + GS[g3]) * 128 // 512)
                            s.wait_ge(S["sCg"].h, S["sCg"].base + 16 * nchunk)
                        if i >= 3:
                            s.wait_ge(S["cWG"].h, S["cWG"].base + (i - 2))
                            dwait(s, S["sTG"], i % 3, i // 3)
                        a0 = GBASE[g3]
                        na = GS[g3]
                        dinc(S["sWG"], i % 3, s.dma_start(out=wgb[:, (i % 3) * 512:(i % 3) * 512 + na * 128],
                                                          in_=env.SNgate.ap()[a0 * 128:a0 * 128 + na * 128,
                                                                              k * 128:(k + 1) * 128],
                                                          transpose=True))
            else:
                for b in range(88):
                    g3, kb = divmod(b, 4)
                    a0 = GBASE[g3]
                    na = GS[g3]
                    if b >= 2:
                        s.wait_ge(S["cWG"].h, S["cWG"].base + 4 * (b - 1))
                    dinc(S["sWG"], b % 2, s.dma_start(
                        out=wgb[:, (b % 2) * 2048:(b % 2) * 2048 + 4 * na * 128].rearrange(
                            "p (kk f) -> p kk f", kk=4),
                        in_=env.STgate.ap()[kb * 512:(kb + 1) * 512,
                                            a0 * 128:a0 * 128 + na * 128].rearrange("(kk p) f -> p kk f", p=128)))
            sef = env.SEff.ap()
            for b in range(64):
                og, ab = divmod(b, 16)
                if wdc:
                    dwait_first(s, S["sSef"], 4 * b + 4)
                if b >= 2:
                    s.wait_ge(S["cSE"].h, S["cSE"].base + 4 * (b - 1))
                dinc(S["sSE"], b % 2, s.dma_start(
                    out=seb[:, (b % 2) * 2048:(b % 2 + 1) * 2048].rearrange("p (a o) -> p a o", a=4),
                    in_=sef[og, 4 * ab:4 * ab + 4].rearrange("a p o -> p a o")))

        @blk.gpsimd
        def _(g):
            _entry_waits(env, g)
            if wdc:
                for j in range(16):
                    inc(S["sCu"], g.dma_start(
                        out=env.SNup.ap()[j * 512:(j + 1) * 512, :],
                        in_=env.io["W_up"][j * 512:(j + 1) * 512, :]), 16)
                for j in range(16):
                    inc(S["sCg"], g.dma_start(
                        out=env.SNgate.ap()[j * 512:(j + 1) * 512, :],
                        in_=env.io["W_gate"][j * 512:(j + 1) * 512, :]), 16)
                sef = env.SEff.ap()
                for i in range(256):
                    og, c = divmod(i, 64)
                    dwait(g, S["sRbr"], i % 3, i // 3 + 1)
                    dinc(S["sSef"], i % 3, g.dma_start(
                        out=sef[og, c], in_=rbb[:, (i % 3) * 512:(i % 3 + 1) * 512]))

            def rd(i):
                og, r8 = divmod(i, 8)
                tc, oci = divmod(r8, 4)
                oc = og * 4 + oci
                if i >= 2:
                    g.wait_ge(S["cEvD"].h, S["cEvD"].base + (i - 1))
                dinc(S["sXD"], i % 2, g.dma_start(
                    out=xcb[:, (i % 2) * 512:(i % 2 + 1) * 512],
                    in_=xin.ap()[oc * 128:(oc + 1) * 128, col0 + tc * 512:col0 + (tc + 1) * 512]))

            def wr(i):
                og, r8 = divmod(i, 8)
                tc, oci = divmod(r8, 4)
                oc = og * 4 + oci
                g.wait_ge(S["cEvD"].h, S["cEvD"].base + i + 1)
                dinc(S["sXW"], i % 2, g.dma_start(
                    out=xout.ap()[oc * 128:(oc + 1) * 128, col0 + tc * 512:col0 + (tc + 1) * 512],
                    in_=xob[:, (i % 2) * 512:(i % 2 + 1) * 512]))

            for i in range(2):
                rd(i)
            for i in range(32):
                wr(i)
                if i + 2 < 32:
                    rd(i + 2)

        @blk.tensor
        def _(t):
            _entry_waits(env, t)
            for g_ in range(16):
                if g_ >= 1:
                    t.wait_ge(S["cEvU"].h, S["cEvU"].base + 8 * g_)
                for k in range(16):
                    i = g_ * 16 + k
                    if wdc:
                        dwait(t, S["sWU"], i % 3, i // 3 + 1)
                        wcol = (i % 3) * 512
                    else:
                        if k % 4 == 0:
                            dwait(t, S["sWU"], (i // 4) % 2, i // 8 + 1)
                        wcol = ((i // 4) % 2) * 2048 + (k % 4) * 512
                    for tc in range(2):
                        for ai in range(4):
                            ins = t.matmul(P[tc * 4 + ai][:, :],
                                           wub[:, wcol + ai * 128:wcol + (ai + 1) * 128],
                                           env.APP[:, k * 1024 + tc * 512:k * 1024 + (tc + 1) * 512],
                                           start=(k == 0), stop=(k == 15))
                    inc(S["cWU"], ins)
            t.wait_ge(S["cEvU"].h, S["cEvU"].base + 128)
            for g3 in range(22):
                na = GS[g3]
                a0 = GBASE[g3]
                if g3 >= 1:
                    t.wait_ge(S["cSil"].h, S["cSil"].base + 2 * sum(GS[:g3]))
                for k in range(16):
                    i = g3 * 16 + k
                    if wdc:
                        dwait(t, S["sWG"], i % 3, i // 3 + 1)
                        wcol = (i % 3) * 512
                    else:
                        if k % 4 == 0:
                            dwait(t, S["sWG"], (i // 4) % 2, i // 8 + 1)
                        wcol = ((i // 4) % 2) * 2048 + (k % 4) * na * 128
                    for tc in range(2):
                        for ai in range(na):
                            ins = t.matmul(P[tc * 3 + ai][:, :],
                                           wgb[:, wcol + ai * 128:wcol + (ai + 1) * 128],
                                           env.APP[:, k * 1024 + tc * 512:k * 1024 + (tc + 1) * 512],
                                           start=(k == 0), stop=(k == 15))
                    inc(S["cWG"], ins)
                for tc in range(2):
                    for ai in range(na):
                        j = 2 * sum(GS[:g3]) + tc * na + ai
                        t.wait_ge(S["cZp"].h, S["cZp"].base + j + 1)
                        if j >= 2:
                            t.wait_ge(S["cZh"].h, S["cZh"].base + (j - 1))
                        inc(S["cTH"], t.matmul(P[6 + j % 2][:, :], env.ct["cHb"][:, :],
                                               zpb[:, (j % 2) * 512:(j % 2 + 1) * 512],
                                               start=True, stop=True))
            t.wait_ge(S["cZh"].h, S["cZh"].base + 128)
            t.wait_ge(S["cSil"].h, S["cSil"].base + 128)
            for og in range(4):
                if og >= 1:
                    t.wait_ge(S["cEvD"].h, S["cEvD"].base + 8 * og)
                for a_ in range(64):
                    i = og * 64 + a_
                    if a_ % 4 == 0:
                        dwait(t, S["sSE"], (i // 4) % 2, i // 8 + 1)
                    scol = ((i // 4) % 2) * 2048 + (a_ % 4) * 512
                    for tc in range(2):
                        for oci in range(4):
                            ins = t.matmul(P[tc * 4 + oci][:, :],
                                           seb[:, scol + oci * 128:scol + (oci + 1) * 128],
                                           env.UZ[:, a_ * 1024 + tc * 512:a_ * 1024 + (tc + 1) * 512],
                                           start=(a_ == 0), stop=(a_ == 63))
                    inc(S["cSE"], ins)

        @blk.scalar
        def _(a):
            _entry_waits(env, a)
            for g_ in range(16):
                if wdc:
                    for k in range(16):
                        i = g_ * 16 + k
                        a.wait_ge(S["cWU"].h, S["cWU"].base + i + 1)
                        dinc(S["sTU"], i % 3, a.dma_start(
                            out=env.STup.ap()[k * 128:(k + 1) * 128, g_ * 512:(g_ + 1) * 512],
                            in_=wub[:, (i % 3) * 512:(i % 3 + 1) * 512]))
                a.wait_ge(S["cWU"].h, S["cWU"].base + 16 * (g_ + 1))
                for tc in range(2):
                    for ai in range(4):
                        au = 4 * g_ + ai
                        inc(S["cEvU"], a.copy(
                            env.UZ[:, au * 1024 + tc * 512:au * 1024 + (tc + 1) * 512],
                            P[tc * 4 + ai][:, :]))
            for g3 in range(22):
                na = GS[g3]
                a0 = GBASE[g3]
                if wdc:
                    for k in range(16):
                        i = g3 * 16 + k
                        a.wait_ge(S["cWG"].h, S["cWG"].base + i + 1)
                        dinc(S["sTG"], i % 3, a.dma_start(
                            out=env.STgate.ap()[k * 128:(k + 1) * 128, a0 * 128:a0 * 128 + na * 128],
                            in_=wgb[:, (i % 3) * 512:(i % 3) * 512 + na * 128]))
                a.wait_ge(S["cWG"].h, S["cWG"].base + 16 * (g3 + 1))
                for tc in range(2):
                    for ai in range(na):
                        j = 2 * sum(GS[:g3]) + tc * na + ai
                        a_ = a0 + ai
                        if j >= 2:
                            a.wait_ge(S["cZp"].h, S["cZp"].base + (j - 1))
                        inc(S["cSil"], a.activation(
                            silb[:, (j % 2) * 512:(j % 2 + 1) * 512],
                            P[tc * 3 + ai][:, :], AF.Silu,
                            scale=env.ct["cg_t"][:, a_:a_ + 1]))
                for tc in range(2):
                    for ai in range(na):
                        j = 2 * sum(GS[:g3]) + tc * na + ai
                        a_ = a0 + ai
                        a.wait_ge(S["cTH"].h, S["cTH"].base + j + 1)
                        inc(S["cZh"], a.copy(
                            env.UZ[:, a_ * 1024 + tc * 512:a_ * 1024 + (tc + 1) * 512],
                            P[6 + j % 2][:, :]))

        @blk.vector
        def _(v):
            _entry_waits(env, v)
            for g3 in range(22):
                na = GS[g3]
                a0 = GBASE[g3]
                for tc in range(2):
                    for ai in range(na):
                        j = 2 * sum(GS[:g3]) + tc * na + ai
                        a_ = a0 + ai
                        v.wait_ge(S["cSil"].h, S["cSil"].base + j + 1)
                        if j >= 2:
                            v.wait_ge(S["cTH"].h, S["cTH"].base + (j - 1))
                        inc(S["cZp"], v.scalar_tensor_tensor(
                            zpb[:, (j % 2) * 512:(j % 2 + 1) * 512],
                            env.UZ[:, a_ * 1024 + tc * 512:a_ * 1024 + (tc + 1) * 512],
                            env.ct["w2_t"][:, a_:a_ + 1],
                            silb[:, (j % 2) * 512:(j % 2 + 1) * 512],
                            ALU.mult, ALU.mult))
            for og in range(4):
                v.wait_ge(S["cSE"].h, S["cSE"].base + 64 * (og + 1))
                for tc in range(2):
                    for oci in range(4):
                        i = og * 8 + tc * 4 + oci
                        oc = og * 4 + oci
                        dwait(v, S["sXD"], i % 2, i // 2 + 1)
                        if i >= 2:
                            dwait(v, S["sXW"], i % 2, i // 2)
                        inc(S["cEvD"], v.scalar_tensor_tensor(
                            xob[:, (i % 2) * 512:(i % 2 + 1) * 512],
                            P[tc * 4 + oci][:, :],
                            env.ct["co2_t"][:, oc:oc + 1],
                            xcb[:, (i % 2) * 512:(i % 2 + 1) * 512],
                            ALU.mult, ALU.add))


def block_B3a1(env, xin):
    nc = env.nc
    S = dict(env.sems)
    S.update(_dsems(env, {"sXA1": (0, 2), "sXA2": (2, 2)}))
    _snap_all(env)
    with (
        nc.sbuf_tensor("c1_xc", [128, 2 * 512], F32) as xc,
        nc.sbuf_tensor("c1_xb", [128, 2 * 512], BF16) as xb,
        nc.sbuf_tensor("c1_sq", [128, 2 * 512], BF16) as sq,
        nc.sbuf_tensor("c1_x2", [128, 2 * 512], F32) as xc2,
        nc.sbuf_tensor("c1_ap", [128, 2 * 512], F32) as apb,
        nc.sbuf_tensor("c1_rows", [1, 8 * 512], F32) as rows,
        nc.psum_tensor("c1_ps0", [128, 512], F32) as ps_s1,
        nc.psum_tensor("c1_ps1", [128, 512], F32) as ps_s2,
        nc.psum_tensor("c1_pm", [128, 512], F32) as ps_m,
        nc.psum_tensor("c1_pr", [128, 512], F32) as ps_r,
        nc.Block() as blk,
    ):
        RM, RE, RM2, RV, RR, RL = 0, 1, 2, 3, 4, 5
        NT = 4

        @blk.gpsimd
        def _(g):
            _entry_waits(env, g)
            for tc in range(NT):
                for c in range(16):
                    i = tc * 16 + c
                    if i >= 2:
                        g.wait_ge(S["cXA1"].h, S["cXA1"].base + (i - 1))
                        g.wait_ge(S["cSq"].h, S["cSq"].base + (i - 1))
                    dinc(S["sXA1"], i % 2, g.dma_start(
                        out=xc[:, (i % 2) * 512:(i % 2 + 1) * 512],
                        in_=xin.ap()[c * 128:(c + 1) * 128, tc * 512:(tc + 1) * 512]))
                for c in range(16):
                    i = tc * 16 + c
                    if i >= 2:
                        g.wait_ge(S["cXA2"].h, S["cXA2"].base + (i - 1))
                    dinc(S["sXA2"], i % 2, g.dma_start(
                        out=xc2[:, (i % 2) * 512:(i % 2 + 1) * 512],
                        in_=xin.ap()[c * 128:(c + 1) * 128, tc * 512:(tc + 1) * 512]))

        @blk.vector
        def _(v):
            _entry_waits(env, v)
            for tc in range(NT):
                for c in range(16):
                    i = tc * 16 + c
                    dwait(v, S["sXA1"], i % 2, i // 2 + 1)
                    if i >= 2:
                        v.wait_ge(S["cTS"].h, S["cTS"].base + (i - 1))
                    inc(S["cXA1"], v.tensor_copy(xb[:, (i % 2) * 512:(i % 2 + 1) * 512],
                                                 xc[:, (i % 2) * 512:(i % 2 + 1) * 512]))
                v.wait_ge(S["cRowA"].h, S["cRowA"].base + tc * 4 + 3)
                inc(S["cRowV"], v.tensor_sub(rows[0:1, RV * 512:RV * 512 + 512],
                                             rows[0:1, RE * 512:RE * 512 + 512],
                                             rows[0:1, RM2 * 512:RM2 * 512 + 512]))
                for c in range(16):
                    i = tc * 16 + c
                    dwait(v, S["sXA2"], i % 2, i // 2 + 1)
                    v.wait_ge(S["cTB"].h, S["cTB"].base + 2 * (tc + 1))
                    inc(S["cXA2"], v.tensor_sub(apb[:, (i % 2) * 512:(i % 2 + 1) * 512],
                                                xc2[:, (i % 2) * 512:(i % 2 + 1) * 512],
                                                ps_m[:, :]))
                    inc(S["cAp"], v.tensor_mul(
                        env.Y2[:, c * 2048 + tc * 512:c * 2048 + (tc + 1) * 512],
                        apb[:, (i % 2) * 512:(i % 2 + 1) * 512], ps_r[:, :]))

        @blk.scalar
        def _(a):
            _entry_waits(env, a)
            for tc in range(NT):
                for c in range(16):
                    i = tc * 16 + c
                    dwait(a, S["sXA1"], i % 2, i // 2 + 1)
                    if i >= 2:
                        a.wait_ge(S["cTS"].h, S["cTS"].base + (i - 1))
                    inc(S["cSq"], a.square(sq[:, (i % 2) * 512:(i % 2 + 1) * 512],
                                           xc[:, (i % 2) * 512:(i % 2 + 1) * 512]))
                a.wait_ge(S["cTS"].h, S["cTS"].base + 16 * (tc + 1))
                inc(S["cRowA"], a.mul(rows[0:1, RM * 512:RM * 512 + 512], ps_s1[0:1, :], 1.0 / 2048.0))
                inc(S["cRowA"], a.mul(rows[0:1, RE * 512:RE * 512 + 512], ps_s2[0:1, :], 1.0 / 2048.0))
                inc(S["cRowA"], a.square(rows[0:1, RM2 * 512:RM2 * 512 + 512],
                                         rows[0:1, RM * 512:RM * 512 + 512]))
                a.wait_ge(S["cRowV"].h, S["cRowV"].base + tc + 1)
                inc(S["cRowA"], a.activation(rows[0:1, RL * 512:RL * 512 + 512],
                                             rows[0:1, RV * 512:RV * 512 + 512],
                                             AF.Ln, bias=EPS))
                inc(S["cRowR"], a.activation(rows[0:1, RR * 512:RR * 512 + 512],
                                             rows[0:1, RL * 512:RL * 512 + 512],
                                             AF.Exp, scale=-0.5))

        @blk.tensor
        def _(t):
            _entry_waits(env, t)
            for tc in range(NT):
                if tc >= 1:
                    t.wait_ge(S["cRowA"].h, S["cRowA"].base + (tc - 1) * 4 + 2)
                for c in range(16):
                    i = tc * 16 + c
                    t.wait_ge(S["cXA1"].h, S["cXA1"].base + i + 1)
                    t.wait_ge(S["cSq"].h, S["cSq"].base + i + 1)
                    t.matmul(ps_s1[0:1, :], env.ct["cOnesColB"][:, :],
                             xb[:, (i % 2) * 512:(i % 2 + 1) * 512],
                             start=(c == 0), stop=(c == 15))
                    inc(S["cTS"], t.matmul(ps_s2[0:1, :], env.ct["cOnesColB"][:, :],
                                           sq[:, (i % 2) * 512:(i % 2 + 1) * 512],
                                           start=(c == 0), stop=(c == 15)))
                if tc >= 1:
                    t.wait_ge(S["cAp"].h, S["cAp"].base + 16 * tc)
                t.wait_ge(S["cRowA"].h, S["cRowA"].base + tc * 4 + 1)
                inc(S["cTB"], t.matmul(ps_m[:, :], env.ct["cOnesRowF"][:, :],
                                       rows[0:1, RM * 512:RM * 512 + 512], start=True, stop=True))
                t.wait_ge(S["cRowR"].h, S["cRowR"].base + tc + 1)
                inc(S["cTB"], t.matmul(ps_r[:, :], env.ct["cOnesRowF"][:, :],
                                       rows[0:1, RR * 512:RR * 512 + 512], start=True, stop=True))


def block_B3a2(env):
    nc = env.nc
    S = dict(env.sems)
    S.update(_dsems(env, {"sWQ": (0, 2), "sWV": (2, 2)}))
    _snap_all(env)
    QU = [("q", h, u4) for h in range(8) for u4 in range(4)] + \
         [("k", kv, u4) for kv in range(2) for u4 in range(4)]
    LOADS = [("q", h) for h in range(8)] + [("k", kv) for kv in range(2)]

    def load_of_unit(ui):
        kind, hh, u4 = QU[ui]
        return hh if kind == "q" else 8 + hh

    with (
        nc.sbuf_tensor("c2_wq", [128, 2 * 4096], BF16) as wqb,
        nc.sbuf_tensor("c2_wv", [128, 2 * 2048], BF16) as wvb,
        nc.sbuf_tensor("c2_cos", [128, 2048], BF16) as cosT,
        nc.sbuf_tensor("c2_sin", [128, 2048], BF16) as sinT,
        nc.sbuf_tensor("c2_ro", [128, 4 * 512], F32) as rob,
        nc.psum_tensor("c2_pa0", [128, 512], F32) as pa0,
        nc.psum_tensor("c2_pb0", [128, 512], F32) as pb0,
        nc.psum_tensor("c2_pa1", [128, 512], F32) as pa1,
        nc.psum_tensor("c2_pb1", [128, 512], F32) as pb1,
        nc.psum_tensor("c2_pv0", [128, 512], F32) as pv0,
        nc.psum_tensor("c2_pv1", [128, 512], F32) as pv1,
        nc.Block() as blk,
    ):
        PA = [pa0, pa1]
        PB = [pb0, pb1]
        PV = [pv0, pv1]

        @blk.gpsimd
        def _(g):
            _entry_waits(env, g)
            inc(S["cst"], g.dma_start(out=cosT[:, :], in_=env.io["cCos"][:, :]), 16)
            inc(S["cst"], g.dma_start(out=sinT[:, :], in_=env.io["cSin"][:, :]), 16)
            for lh, (kind, hh) in enumerate(LOADS):
                if lh >= 2:
                    g.wait_ge(S["cQK"].h, S["cQK"].base + 4 * (lh - 1))
                src = env.io["Wq"] if kind == "q" else env.io["Wk"]
                dinc(S["sWQ"], lh % 2, g.dma_start(
                    out=wqb[:, (lh % 2) * 4096:(lh % 2 + 1) * 4096].rearrange("p (k d) -> p k d", k=16),
                    in_=src[:, hh * 256:(hh + 1) * 256].rearrange("(k p) d -> p k d", p=128)))
            for b in range(64):
                tt, kb = divmod(b, 4)
                if b >= 2:
                    g.wait_ge(S["cVmm"].h, S["cVmm"].base + 4 * b - 4)
                dinc(S["sWV"], b % 2, g.dma_start(
                    out=wvb[:, (b % 2) * 2048:(b % 2 + 1) * 2048].rearrange("p (kk d) -> p kk d", kk=4),
                    in_=env.io["Wv"][kb * 512:(kb + 1) * 512, :].rearrange("(kk p) d -> p kk d", p=128)))

        @blk.tensor
        def _(t):
            _entry_waits(env, t)
            for ui in range(len(QU)):
                kind, hh, u4 = QU[ui]
                par = ui % 2
                lh = load_of_unit(ui)
                dwait(t, S["sWQ"], lh % 2, lh // 2 + 1)
                if ui >= 2:
                    t.wait_ge(S["cRope"].h, S["cRope"].base + 6 * (ui - 1))
                for k in range(16):
                    wcol = (lh % 2) * 4096 + k * 256
                    t.matmul(PA[par][:, :], wqb[:, wcol:wcol + 128],
                             env.Y2[:, k * 2048 + u4 * 512:k * 2048 + (u4 + 1) * 512],
                             start=(k == 0), stop=(k == 15))
                    insB = t.matmul(PB[par][:, :], wqb[:, wcol + 128:wcol + 256],
                                    env.Y2[:, k * 2048 + u4 * 512:k * 2048 + (u4 + 1) * 512],
                                    start=(k == 0), stop=(k == 15))
                inc(S["cQK"], insB)
            for tt in range(16):
                if tt >= 2:
                    t.wait_ge(S["cVe"].h, S["cVe"].base + (tt - 1))
                for k in range(16):
                    i = tt * 16 + k
                    if k % 4 == 0:
                        dwait(t, S["sWV"], (i // 4) % 2, i // 8 + 1)
                    vcol = ((i // 4) % 2) * 2048 + (k % 4) * 512
                    insV = t.matmul(PV[tt % 2][:, :],
                                    env.Y2[:, k * 2048 + tt * 128:k * 2048 + (tt + 1) * 128],
                                    wvb[:, vcol:vcol + 512],
                                    start=(k == 0), stop=(k == 15))
                    inc(S["cVmm"], insV)
                insV = None

        @blk.vector
        def _(v):
            _entry_waits(env, v)
            v.wait_ge(S["cst"].h, S["cst"].n)
            for ui in range(len(QU)):
                kind, hh, u4 = QU[ui]
                par = ui % 2
                v.wait_ge(S["cQK"].h, S["cQK"].base + ui + 1)
                dst = env.QT if kind == "q" else env.KT
                c0 = (2 * hh) * 2048 + u4 * 512
                c1 = (2 * hh + 1) * 2048 + u4 * 512
                coss = cosT[:, u4 * 512:(u4 + 1) * 512]
                sins = sinT[:, u4 * 512:(u4 + 1) * 512]
                v.tensor_mul(rob[:, 0:512], PA[par][:, :], coss)
                v.tensor_mul(rob[:, 512:1024], PB[par][:, :], sins)
                v.tensor_mul(rob[:, 1024:1536], PB[par][:, :], coss)
                v.tensor_mul(rob[:, 1536:2048], PA[par][:, :], sins)
                v.tensor_sub(dst[:, c0:c0 + 512], rob[:, 0:512], rob[:, 512:1024])
                inc(S["cRope"], v.tensor_add(dst[:, c1:c1 + 512],
                                             rob[:, 1024:1536], rob[:, 1536:2048]), 6)

        @blk.scalar
        def _(a):
            _entry_waits(env, a)
            for tt in range(16):
                a.wait_ge(S["cVmm"].h, S["cVmm"].base + 16 * (tt + 1))
                inc(S["cVe"], a.copy(env.V[:, tt * 512:(tt + 1) * 512], PV[tt % 2][:, :]))


def block_B3b(env, xin, xout):
    nc = env.nc
    S = dict(env.sems)
    S.update(_dsems(env, {"sWO": (0, 2), "sXD3": (2, 2), "sXW3": (4, 2)}))
    _snap_all(env)
    from contextlib import ExitStack
    with ExitStack() as _es:
        _e = _es.enter_context
        Eb = _e(nc.sbuf_tensor("c3_E", [128, 2 * 512], BF16))
        gEb = _e(nc.sbuf_tensor("c3_gE", [128, 2 * 512], BF16))
        albe = _e(nc.sbuf_tensor("c3_ab", [128, 2 * 512], F32))
        rows = _e(nc.sbuf_tensor("c3_rows", [1, 3 * 512], F32))
        ocb = _e(nc.sbuf_tensor("c3_oc", [128, 2 * 512], F32))
        wob = _e(nc.sbuf_tensor("c3_wo", [128, 2 * 1024], BF16))
        xcb = _e(nc.sbuf_tensor("c3_xc", [128, 2 * 512], F32))
        xob = _e(nc.sbuf_tensor("c3_xo", [128, 2 * 512], F32))
        sqq = _e(nc.sbuf_tensor("c3_sqq", [128, 2 * 512], BF16))
        qpb = _e(nc.sbuf_tensor("c3_qpb", [1, 2048], BF16))
        kpb = _e(nc.sbuf_tensor("c3_kpb", [1, 2048], BF16))
        scl = _e(nc.sbuf_tensor("c3_scl", [1, 64], F32))
        mcol = _e(nc.sbuf_tensor("c3_mcol", [128, 16], F32))
        oe0 = _e(nc.psum_tensor("c3_oe0", [128, 512], F32))
        oe1 = _e(nc.psum_tensor("c3_oe1", [128, 512], F32))
        og0 = _e(nc.psum_tensor("c3_og0", [128, 512], F32))
        og1 = _e(nc.psum_tensor("c3_og1", [128, 512], F32))
        prs = _e(nc.psum_tensor("c3_rs", [128, 512], F32))
        pgr = _e(nc.psum_tensor("c3_gr", [128, 512], F32))
        ps0 = _e(nc.psum_tensor("c3_s0", [128, 512], F32))
        ps1 = _e(nc.psum_tensor("c3_s1", [128, 512], F32))
        blk = _e(nc.Block())
        OE = [oe0, oe1]
        OG = [og0, og1]
        PS = [ps0, ps1]
        def prostats(h):
            items = []
            if h % 4 == 0:
                items += [("k2", u) for u in range(4)]
            items += [("kp", u) for u in range(4)]
            items += [("qp", u) for u in range(4)]
            items += [("q2", u) for u in range(4)]
            return items

        pro_base = [0] * 9
        for h in range(8):
            pro_base[h + 1] = pro_base[h] + len(prostats(h))
        sqcnt = []
        acc = 0
        for h in range(8):
            for (kindp, u) in prostats(h):
                if kindp in ("q2", "k2"):
                    acc += 1
                sqcnt.append(acc)
        km_after_kmax = {}
        km_after_qmax = {}
        km = 0
        for h in range(8):
            if h % 4 == 0:
                km += 1
                km_after_kmax[h] = km
            km += 1
            km_after_qmax[h] = km
            km += 2

        def nsc_of(hi):
            return sum(2 * (4 * (x % 4) + 4) for x in range(hi))

        HU = [(h, u) for h in range(8) for u in range(4)]
        NJ = {hi: 4 * HU[hi][1] + 4 for hi in range(32)}

        def av_base(hi):
            return sum(NJ[x] for x in range(hi))

        @blk.tensor
        def _(t):
            _entry_waits(env, t)
            for h in range(8):
                kv = h // 4
                if h >= 1:
                    t.wait_ge(S["cAbe"].h, S["cAbe"].base + 2 * (4 * h))
                for pi, (kindp, u) in enumerate(prostats(h)):
                    gi = pro_base[h] + pi
                    if gi >= 1:
                        t.wait_ge(S["cStE"].h, S["cStE"].base + gi)
                    if kindp == "kp":
                        ins = t.matmul(PS[0][0:1, :], env.ct["cDirs"][:, h:h + 1],
                                       env.KT[0:3, (2 * kv) * 2048 + u * 512:(2 * kv) * 2048 + (u + 1) * 512],
                                       start=True, stop=True)
                    elif kindp == "qp":
                        ins = t.matmul(PS[0][0:1, :], env.ct["cDirs"][:, h:h + 1],
                                       env.QT[0:3, (2 * h) * 2048 + u * 512:(2 * h) * 2048 + (u + 1) * 512],
                                       start=True, stop=True)
                    else:
                        t.wait_ge(S["cSqQ"].h, S["cSqQ"].base + 2 * sqcnt[gi])
                        for half in range(2):
                            ins = t.matmul(PS[0][32:33, :], env.ct["cOnesColB"][:, :],
                                           sqq[:, half * 512:(half + 1) * 512],
                                           start=(half == 0), stop=(half == 1))
                    inc(S["cStM"], ins)
                t.wait_ge(S["cBias"].h, S["cBias"].base + 2 * (h + 1))
                inc(S["cBcM"], t.matmul(PS[1][:, 0:1], env.ct["cOnesRowF"][:, :],
                                        scl[0:1, 10:11], start=True, stop=True))
                inc(S["cBcM"], t.matmul(PS[1][:, 1:2], env.ct["cOnesRowF"][:, :],
                                        scl[0:1, 11:12], start=True, stop=True))
                for u in range(4):
                    hi = h * 4 + u
                    nj = NJ[hi]
                    nsc = nsc_of(hi)
                    if hi >= 1:
                        t.wait_ge(S["cRowF"].h, S["cRowF"].base + 2 * hi)
                        t.wait_ge(S["cCmb"].h, S["cCmb"].base + 6 * hi)
                    t.wait_ge(S["cMce"].h, S["cMce"].base + 2 * (h + 1))
                    if u == 0:
                        t.wait_ge(S["cStE"].h, S["cStE"].base + pro_base[h + 1])
                    for j in range(nj):
                        last_main = (j < 4 * u)
                        if nsc >= 2:
                            t.wait_ge(S["cExpE"].h, S["cExpE"].base + nsc - 1)
                        t.matmul(PS[nsc % 2][:, :],
                                 env.KT[:, (2 * kv) * 2048 + j * 128:(2 * kv) * 2048 + (j + 1) * 128],
                                 env.QT[:, (2 * h) * 2048 + u * 512:(2 * h) * 2048 + (u + 1) * 512],
                                 start=True, stop=False)
                        ins = t.matmul(PS[nsc % 2][:, :],
                                       env.KT[:, (2 * kv + 1) * 2048 + j * 128:(2 * kv + 1) * 2048 + (j + 1) * 128],
                                       env.QT[:, (2 * h + 1) * 2048 + u * 512:(2 * h + 1) * 2048 + (u + 1) * 512],
                                       start=False, stop=last_main)
                        if not last_main:
                            ins = t.matmul(PS[nsc % 2][:, :], env.ct["cIDb"][:, :],
                                           env.ct["cMask"][:, (j - 4 * u) * 512:(j - 4 * u + 1) * 512],
                                           start=False, stop=True)
                        inc(S["cScM"], ins)
                        nsc += 1
                        if nsc >= 2:
                            t.wait_ge(S["cExpE"].h, S["cExpE"].base + nsc - 1)
                        ins = t.matmul(PS[nsc % 2][:, :],
                                       kpb[0:1, j * 128:(j + 1) * 128],
                                       qpb[0:1, u * 512:(u + 1) * 512],
                                       start=True, stop=last_main)
                        if not last_main:
                            ins = t.matmul(PS[nsc % 2][:, :], env.ct["cIDb"][:, :],
                                           env.ct["cMask"][:, (j - 4 * u) * 512:(j - 4 * u + 1) * 512],
                                           start=False, stop=True)
                        inc(S["cScM"], ins)
                        nsc += 1
                        t.wait_ge(S["cExpE"].h, S["cExpE"].base + nsc)
                        t.matmul(prs[0:1, :], env.ct["cOnesColB"][:, :],
                                 Eb[:, (j % 2) * 512:(j % 2 + 1) * 512],
                                 start=(j == 0), stop=(j == nj - 1))
                        t.matmul(pgr[0:1, :], env.ct["cOnesColB"][:, :],
                                 gEb[:, (j % 2) * 512:(j % 2 + 1) * 512],
                                 start=(j == 0), stop=(j == nj - 1))
                        for half in range(2):
                            vsl = j * 512 + kv * 256 + half * 128
                            t.matmul(OE[half][:, :], env.V[:, vsl:vsl + 128],
                                     Eb[:, (j % 2) * 512:(j % 2 + 1) * 512],
                                     start=(j == 0), stop=(j == nj - 1))
                            ins = t.matmul(OG[half][:, :], env.V[:, vsl:vsl + 128],
                                           gEb[:, (j % 2) * 512:(j % 2 + 1) * 512],
                                           start=(j == 0), stop=(j == nj - 1))
                        inc(S["cAV"], ins)
                    t.wait_ge(S["cRowF"].h, S["cRowF"].base + 2 * (hi + 1))
                    t.wait_ge(S["cExpE"].h, S["cExpE"].base + nsc)
                    inc(S["cBcA"], t.matmul(PS[0][:, :], env.ct["cOnesRowF"][:, :],
                                            rows[0:1, 0:512], start=True, stop=True))
                    inc(S["cBcA"], t.matmul(PS[1][:, :], env.ct["cOnesRowF"][:, :],
                                            rows[0:1, 512:1024], start=True, stop=True))
            t.wait_ge(S["cCmb"].h, S["cCmb"].base + 32 * 6)
            t.wait_ge(S["cAbe"].h, S["cAbe"].base + 64)
            for oc in range(16):
                for u in range(4):
                    i = oc * 4 + u
                    w = oc * 2 + u // 2
                    if i >= 2:
                        t.wait_ge(S["cOe"].h, S["cOe"].base + (i - 1))
                    for l in range(2):
                        LI = w * 2 + l
                        dwait(t, S["sWO"], LI % 2, LI // 2 + 1)
                        for k8 in range(8):
                            k = l * 8 + k8
                            insO = t.matmul(PS[i % 2][:, :],
                                            wob[:, (LI % 2) * 1024 + k8 * 128:(LI % 2) * 1024 + (k8 + 1) * 128],
                                            env.AO[:, k * 2048 + u * 512:k * 2048 + (u + 1) * 512],
                                            start=(k == 0), stop=(k == 15))
                    inc(S["cWoM"], insO)

        @blk.vector
        def _(v):
            _entry_waits(env, v)
            for h in range(8):
                kv = h // 4
                for pi, (kindp, u) in enumerate(prostats(h)):
                    gi = pro_base[h] + pi
                    v.wait_ge(S["cStM"].h, S["cStM"].base + gi + 1)
                    if kindp == "kp":
                        if pi == [p for p, (kk, _) in enumerate(prostats(h)) if kk == "kp"][0] and h >= 1:
                            v.wait_ge(S["cScM"].h, S["cScM"].base + nsc_of(4 * h))
                        inc(S["cStE"], v.tensor_copy(kpb[0:1, u * 512:(u + 1) * 512], PS[0][0:1, :]))
                    elif kindp == "qp":
                        inc(S["cStE"], v.tensor_copy(qpb[0:1, u * 512:(u + 1) * 512], PS[0][0:1, :]))
                    elif kindp == "q2":
                        inc(S["cStE"], v.reduce_max(scl[0:1, u:u + 1], PS[0][32:33, :], AX.X))
                    else:
                        inc(S["cStE"], v.reduce_max(scl[0:1, 16 + u:17 + u], PS[0][32:33, :], AX.X))
                if h % 4 == 0:
                    inc(S["cKm"], v.reduce_max(scl[0:1, 20:21], scl[0:1, 16:20], AX.X))
                inc(S["cKm"], v.reduce_max(scl[0:1, 4:5], scl[0:1, 0:4], AX.X))
                inc(S["cKm"], v.reduce_max(scl[0:1, 8:9], qpb[0:1, :], AX.X,
                                           apply_absolute_value=True))
                inc(S["cKm"], v.reduce_max(scl[0:1, 9:10], kpb[0:1, :], AX.X,
                                           apply_absolute_value=True))
                v.wait_ge(S["cSqt"].h, S["cSqt"].base + (h + 1) + (h // 4 + 1))
                inc(S["cBias"], v.tensor_scalar(scl[0:1, 10:11], scl[0:1, 5:6],
                                                scl[0:1, 6 + kv:7 + kv], -SAFE / 16.0,
                                                ALU.mult, ALU.mult))
                inc(S["cBias"], v.tensor_scalar(scl[0:1, 11:12], scl[0:1, 8:9],
                                                scl[0:1, 9:10], -SAFE,
                                                ALU.mult, ALU.mult))
                for u in range(4):
                    hi = h * 4 + u
                    nj = NJ[hi]
                    v.wait_ge(S["cAV"].h, S["cAV"].base + av_base(hi) + nj)
                    v.reciprocal(rows[0:1, 1024:1536], prs[0:1, :])
                    inc(S["cRowF"], v.tensor_single_scalar(rows[0:1, 0:512],
                                                           rows[0:1, 1024:1536],
                                                           env.ct["hs1"][0:1, h:h + 1], ALU.mult))
                    v.reciprocal(rows[0:1, 1024:1536], pgr[0:1, :])
                    inc(S["cRowF"], v.tensor_single_scalar(rows[0:1, 512:1024],
                                                           rows[0:1, 1024:1536],
                                                           env.ct["hs"][0:1, h:h + 1], ALU.mult))
                    v.wait_ge(S["cAbe"].h, S["cAbe"].base + 2 * (hi + 1))
                    for half in range(2):
                        v.tensor_mul(ocb[:, 0:512], OE[half][:, :], albe[:, 0:512])
                        v.tensor_mul(ocb[:, 512:1024], OG[half][:, :], albe[:, 512:1024])
                        inc(S["cCmb"], v.tensor_add(
                            env.AO[:, (2 * h + half) * 2048 + u * 512:(2 * h + half) * 2048 + (u + 1) * 512],
                            ocb[:, 0:512], ocb[:, 512:1024]), 3)
            for oc in range(16):
                for u in range(4):
                    i = oc * 4 + u
                    v.wait_ge(S["cWoM"].h, S["cWoM"].base + i + 1)
                    dwait(v, S["sXD3"], i % 2, i // 2 + 1)
                    if i >= 2:
                        dwait(v, S["sXW3"], i % 2, i // 2)
                    inc(S["cOe"], v.tensor_add(xob[:, (i % 2) * 512:(i % 2 + 1) * 512],
                                               PS[i % 2][:, :],
                                               xcb[:, (i % 2) * 512:(i % 2 + 1) * 512]))

        @blk.scalar
        def _(a):
            _entry_waits(env, a)
            for h in range(8):
                kv = h // 4
                for pi, (kindp, u) in enumerate(prostats(h)):
                    gi = pro_base[h] + pi
                    if kindp == "k2":
                        src_c = 2 * kv
                        srcT = env.KT
                    elif kindp == "q2":
                        src_c = 2 * h
                        srcT = env.QT
                    else:
                        continue
                    if gi >= 1:
                        a.wait_ge(S["cStE"].h, S["cStE"].base + gi)
                    for half in range(2):
                        inc(S["cSqQ"], a.square(
                            sqq[:, half * 512:(half + 1) * 512],
                            srcT[:, (src_c + half) * 2048 + u * 512:(src_c + half) * 2048 + (u + 1) * 512]))
                if h % 4 == 0:
                    a.wait_ge(S["cKm"].h, S["cKm"].base + km_after_kmax[h])
                    inc(S["cSqt"], a.activation(scl[0:1, 6 + kv:7 + kv], scl[0:1, 20:21], AF.Sqrt))
                a.wait_ge(S["cKm"].h, S["cKm"].base + km_after_qmax[h])
                inc(S["cSqt"], a.activation(scl[0:1, 5:6], scl[0:1, 4:5], AF.Sqrt))
                a.wait_ge(S["cBcM"].h, S["cBcM"].base + 2 * (h + 1))
                inc(S["cMce"], a.copy(mcol[:, 2 * h:2 * h + 1], PS[1][:, 0:1]))
                inc(S["cMce"], a.copy(mcol[:, 2 * h + 1:2 * h + 2], PS[1][:, 1:2]))
                for u in range(4):
                    hi = h * 4 + u
                    nj = NJ[hi]
                    nsc = nsc_of(hi)
                    for j in range(nj):
                        a.wait_ge(S["cScM"].h, S["cScM"].base + nsc + 1)
                        if j >= 2:
                            a.wait_ge(S["cAV"].h, S["cAV"].base + av_base(hi) + (j - 1))
                        inc(S["cExpE"], a.activation(Eb[:, (j % 2) * 512:(j % 2 + 1) * 512],
                                                     PS[nsc % 2][:, :], AF.Exp,
                                                     bias=mcol[:, 2 * h:2 * h + 1], scale=0.0625))
                        nsc += 1
                        a.wait_ge(S["cScM"].h, S["cScM"].base + nsc + 1)
                        inc(S["cExpE"], a.activation(gEb[:, (j % 2) * 512:(j % 2 + 1) * 512],
                                                     PS[nsc % 2][:, :], AF.Exp,
                                                     bias=mcol[:, 2 * h + 1:2 * h + 2]))
                        nsc += 1
                    a.wait_ge(S["cBcA"].h, S["cBcA"].base + 2 * (hi + 1))
                    if hi >= 1:
                        a.wait_ge(S["cCmb"].h, S["cCmb"].base + 6 * hi)
                    inc(S["cAbe"], a.copy(albe[:, 0:512], PS[0][:, :]))
                    inc(S["cAbe"], a.copy(albe[:, 512:1024], PS[1][:, :]))

        @blk.gpsimd
        def _(g):
            _entry_waits(env, g)

            def ld(LI):
                w, l = divmod(LI, 2)
                oc, up = divmod(w, 2)
                if LI >= 2:
                    g.wait_ge(S["cWoM"].h, S["cWoM"].base + 2 * ((LI - 2) // 2 + 1))
                dinc(S["sWO"], LI % 2, g.dma_start(
                    out=wob[:, (LI % 2) * 1024:(LI % 2 + 1) * 1024].rearrange("p (k o) -> p k o", k=8),
                    in_=env.io["Wo"][l * 1024:(l + 1) * 1024, oc * 128:(oc + 1) * 128].rearrange("(k p) o -> p k o", p=128)))

            def rd(i):
                oc, u = divmod(i, 4)
                if i >= 2:
                    g.wait_ge(S["cOe"].h, S["cOe"].base + (i - 1))
                dinc(S["sXD3"], i % 2, g.dma_start(
                    out=xcb[:, (i % 2) * 512:(i % 2 + 1) * 512],
                    in_=xin.ap()[oc * 128:(oc + 1) * 128, u * 512:(u + 1) * 512]))

            def wr(i):
                oc, u = divmod(i, 4)
                g.wait_ge(S["cOe"].h, S["cOe"].base + i + 1)
                dinc(S["sXW3"], i % 2, g.dma_start(
                    out=xout.ap()[oc * 128:(oc + 1) * 128, u * 512:(u + 1) * 512],
                    in_=xob[:, (i % 2) * 512:(i % 2 + 1) * 512]))

            ld(0)
            ld(1)
            rd(0)
            rd(1)
            for i in range(64):
                if i >= 2 and i % 2 == 0:
                    ld(i)
                    ld(i + 1)
                wr(i)
                if i + 2 < 64:
                    rd(i + 2)



def block_P5(env, src, dst, final):
    nc = env.nc
    S = dict(env.sems)
    S.update(_dsems(env, {"sX0": (0, 2), "sXTw": (2, 2)}))
    _snap_all(env)
    with (
        nc.sbuf_tensor("p5_xbuf", [128, 2 * 2048], F32) as xbuf,
        nc.sbuf_tensor("p5_trb", [128, 2 * 128], F32) as trb,
        nc.psum_tensor("p5_pt0", [128, 128], F32) as pt0,
        nc.psum_tensor("p5_pt1", [128, 128], F32) as pt1,
        nc.psum_tensor("p5_pt2", [128, 128], F32) as pt2,
        nc.psum_tensor("p5_pt3", [128, 128], F32) as pt3,
        nc.Block() as blk,
    ):
        pt = [pt0, pt1, pt2, pt3]

        @blk.gpsimd
        def _(g):
            _entry_waits(env, g)
            for tt in range(16):
                if tt >= 2:
                    g.wait_ge(S["cP1t"].h, S["cP1t"].base + 16 * (tt - 1))
                dinc(S["sX0"], tt % 2, g.dma_start(out=xbuf[:, (tt % 2) * 2048:(tt % 2 + 1) * 2048],
                                                   in_=src.ap()[tt * 128:(tt + 1) * 128, :]))

        @blk.tensor
        def _(t):
            _entry_waits(env, t)
            for i in range(256):
                tt, k = divmod(i, 16)
                if k == 0:
                    dwait(t, S["sX0"], tt % 2, tt // 2 + 1)
                if i >= 4:
                    t.wait_ge(S["cP1e"].h, S["cP1e"].base + (i - 3))
                inc(S["cP1t"], t.transpose(pt[i % 4][:, :],
                                           xbuf[:, (tt % 2) * 2048 + k * 128:(tt % 2) * 2048 + (k + 1) * 128],
                                           env.ct["cIDf"][:, :]))

        @blk.scalar
        def _(a):
            _entry_waits(env, a)
            for i in range(256):
                a.wait_ge(S["cP1t"].h, S["cP1t"].base + i + 1)
                if i >= 2:
                    dwait(a, S["sXTw"], i % 2, i // 2)
                inc(S["cP1e"], a.copy(trb[:, (i % 2) * 128:(i % 2 + 1) * 128],
                                      pt[i % 4][:, :]))

        @blk.sync
        def _(s):
            _entry_waits(env, s)
            for i in range(256):
                tt, k = divmod(i, 16)
                s.wait_ge(S["cP1e"].h, S["cP1e"].base + i + 1)
                dinc(S["sXTw"], i % 2, s.dma_start(out=dst.ap()[k * 128:(k + 1) * 128, tt * 128:(tt + 1) * 128],
                                                   in_=trb[:, (i % 2) * 128:(i % 2 + 1) * 128]))
            if final:
                dwait_all(s, S["sXTw"])


def build_nc():
    nc = bass.Bass("TRN2", target_bir_lowering=False)
    env = Env()
    env.nc = nc

    io = {}
    for name, arr in _CONSTS.items():
        io[name] = nc.dram_tensor(name, list(arr.shape),
                                  BF16 if arr.dtype == BF else F32,
                                  kind="ExternalInput").ap()
    for name, shape in [("x", [T, C]), ("Wq", [C, C]), ("Wk", [C, 512]),
                        ("Wv", [C, 512]), ("Wo", [C, C]), ("head_scales", [8]),
                        ("SU_ug", [C]), ("SU_dp", [INTER]), ("Wscale_ug", [2 * INTER]),
                        ("Wscale_dp", [C]), ("W_up", [INTER, C]), ("W_gate", [INTER, C]),
                        ("W_down", [C, INTER])]:
        io[name] = nc.dram_tensor(name, shape, F32, kind="ExternalInput").ap()
    env.io = io

    env.xT = nc.dram_tensor("xT", [C, T], F32, kind="Internal")
    env.x1T = nc.dram_tensor("x1T", [C, T], F32, kind="Internal")
    env.x2T = nc.dram_tensor("x2T", [C, T], F32, kind="Internal")
    env.x3T = nc.dram_tensor("x3T", [C, T], F32, kind="Internal")
    env.SNup = nc.dram_tensor("SNup", [INTER, C], BF16, kind="Internal")
    env.SNgate = nc.dram_tensor("SNgate", [INTER, C], BF16, kind="Internal")
    env.SNdown = nc.dram_tensor("SNdown", [C, INTER], BF16, kind="Internal")
    env.STdown = nc.dram_tensor("STdown", [INTER, C], BF16, kind="Internal")
    env.STup = nc.dram_tensor("STup", [C, INTER], BF16, kind="Internal")
    env.STgate = nc.dram_tensor("STgate", [C, INTER], BF16, kind="Internal")
    env.PBdump = nc.dram_tensor("PBdump", [4, 64, 128, 512], BF16, kind="Internal")
    env.SEff = nc.dram_tensor("SEff", [4, 64, 128, 512], BF16, kind="Internal")
    env.y = nc.dram_tensor("y", [T, C], F32, kind="ExternalOutput")
    if DBG:
        env.dbg = {k: nc.dram_tensor(k, [C, T], F32, kind="ExternalOutput")
                   for k in ("dbg_x1", "dbg_x2")}
        env.dbg["dbg_app"] = nc.dram_tensor("dbg_app", [128, 16 * 1024], BF16, kind="ExternalOutput")
        env.dbg["dbg_seff"] = nc.dram_tensor("dbg_seff", [4, 64, 128, 512], BF16, kind="ExternalOutput")
        env.dbg["dbg_y2"] = nc.dram_tensor("dbg_y2", [128, 16 * 2048], BF16, kind="ExternalOutput")
        env.dbg["dbg_qt"] = nc.dram_tensor("dbg_qt", [128, 16 * 2048], BF16, kind="ExternalOutput")
        env.dbg["dbg_kt"] = nc.dram_tensor("dbg_kt", [128, 4 * 2048], BF16, kind="ExternalOutput")
        env.dbg["dbg_v"] = nc.dram_tensor("dbg_v", [128, 16 * 512], BF16, kind="ExternalOutput")

    sem_names = [
        "cst", "cP1t", "cP1e", "sCu", "sCg", "sCd",
        "cPBm", "cPBe", "cDt", "cDe", "cDw",
        "cXA1", "cSq", "cXA2", "cTS", "cRowA", "cRowV", "cRowR",
        "cTB", "cAp", "cH16", "cYb", "cT128", "cAe",
        "cWU", "cWG", "cSE", "cEvU", "cSil",
        "cZp", "cTH", "cZh", "cEvD",
        "cQK", "cRope", "cVmm", "cVe", "cStM", "cStE", "cSqQ",
        "cScM", "cExpE", "cAV", "cRowF", "cBcA", "cAbe", "cCmb", "cWoM", "cOe",
        "cBias", "cBcM", "cMce", "cKm", "cSqt",
    ]
    N_DCELLS = 29

    from contextlib import ExitStack
    with ExitStack() as st:
        env.sems = {}
        for nm in sem_names:
            env.sems[nm] = Sem(st.enter_context(nc.semaphore(nm)))
        env.dcells = [Sem(st.enter_context(nc.semaphore(f"dma{k}")))
                      for k in range(N_DCELLS)]

        ct = {}
        for name, arr in _CONSTS.items():
            if name in ("cCos", "cSin"):
                continue
            dt = BF16 if arr.dtype == BF else F32
            ct[name] = st.enter_context(nc.sbuf_tensor("t_" + name, list(arr.shape), dt))
        ct["hs"] = st.enter_context(nc.sbuf_tensor("t_hs", [1, 8], F32))
        ct["hs1"] = st.enter_context(nc.sbuf_tensor("t_hs1", [1, 8], F32))
        ct["su_t"] = st.enter_context(nc.sbuf_tensor("t_su", [128, 16], F32))
        ct["co2_t"] = st.enter_context(nc.sbuf_tensor("t_co2", [128, 16], F32))
        ct["sd_t"] = st.enter_context(nc.sbuf_tensor("t_sd", [128, 64], F32))
        ct["cu_t"] = st.enter_context(nc.sbuf_tensor("t_cu", [128, 64], F32))
        ct["cg_t"] = st.enter_context(nc.sbuf_tensor("t_cg", [128, 64], F32))
        ct["w2_t"] = st.enter_context(nc.sbuf_tensor("t_w2", [128, 64], F32))
        env.ct = ct
        env.cst = None
        nc.const_aps.register(F32, EPS, ct["cMisc"][0:1, 0:1])

        block_B1(env)

        def ffn_call(xin, xout, tagp, wdc_call=False):
            g_app = nc.sbuf_tensor("t_app" + tagp, [128, 16 * SUP], BF16)
            env.APP = g_app.__enter__()
            for u in range(NSUP):
                wdc = wdc_call and u == 0
                block_B2a(env, xin, u, tagp + f"u{u}", wdc=wdc)
                g_uz = nc.sbuf_tensor(f"t_uz_{tagp}u{u}", [128, 64 * SUP], BF16)
                env.UZ = g_uz.__enter__()
                block_B2b(env, xin, xout, u, tagp + f"u{u}", wdc=wdc)
                g_uz.__exit__(None, None, None)
            g_app.__exit__(None, None, None)

        ffn_call(env.xT, env.x1T, "f1", wdc_call=True)

        g_qt = nc.sbuf_tensor("t_qt", [128, 16 * 2048], BF16)
        g_kt = nc.sbuf_tensor("t_kt", [128, 4 * 2048], BF16)
        g_v = nc.sbuf_tensor("t_v", [128, 16 * 512], BF16)
        env.QT = g_qt.__enter__()
        env.KT = g_kt.__enter__()
        env.V = g_v.__enter__()
        g_y2 = nc.sbuf_tensor("t_y2", [128, 16 * 2048], BF16)
        env.Y2 = g_y2.__enter__()
        block_B3a1(env, env.x1T)
        block_B3a2(env)
        g_y2.__exit__(None, None, None)
        g_ao = nc.sbuf_tensor("t_ao", [128, 16 * 2048], BF16)
        env.AO = g_ao.__enter__()
        block_B3b(env, env.x1T, env.x2T)
        g_ao.__exit__(None, None, None)
        g_v.__exit__(None, None, None)
        g_kt.__exit__(None, None, None)
        g_qt.__exit__(None, None, None)

        ffn_call(env.x2T, env.x3T, "f2")

        block_P5(env, env.x3T, env.y, final=True)

        if DBG:
            block_dbg(env)
    return nc


def block_dbg(env):
    nc = env.nc
    S = env.sems
    _snap_all(env)
    with nc.Block() as blk:
        @blk.gpsimd
        def _(g):
            _entry_waits(env, g)
            for i in range(4):
                inc(S["cst"], g.dma_start(out=env.dbg["dbg_x1"].ap()[i * 512:(i + 1) * 512, :],
                                          in_=env.x1T.ap()[i * 512:(i + 1) * 512, :]), 16)
                inc(S["cst"], g.dma_start(out=env.dbg["dbg_x2"].ap()[i * 512:(i + 1) * 512, :],
                                          in_=env.x2T.ap()[i * 512:(i + 1) * 512, :]), 16)
            for og in range(4):
                inc(S["cst"], g.dma_start(out=env.dbg["dbg_seff"].ap()[og],
                                          in_=env.SEff.ap()[og]), 16)
            g.wait_ge(S["cst"].h, S["cst"].n)


_NC_CACHE = None


def kernel(**inputs):
    global _NC_CACHE
    if _NC_CACHE is None:
        _NC_CACHE = build_nc()
    nc = _NC_CACHE
    x = np.ascontiguousarray(np.asarray(inputs["L0_attn_x"], dtype=np.float32))
    base = {}
    for name, arr in _CONSTS.items():
        base[name] = np.ascontiguousarray(arr)
    key_map = {
        "Wq": "L0_attn_Wq", "Wk": "L0_attn_Wk", "Wv": "L0_attn_Wv", "Wo": "L0_attn_Wo",
        "head_scales": "L0_attn_head_scales", "SU_ug": "L0_ffn_SU_ug",
        "SU_dp": "L0_ffn_SU_dp", "Wscale_ug": "L0_ffn_Wscale_ug",
        "Wscale_dp": "L0_ffn_Wscale_dp", "W_up": "L0_ffn_W_up",
        "W_gate": "L0_ffn_W_gate", "W_down": "L0_ffn_W_down",
    }
    for k, src in key_map.items():
        base[k] = np.ascontiguousarray(np.asarray(inputs[src], dtype=np.float32))
    in_maps = []
    for b in range(8):
        m = dict(base)
        m["x"] = x[b]
        in_maps.append(m)
    res = run_bass_kernel_spmd(nc, in_maps, core_ids=list(range(8)))
    out = np.stack([np.asarray(r["y"]) for r in res.results], axis=0)
    if DBG:
        kernel._dbg = [{k: np.asarray(v) for k, v in r.items()} for r in res.results]
    return out
